# version 02
# baseline (speedup 1.0000x reference)
import math
from contextlib import ExitStack

import numpy as np
import ml_dtypes

import concourse.bass as bass
import concourse.mybir as mybir
from concourse.bass_utils import run_bass_kernel_spmd

F32 = mybir.dt.float32
BF16 = mybir.dt.bfloat16
AF = mybir.ActivationFunctionType
ALU = mybir.AluOpType
BF = ml_dtypes.bfloat16

B, S, D = 4, 4096, 1024
NH, DK = 16, 64
NCORES = 8

ENGINES = ("tensor", "scalar", "vector", "gpsimd", "sync")

PAIR_SWAP_MASK = []
for _i in range(0, 32, 2):
    PAIR_SWAP_MASK += [_i + 1, _i]


class Ev:
    __slots__ = ("sem", "val")

    def __init__(self, sem, val):
        self.sem = sem
        self.val = val

    def __repr__(self):
        return f"Ev({self.sem},{self.val})"


class Prog:

    def __init__(self, name="prog"):
        self.nc = bass.Bass("TRN2", target_bir_lowering=False)
        self.ops = {e: [] for e in ENGINES}
        self.sem_names = []
        self.clock_sem = {e: self._sem(f"clk_{e}") for e in ENGINES}
        self.clock = {e: 0 for e in ENGINES}
        self.sem_handles = {}
        self._uid = 0

    def _sem(self, name):
        self.sem_names.append(name)
        return name

    def stream_sem(self, name):
        return self._sem(name)

    def uid(self):
        self._uid += 1
        return self._uid

    def in_(self, name, shape, dtype):
        return self.nc.dram_tensor(name, list(shape), dtype, kind="ExternalInput")

    def out(self, name, shape, dtype):
        return self.nc.dram_tensor(name, list(shape), dtype, kind="ExternalOutput")

    def internal(self, name, shape, dtype):
        return self.nc.dram_tensor(name, list(shape), dtype, kind="Internal")

    def const(self, name, arr):
        return self.nc.inline_tensor(np.ascontiguousarray(arr), name=name)

    def sbuf(self, name, shape, dtype):
        return self.nc.alloc_sbuf_tensor(name, list(shape), dtype)

    def psum(self, name, shape, dtype=F32):
        return self.nc.alloc_psum_tensor(name, list(shape), dtype)

    def op(self, eng, builder, waits=(), sig=False):
        waits = [w for w in waits if w is not None]
        sig_val = None
        if sig:
            self.clock[eng] += 1
            sig_val = self.clock[eng]
        self.ops[eng].append(("op", builder, tuple(waits), sig_val))
        if sig:
            return Ev(self.clock_sem[eng], sig_val)
        return None

    def seq(self, eng, builders, waits=()):
        ev = None
        first = True
        for b in builders:
            ev = self.op(eng, b, waits=(list(waits) + [ev]) if first or ev else [ev],
                         sig=True)
            first = False
        return ev

    def dma(self, eng, sem, cum, builder, waits=()):
        waits = [w for w in waits if w is not None]
        self.ops[eng].append(("dma", builder, tuple(waits), (sem, 16)))
        return Ev(sem, cum)

    def build(self, sim_serial=False):
        nc = self.nc
        aux = {}
        if sim_serial:
            for e in ENGINES:
                aux[e] = self._sem(f"aux_{e}")
        with ExitStack() as ctx:
            for s in self.sem_names:
                self.sem_handles[s] = ctx.enter_context(nc.semaphore(s))
            block = ctx.enter_context(nc.Block())
            for eng_name in ENGINES:
                ops = self.ops[eng_name]
                if not ops:
                    continue
                clk = self.sem_handles[self.clock_sem[eng_name]]
                auxs = self.sem_handles.get(aux.get(eng_name))

                def replay(engine, ops=ops, clk=clk, eng_name=eng_name, auxs=auxs):
                    seen = {}
                    naux = 0
                    marker = None
                    for kind, builder, waits, sig in ops:
                        for w in waits:
                            if seen.get(w.sem, -1) >= w.val:
                                continue
                            engine.wait_ge(self.sem_handles[w.sem], w.val)
                            seen[w.sem] = w.val
                        if auxs is not None and marker is not None and kind != "dma":
                            engine.wait_ge(marker[0], marker[1])
                        ins = builder(engine)
                        if ins is None:
                            continue
                        if kind == "dma":
                            sem, inc = sig
                            ins.then_inc(self.sem_handles[sem], inc)
                        elif sig is not None:
                            ins.then_inc(clk, 1)
                            marker = (clk, sig)
                        elif auxs is not None:
                            ins.then_inc(auxs, 1)
                            naux += 1
                            marker = (auxs, naux)

                getattr(block, eng_name)(replay)
        return nc


class Stream:

    def __init__(self, prog, eng, name, nslots=2):
        self.p = prog
        self.eng = eng
        self.nslots = nslots
        self.sems = [prog.stream_sem(f"{name}{i}") for i in range(nslots)]
        self.cum = [0] * nslots
        self.i = 0

    def drain_evs(self):
        return [Ev(self.sems[i], self.cum[i]) for i in range(self.nslots)
                if self.cum[i]]

    def put(self, builders, waits=()):
        slot = self.i % self.nslots
        self.i += 1
        sem = self.sems[slot]
        gate = Ev(sem, self.cum[slot]) if self.cum[slot] else None
        ev = None
        for k, b in enumerate(builders):
            self.cum[slot] += 16
            ev = self.p.dma(self.eng, sem, self.cum[slot], b,
                            waits=(list(waits) + [gate]) if k == 0 else ())
        return ev


class PsumPool:

    def __init__(self, prog, n_f32=7):
        self.slots = []
        for i in range(n_f32):
            t = prog.psum(f"pp{i}", [128, 512], F32)
            self.slots.append([t, None])
        self.rr = 0

    def get(self):
        idx = self.rr % len(self.slots)
        self.rr += 1
        t, rel = self.slots[idx]
        return idx, t, rel

    def release(self, idx, ev):
        self.slots[idx][1] = ev


def cdiv(a, b):
    return (a + b - 1) // b



class Sect:
    def __init__(self, p: Prog, T, n_pool=7):
        self.p = p
        self.T = T
        self.NQC = cdiv(T, 512)
        self.ld = Stream(p, "gpsimd", "ld", 4)
        self.act_ld = Stream(p, "sync", "ald", 4)
        self.st = Stream(p, "sync", "st", 6)
        self.pool = PsumPool(p, n_f32=n_pool)
        self.scr = [p.sbuf(f"scr{i}", [128, 512], F32) for i in range(4)]
        self.ones_col = p.sbuf("ones_col", [128, 1], F32)
        self.ones_row = p.sbuf("ones_row", [1, 128], F32)
        c1 = p.const("c_ones_col", np.ones((128, 1), np.float32))
        c2 = p.const("c_ones_row", np.ones((1, 128), np.float32))
        c3 = p.const("c_ones_row_bf", np.ones((1, 128), BF))
        self.ones_row_bf = p.sbuf("ones_row_bf", [1, 128], BF16)
        e1 = self.ld.put([lambda g: g.dma_start(out=self.ones_col[:, :], in_=c1.ap())])
        e2 = self.ld.put([lambda g: g.dma_start(out=self.ones_row[:, :], in_=c2.ap())])
        e3 = self.ld.put([lambda g: g.dma_start(out=self.ones_row_bf[:, :], in_=c3.ap())])
        self.ones_bf_ev = e3
        self.ones_ev = e2
        self.ones_col_ev = e1
        self.ident_sb = None
        self.final_evs = []

    def wpool_get(self):
        if not hasattr(self, "_wpool"):
            self._wpool = []
            for si in range(2):
                tiles = [self.p.sbuf(f"wp{si}_{i}", [128, 1024], BF16)
                         for i in range(8)]
                self._wpool.append([tiles, None])
            self._wpool_rr = 0
        idx = self._wpool_rr % 2
        self._wpool_rr += 1
        tiles, rel = self._wpool[idx]
        return idx, tiles, rel

    def wpool_release(self, idx, ev):
        self._wpool[idx][1] = ev

    def ln_rows(self):
        if not hasattr(self, "_ln_rows"):
            p = self.p
            self._ln_rows = [p.sbuf(f"lnr{i}", [1, 512], F32) for i in range(4)]
        return self._ln_rows

    def prep_bufs(self):
        if not hasattr(self, "_prep"):
            p = self.p
            self._prep = {
                "w": [p.sbuf(f"prw{i}", [128, 1024], BF16) for i in range(2)],
                "s": [p.sbuf(f"prs{i}", [128, 1024], BF16) for i in range(2)],
                "t": [p.sbuf(f"prt{i}", [128, 128], BF16) for i in range(2)],
                "fw": [None, None], "fs": [None, None], "ft": [None, None],
            }
        return self._prep

    def tpsum(self):
        if not hasattr(self, "_tpsum"):
            self._tpsum = [self.p.psum("ps_tr0", [128, 1024], BF16),
                           self.p.psum("ps_tr1", [128, 1024], BF16)]
            self._tpsum_free = [None, None]
        return self._tpsum

    def ident32(self):
        if not hasattr(self, "_id32"):
            p = self.p
            c = p.const("c_ident_f32", np.eye(128, dtype=np.float32))
            self._id32 = p.sbuf("ident32_sb", [128, 128], F32)
            h = self._id32
            self._id32_ev = self.ld.put(
                [lambda g: g.dma_start(out=h[:, :], in_=c.ap())])
        return self._id32, self._id32_ev

    def ident(self):
        if self.ident_sb is None:
            p = self.p
            c = p.const("c_ident_bf", np.eye(128, dtype=BF))
            self.ident_sb = p.sbuf("ident_sb", [128, 128], BF16)
            h = self.ident_sb
            self.ident_ev = self.ld.put(
                [lambda g: g.dma_start(out=h[:, :], in_=c.ap())])
        return self.ident_sb, self.ident_ev



def emit_load_fm(sect, dram_ap, sb_tiles, waits=()):
    evs = []
    for dc in range(len(sb_tiles)):
        def b(g, dc=dc, sb=sb_tiles[dc]):
            return g.dma_start(out=sb[:, :],
                               in_=dram_ap[dc * 128:(dc + 1) * 128, :])
        evs.append(sect.act_ld.put([b], waits=waits))
    return evs


def emit_ln(sect, x_tiles, x_evs, z_tiles, keep_stats=None):
    p, T = sect.p, sect.T
    xsq = sect.scr[0]
    stat_sum, stat_sq, arow, brow = sect.ln_rows()

    last_z = None
    row_free = None
    for qc in range(sect.NQC):
        q0 = qc * 512
        qn = min(512, T - q0)
        sl = slice(0, qn)
        i_s, ps_s, rel_s = sect.pool.get()
        i_q, ps_q, rel_q = sect.pool.get()
        ev_mm = ev_mm2 = None
        for dc in range(8):
            ev_sq = p.op("scalar", lambda s, dc=dc, q0=q0, qn=qn: s.activation(
                xsq[:, :qn], x_tiles[dc][:, q0:q0 + qn], AF.Square),
                waits=[x_evs[dc], ev_mm2], sig=True)
            ev_mm = p.op("tensor", lambda t, dc=dc, q0=q0, qn=qn, ps=ps_s: t.matmul(
                ps[0:1, :qn], sect.ones_col[:, :], x_tiles[dc][:, q0:q0 + qn],
                start=(dc == 0), stop=(dc == 7)),
                waits=[x_evs[dc], sect.ones_col_ev] + ([rel_s] if dc == 0 else []),
                sig=(dc == 7))
            ev_mm2 = p.op("tensor", lambda t, dc=dc, qn=qn, ps=ps_q: t.matmul(
                ps[0:1, :qn], sect.ones_col[:, :], xsq[:, :qn],
                start=(dc == 0), stop=(dc == 7)),
                waits=[ev_sq] + ([rel_q] if dc == 0 else []), sig=True)
        ev_c1 = p.op("vector", lambda v, qn=qn, ps=ps_s: v.tensor_copy(
            stat_sum[:, :qn], ps[0:1, :qn]), waits=[ev_mm, row_free], sig=True)
        ev_c2 = p.op("vector", lambda v, qn=qn, ps=ps_q: v.tensor_copy(
            stat_sq[:, :qn], ps[0:1, :qn]), waits=[ev_mm2], sig=True)
        sect.pool.release(i_s, ev_c1)
        sect.pool.release(i_q, ev_c2)

        ev_r1 = p.seq("vector", [
            lambda v, sl=sl: v.tensor_scalar_mul(stat_sum[:, sl], stat_sum[:, sl], 1.0 / 1024.0),
            lambda v, sl=sl: v.tensor_scalar_mul(stat_sq[:, sl], stat_sq[:, sl], 1.0 / 1024.0),
            lambda v, sl=sl: v.tensor_tensor(arow[:, sl], stat_sum[:, sl], stat_sum[:, sl], ALU.mult),
            lambda v, sl=sl: v.tensor_sub(stat_sq[:, sl], stat_sq[:, sl], arow[:, sl]),
            lambda v, sl=sl: v.tensor_scalar_add(stat_sq[:, sl], stat_sq[:, sl], 1e-5),
        ], waits=[ev_c1, ev_c2])
        ev_sqrt = p.op("scalar", lambda s, qn=qn: s.activation(
            arow[:, :qn], stat_sq[:, :qn], AF.Sqrt),
            waits=[ev_r1], sig=True)

        seq2 = [lambda v, sl=sl: v.reciprocal(arow[:, sl], arow[:, sl])]
        if keep_stats is not None:
            seq2.append(lambda v, sl=sl, q0=q0, qn=qn: v.tensor_copy(
                keep_stats[0][:, q0:q0 + qn], stat_sum[:, sl]))
            seq2.append(lambda v, sl=sl, q0=q0, qn=qn: v.tensor_copy(
                keep_stats[1][:, q0:q0 + qn], arow[:, sl]))
        seq2.append(lambda v, sl=sl: v.tensor_tensor(brow[:, sl], stat_sum[:, sl], arow[:, sl], ALU.mult))
        seq2.append(lambda v, sl=sl: v.tensor_scalar_mul(brow[:, sl], brow[:, sl], -1.0))
        ev_r2 = p.seq("vector", seq2, waits=[ev_sqrt])

        i_a, ps_a, rel_a = sect.pool.get()
        i_b, ps_b, rel_b = sect.pool.get()
        ev_ba = p.op("tensor", lambda t, qn=qn, ps=ps_a: t.matmul(
            ps[:, :qn], sect.ones_row[:, :], arow[:, :qn],
            start=True, stop=True), waits=[ev_r2, rel_a, sect.ones_ev], sig=True)
        ev_bb = p.op("tensor", lambda t, qn=qn, ps=ps_b: t.matmul(
            ps[:, :qn], sect.ones_row[:, :], brow[:, :qn],
            start=True, stop=True), waits=[ev_r2, rel_b], sig=True)
        row_free = ev_bb
        for dc in range(8):
            last_z = p.seq("vector", [
                lambda v, dc=dc, q0=q0, qn=qn, ps_a=ps_a: v.tensor_tensor(
                    sect.scr[1][:, :qn], x_tiles[dc][:, q0:q0 + qn],
                    ps_a[:, :qn], ALU.mult),
                lambda v, dc=dc, q0=q0, qn=qn, ps_b=ps_b: v.tensor_tensor(
                    z_tiles[dc][:, q0:q0 + qn],
                    sect.scr[1][:, :qn], ps_b[:, :qn], ALU.add),
            ], waits=[ev_ba, ev_bb, x_evs[dc]])
        sect.pool.release(i_a, last_z)
        sect.pool.release(i_b, last_z)
    return last_z


def emit_weight_load(sect, dram, rows, cols, name, cast=True, waits=()):
    assert rows <= 1024 and cols <= 1024 and cast
    idx, tiles, rel = sect.wpool_get()
    evs = []
    for rc in range(cdiv(rows, 128)):
        def b(g, t=tiles[rc], rc=rc):
            return g.dma_start(out=t[:, 0:cols],
                               in_=dram[rc * 128:(rc + 1) * 128, :])
        evs.append(sect.ld.put([b], waits=list(waits) + [rel]))
    return tiles, evs, idx


def emit_proj_fm(sect, z_tiles, z_ev, w_tiles, w_evs, out_cb,
                 out_cols=1024, kchunks=8):
    p, T = sect.p, sect.T
    for oc in range(cdiv(out_cols, 128)):
        on = min(128, out_cols - oc * 128)
        for qc in range(sect.NQC):
            q0 = qc * 512
            qn = min(512, T - q0)
            i_p, ps, rel = sect.pool.get()
            ev = None
            for dc in range(kchunks):
                ev = p.op("tensor", lambda t, dc=dc, oc=oc, on=on, q0=q0, qn=qn, ps=ps: t.matmul(
                    ps[:on, :qn], w_tiles[dc][:, oc * 128:oc * 128 + on],
                    z_tiles[dc][:, q0:q0 + qn],
                    start=(dc == 0), stop=(dc == kchunks - 1)),
                    waits=[z_ev, w_evs[dc]] + ([rel] if dc == 0 else []),
                    sig=(dc == kchunks - 1))
            out_cb(oc, qc, q0, qn, i_p, ps, ev)


def emit_qk_rope_to_hbm(sect, z_tiles, z_ev, w_tiles, w_evs, rt_cos, rt_sin,
                        rt_evs, out_dram, out_f32_tiles=None):
    p = sect.p
    if not callable(rt_cos):
        _rc, _rs = rt_cos, rt_sin
        rt_cos = lambda q0, qn: _rc[:, q0:q0 + qn]
        rt_sin = lambda q0, qn: _rs[:, q0:q0 + qn]

    if not hasattr(sect, "_scr3_prev"):
        sect._scr3_prev = {"ev": None}
    prev = sect._scr3_prev

    def cb2(oc, qc, q0, qn, i_p, ps, ev_mm):
        e1 = p.op("vector", lambda v, qn=qn, ps=ps: v.stream_shuffle(
            sect.scr[2][:, :qn], ps[:, :qn], PAIR_SWAP_MASK),
            waits=[ev_mm, prev["ev"]] + list(rt_evs), sig=True)
        e2 = p.op("vector", lambda v, q0=q0, qn=qn, ps=ps: v.tensor_tensor(
            sect.scr[1][:, :qn], ps[:, :qn], rt_cos(q0, qn), ALU.mult),
            waits=[e1], sig=True)
        e3 = p.op("vector", lambda v, q0=q0, qn=qn: v.tensor_tensor(
            sect.scr[2][:, :qn], sect.scr[2][:, :qn], rt_sin(q0, qn),
            ALU.mult), waits=[e1], sig=True)
        ev = p.op("vector", lambda v, qn=qn: v.tensor_tensor(
            sect.scr[3][:, :qn], sect.scr[1][:, :qn], sect.scr[2][:, :qn],
            ALU.add), waits=[e2, e3], sig=True)
        sect.pool.release(i_p, ev)

        def st(y, oc=oc, q0=q0, qn=qn):
            return y.dma_start(out=out_dram[oc * 128:(oc + 1) * 128, q0:q0 + qn],
                               in_=sect.scr[3][:, :qn])
        sev = sect.st.put([st], waits=[ev])
        sect.final_evs.append(sev)
        prev["ev"] = sev

    emit_proj_fm(sect, z_tiles, z_ev, w_tiles, w_evs, cb2)


def emit_v_tm_to_hbm(sect, z_tiles, z_ev, w_tiles, w_evs, out_dram,
                     bias_row=None, bias_ev=None):
    p, T = sect.p, sect.T
    if not hasattr(sect, "_scr3_prev"):
        sect._scr3_prev = {"ev": None}
    prev = sect._scr3_prev
    for tc in range(cdiv(T, 128)):
        t0 = tc * 128
        for half in range(2):
            i_p, ps, rel = sect.pool.get()
            ev = None
            for dc in range(8):
                last = (dc == 7 and bias_row is None)
                ev = p.op("tensor", lambda t, dc=dc, t0=t0, half=half, ps=ps, last=last: t.matmul(
                    ps[:, :], z_tiles[dc][:, t0:t0 + 128],
                    w_tiles[dc][:, half * 512:(half + 1) * 512],
                    start=(dc == 0), stop=last),
                    waits=[z_ev, w_evs[dc]] + ([rel] if dc == 0 else []),
                    sig=last)
            if bias_row is not None:
                ev = p.op("tensor", lambda t, half=half, ps=ps: t.matmul(
                    ps[:, :], sect.ones_row[:, :],
                    bias_row[:, half * 512:(half + 1) * 512],
                    start=False, stop=True),
                    waits=[bias_ev, sect.ones_ev], sig=True)
            ev_e = p.op("scalar", lambda s, ps=ps: s.copy(
                sect.scr[3][:, :], ps[:, :]),
                waits=[ev, prev["ev"]], sig=True)
            sect.pool.release(i_p, ev_e)
            sev = sect.st.put([lambda y, t0=t0, half=half: y.dma_start(
                out=out_dram[t0:t0 + 128, half * 512:(half + 1) * 512],
                in_=sect.scr[3][:, :])], waits=[ev_e])
            sect.final_evs.append(sev)
            prev["ev"] = sev


def finish(p, sect):
    p.op("sync", lambda y: None, waits=sect.final_evs, sig=False)



def rope_tables_L0(positions):
    T = len(positions)
    dims = np.arange(DK // 2, dtype=np.float64)
    th = positions[:, None].astype(np.float64) / (10000.0 ** (2.0 * dims / DK))[None, :]
    cos = np.repeat(np.cos(th), 2, axis=-1).T
    sin = np.repeat(np.sin(th), 2, axis=-1).T
    sgn = np.tile(np.array([-1.0, 1.0]), DK // 2)[:, None]
    sinS = sin * sgn
    return (np.concatenate([cos, cos], 0).astype(np.float32),
            np.concatenate([sinS, sinS], 0).astype(np.float32))


def build_pre(T):
    p = Prog("pre")
    sect = Sect(p, T)
    x_d = p.in_("x", [D, T], F32).ap()
    wq_d = p.in_("wqt", [D, D], F32).ap()
    wk_d = p.in_("wkt", [D, D], F32).ap()
    wv_d = p.in_("wvt", [D, D], F32).ap()
    rtc_d = p.in_("rtc", [128, T], F32).ap()
    rts_d = p.in_("rts", [128, T], F32).ap()
    q_o = p.out("q", [D, T], F32).ap()
    k_o = p.out("k", [D, T], F32).ap()
    v_o = p.out("v", [T, D], F32).ap()

    x_tiles = [p.sbuf(f"x{i}", [128, T], F32) for i in range(8)]
    z_tiles = [p.sbuf(f"z{i}", [128, T], BF16) for i in range(8)]
    rtc = p.sbuf("rtc_s", [128, T], F32)
    rts = p.sbuf("rts_s", [128, T], F32)

    x_evs = emit_load_fm(sect, x_d, x_tiles)
    ev_rt = sect.act_ld.put([lambda g: g.dma_start(out=rtc[:, :], in_=rtc_d)])
    ev_rt2 = sect.act_ld.put([lambda g: g.dma_start(out=rts[:, :], in_=rts_d)])

    z_ev = emit_ln(sect, x_tiles, x_evs, z_tiles)

    wq_t, wq_e, wq_i = emit_weight_load(sect, wq_d, D, D, "wq")
    emit_qk_rope_to_hbm(sect, z_tiles, z_ev, wq_t, wq_e, rtc, rts, [ev_rt, ev_rt2], q_o)
    sect.wpool_release(wq_i, Ev(p.clock_sem["tensor"], p.clock["tensor"]))
    wk_t, wk_e, wk_i = emit_weight_load(sect, wk_d, D, D, "wk")
    emit_qk_rope_to_hbm(sect, z_tiles, z_ev, wk_t, wk_e, rtc, rts, [ev_rt, ev_rt2], k_o)
    sect.wpool_release(wk_i, Ev(p.clock_sem["tensor"], p.clock["tensor"]))
    wv_t, wv_e, wv_i = emit_weight_load(sect, wv_d, D, D, "wv")
    emit_v_tm_to_hbm(sect, z_tiles, z_ev, wv_t, wv_e, v_o)
    sect.wpool_release(wv_i, Ev(p.clock_sem["tensor"], p.clock["tensor"]))

    finish(p, sect)
    return p.build()




TRI_OFFS = (0, 128, 256, 384)


def tri_mask_np(off):
    i = np.arange(128)[:, None]
    j = np.arange(512)[None, :]
    return (j >= off + i).astype(BF)


def build_attn_L0(NP=8, T=S):
    p = Prog("attnL0")
    sect = Sect(p, T, n_pool=5)
    qf_d = p.in_("qf", [NP, 64, T], F32).ap()
    kf_d = p.in_("kf", [NP, 64, T], F32).ap()
    vt_d = p.in_("vt", [NP, T, 68], F32).ap()
    of_d = p.out("of", [NP, 64, T], F32).ap()

    NKT = T // 128
    NQC = T // 512
    scale = 1.0 / 8.0

    tri_sb = []
    tri_evs = []
    for off in TRI_OFFS:
        c = p.const(f"tri{off}", tri_mask_np(off))
        t = p.sbuf(f"tri{off}_s", [128, 512], BF16)
        tri_sb.append(t)
        tri_evs.append(sect.ld.put([lambda g, t=t, c=c: g.dma_start(out=t[:, :], in_=c.ap())]))

    qf_sb = [p.sbuf(f"qf{i}", [64, T], BF16) for i in range(2)]
    kf_sb = [p.sbuf(f"kf{i}", [64, T], BF16) for i in range(2)]
    vt_sb = [p.sbuf(f"vt{i}", [128, NKT, 68], BF16) for i in range(2)]
    pair_free = [None, None]

    a_sb = [p.sbuf(f"a{i}", [128, 512], BF16) for i in range(3)]
    a_free = [None] * 3
    o_acc = [p.psum(f"oacc{i}", [128, 512], F32) for i in range(2)]
    o_free = [None, None]
    o_sb = [p.sbuf(f"o_sb{i}", [128, 512], F32) for i in range(2)]
    bc_sb = [p.sbuf(f"bc_sb{i}", [64, 512], F32) for i in range(2)]
    nrm_row = [p.sbuf(f"nrm_row{i}", [1, 512], F32) for i in range(2)]
    ost_free = [None, None]
    bc_free = [None, None]

    for pp in range(NP):
        buf = pp % 2
        ev_q = sect.ld.put([lambda g, pp=pp, buf=buf: g.dma_start(
            out=qf_sb[buf][:, :], in_=qf_d[pp])], waits=[pair_free[buf]])
        ev_k = sect.ld.put([lambda g, pp=pp, buf=buf: g.dma_start(
            out=kf_sb[buf][:, :], in_=kf_d[pp])])
        ev_v = sect.ld.put([lambda g, pp=pp, buf=buf: g.dma_start(
            out=vt_sb[buf][:, :, :],
            in_=vt_d[pp].rearrange("(kt p) c -> p kt c", p=128))])
        last_compute = None
        for qc in range(NQC):
            q0 = qc * 512
            ob = qc % 2
            kt_hi = min(NKT, (q0 + 512) // 128)
            ev_av = None
            for kt in range(kt_hi):
                k0 = kt * 128
                i_s, ps, rel = sect.pool.get()
                ev_mm = p.op("tensor", lambda t, buf=buf, k0=k0, q0=q0, ps=ps: t.matmul(
                    ps[:, :], kf_sb[buf][:, k0:k0 + 128], qf_sb[buf][:, q0:q0 + 512],
                    start=True, stop=True),
                    waits=[ev_q, ev_k, rel], sig=True)
                ai = (qc * kt_hi + kt) % 3
                ev_exp = p.op("scalar", lambda s, ai=ai, ps=ps: s.activation(
                    a_sb[ai][:, :], ps[:, :], AF.Exp, scale=scale),
                    waits=[ev_mm, a_free[ai]], sig=True)
                sect.pool.release(i_s, ev_exp)
                if k0 >= q0:
                    ev_exp = p.op("vector", lambda v, ai=ai, off=k0 - q0: v.tensor_mul(
                        a_sb[ai][:, :], a_sb[ai][:, :], tri_sb[TRI_OFFS.index(off)][:, :]),
                        waits=[ev_exp] + tri_evs, sig=True)
                ev_av = p.op("tensor", lambda t, buf=buf, kt=kt, ai=ai, ob=ob: t.matmul(
                    o_acc[ob][0:68, :], vt_sb[buf][:, kt, :], a_sb[ai][:, :],
                    start=(kt == 0), stop=(kt == kt_hi - 1)),
                    waits=[ev_exp, ev_v] + ([o_free[ob]] if kt == 0 else []), sig=True)
                a_free[ai] = ev_av
            ev_r = p.op("vector", lambda v, ob=ob: v.reciprocal(
                nrm_row[ob][:, :], o_acc[ob][64:65, :]), waits=[ev_av], sig=True)
            i_b, ps_b, rel_b = sect.pool.get()
            ev_bc = p.op("tensor", lambda t, ob=ob, ps_b=ps_b: t.matmul(
                ps_b[0:64, :], sect.ones_row[:, 0:64], nrm_row[ob][:, :],
                start=True, stop=True), waits=[ev_r, rel_b, sect.ones_ev], sig=True)
            ev_ev = p.op("scalar", lambda s, ob=ob, ps_b=ps_b: s.copy(
                bc_sb[ob][0:64, :], ps_b[0:64, :]), waits=[ev_bc, bc_free[ob]], sig=True)
            sect.pool.release(i_b, ev_ev)
            ev_on = p.op("vector", lambda v, ob=ob: v.tensor_mul(
                o_sb[ob][0:64, :], o_acc[ob][0:64, :], bc_sb[ob][0:64, :]),
                waits=[ev_ev, ost_free[ob]], sig=True)
            bc_free[ob] = ev_on
            o_free[ob] = ev_on
            sev = sect.st.put([lambda y, pp=pp, q0=q0, ob=ob: y.dma_start(
                out=of_d[pp][:, q0:q0 + 512], in_=o_sb[ob][0:64, :])], waits=[ev_on])
            sect.final_evs.append(sev)
            ost_free[ob] = sev
            last_compute = ev_on
        pair_free[buf] = last_compute
    finish(p, sect)
    return p.build()




def build_attn_col(flavor, NP=8, T=S):
    causal = flavor == "L1"
    scale = 0.125 if flavor == "L1" else 1.0 / 32.0
    p = Prog("attn_" + flavor)
    sect = Sect(p, T, n_pool=4)
    qf_d = p.in_("qf", [NP, 64, T], F32).ap()
    kf_d = p.in_("kf", [NP, 64, T], F32).ap()
    vt_d = p.in_("vt", [NP, T, 64], F32).ap()
    of_d = p.out("of", [NP, 64, T], F32).ap()

    NKT = T // 128
    NQC = T // 512

    tri_sb = []
    tri_evs = []
    for off in TRI_OFFS:
        c = p.const(f"tri{off}", tri_mask_np(off))
        t = p.sbuf(f"tri{off}_s", [128, 512], BF16)
        tri_sb.append(t)
        tri_evs.append(sect.ld.put([lambda g, t=t, c=c: g.dma_start(out=t[:, :], in_=c.ap())]))

    qf_sb = [p.sbuf(f"qf{i}", [64, T], BF16) for i in range(2)]
    kf_sb = [p.sbuf(f"kf{i}", [64, T], BF16) for i in range(2)]
    vt_sb = [p.sbuf(f"vt{i}", [128, NKT, 64], BF16) for i in range(2)]
    pair_free = [None, None]

    a_sb = [p.sbuf(f"a{i}", [128, T], BF16) for i in range(2)]
    a_last_use = [dict(), dict()]
    rs = p.sbuf("rs", [128, NQC], F32)
    zr = p.sbuf("zr", [128, 1], F32)
    vs_sb = [p.sbuf(f"vs{i}", [128, 64], BF16) for i in range(2)]
    vs_free = [None, None]
    o_acc = [p.psum(f"oacc{i}", [128, 512], F32) for i in range(4)]
    o_free = [None] * 4
    o_sb = [p.sbuf(f"o_sb{i}", [128, 512], F32) for i in range(2)]
    ost_free = [[None, None], [None, None]]
    prev_z = None

    def last_kt(qc):
        return (NKT - 1) if not causal else min(NKT, (qc * 512 + 512) // 128) - 1

    for pp in range(NP):
        buf = pp % 2
        ev_q = sect.ld.put([lambda g, pp=pp, buf=buf: g.dma_start(
            out=qf_sb[buf][:, :], in_=qf_d[pp])], waits=[pair_free[buf]])
        ev_k = sect.ld.put([lambda g, pp=pp, buf=buf: g.dma_start(
            out=kf_sb[buf][:, :], in_=kf_d[pp])])
        ev_v = sect.ld.put([lambda g, pp=pp, buf=buf: g.dma_start(
            out=vt_sb[buf][:, :, :],
            in_=vt_d[pp].rearrange("(kt p) c -> p kt c", p=128))])
        last_av = None
        for kt in range(NKT):
            k0 = kt * 128
            ab = kt % 2
            qc_lo = (k0 // 512) if causal else 0
            a_ready = {}
            rs_evs = []
            for qc in range(qc_lo, NQC):
                q0 = qc * 512
                i_s, ps, rel = sect.pool.get()
                ev_mm = p.op("tensor", lambda t, buf=buf, k0=k0, q0=q0, ps=ps: t.matmul(
                    ps[:, :], kf_sb[buf][:, k0:k0 + 128], qf_sb[buf][:, q0:q0 + 512],
                    start=True, stop=True),
                    waits=[ev_q, ev_k, rel], sig=True)
                war = [a_last_use[ab].get(qc), prev_z]
                boundary = causal and qc == qc_lo
                if boundary:
                    ev_exp = p.op("scalar", lambda s, ab=ab, qc=qc, ps=ps: s.activation(
                        a_sb[ab][:, qc * 512:(qc + 1) * 512], ps[:, :], AF.Exp,
                        scale=scale),
                        waits=[ev_mm] + war, sig=True)
                    off = k0 - q0
                    ev_m = p.op("vector", lambda v, ab=ab, qc=qc, off=off: v.tensor_mul(
                        a_sb[ab][:, qc * 512:(qc + 1) * 512],
                        a_sb[ab][:, qc * 512:(qc + 1) * 512],
                        tri_sb[TRI_OFFS.index(off)][:, :]),
                        waits=[ev_exp] + tri_evs, sig=True)
                    ev_rs = p.op("vector", lambda v, ab=ab, qc=qc: v.tensor_reduce(
                        rs[:, qc:qc + 1], a_sb[ab][:, qc * 512:(qc + 1) * 512],
                        mybir.AxisListType.X, ALU.add), waits=[ev_m], sig=True)
                    a_ready[qc] = ev_rs
                    rs_evs.append(ev_rs)
                    ev_done = ev_rs
                else:
                    ev_exp = p.op("scalar", lambda s, ab=ab, qc=qc, ps=ps: s.activation(
                        a_sb[ab][:, qc * 512:(qc + 1) * 512], ps[:, :], AF.Exp,
                        scale=scale, accum_out=rs[:, qc:qc + 1]),
                        waits=[ev_mm] + war, sig=True)
                    a_ready[qc] = ev_exp
                    rs_evs.append(ev_exp)
                    ev_done = ev_exp
                sect.pool.release(i_s, ev_done)
            nqc = NQC - qc_lo
            ev_z = p.op("vector", lambda v, qc_lo=qc_lo, nqc=nqc: v.tensor_reduce(
                zr[:, :], rs[:, qc_lo:qc_lo + nqc], mybir.AxisListType.X, ALU.add),
                waits=rs_evs, sig=True)
            prev_z = ev_z
            ev_zr = p.op("vector", lambda v: v.reciprocal(zr[:, :], zr[:, :]),
                         waits=[ev_z], sig=True)
            vb = kt % 2
            ev_vs = p.op("vector", lambda v, buf=buf, kt=kt, vb=vb: v.tensor_scalar_mul(
                vs_sb[vb][:, :], vt_sb[buf][:, kt, :], zr[:, :]),
                waits=[ev_zr, ev_v, vs_free[vb]], sig=True)
            ev_av = None
            for qc in range(qc_lo, NQC):
                bank, half = qc // 2, qc % 2
                is_first = (kt == 0)
                is_last = (kt == last_kt(qc))
                ev_av = p.op("tensor", lambda t, ab=ab, qc=qc, bank=bank, half=half,
                             vb=vb, is_first=is_first, is_last=is_last: t.matmul(
                    o_acc[bank][half * 64:(half + 1) * 64, :], vs_sb[vb][:, :],
                    a_sb[ab][:, qc * 512:(qc + 1) * 512],
                    start=is_first, stop=is_last),
                    waits=[ev_vs, a_ready[qc]] +
                          ([o_free[bank]] if is_first and half == 0 else []),
                    sig=True)
                a_last_use[ab][qc] = ev_av
                last_av = ev_av
                bank_done = is_last and (qc == 2 * bank + 1 or qc == NQC - 1 or
                                         last_kt(2 * bank + 1) == last_kt(qc) and half == 1)
                if is_last and half == 1:
                    sb_i = bank % 2
                    for hh, qq in ((0, qc - 1), (1, qc)):
                        ev_e = p.op("scalar", lambda s, bank=bank, hh=hh, sb_i=sb_i: s.copy(
                            o_sb[sb_i][hh * 64:(hh + 1) * 64, :],
                            o_acc[bank][hh * 64:(hh + 1) * 64, :]),
                            waits=[ev_av, ost_free[sb_i][hh]], sig=True)
                        sev = sect.st.put([lambda y, pp=pp, qq=qq, hh=hh, sb_i=sb_i: y.dma_start(
                            out=of_d[pp][:, qq * 512:(qq + 1) * 512],
                            in_=o_sb[sb_i][hh * 64:(hh + 1) * 64, :])], waits=[ev_e])
                        sect.final_evs.append(sev)
                        ost_free[sb_i][hh] = sev
                        o_free[bank] = ev_e
            vs_free[vb] = ev_av
        pair_free[buf] = last_av
    finish(p, sect)
    return p.build()


ALIBI_CUT = 36.0


def alibi_slope(h):
    start = 2.0 ** (-2.0 ** (-(np.log2(16) - 3)))
    return start * start ** h


def build_attn_L2(NP=8, T=S):
    p = Prog("attnL2")
    sect = Sect(p, T, n_pool=3)
    qf_d = p.in_("qf", [NP, 64, T], F32).ap()
    kf_d = p.in_("kf", [NP, 64, T], F32).ap()
    vt_d = p.in_("vt", [NP, T, 68], F32).ap()
    eb_d = p.in_("eb", [NP, 128, 64], F32).ap()
    sid_d = p.in_("sid", [NP, 128, 128], F32).ap()
    grow_d = p.in_("grow", [NP, 1, 1024], F32).ap()
    of_d = p.out("of", [NP, 64, T], F32).ap()

    NKT = T // 128
    NQC = T // 512
    scale = 0.125

    abs1_sb, abs2_sb = [], []
    abs_evs = []
    for d in range(-1, 4):
        i = np.arange(128)[None, :, None]
        j = np.arange(512)[None, None, :]
        ad = np.abs(j - d * 128 - i).astype(np.float64)[0]
        a1 = -np.minimum(ad, 256.0)
        a2 = -np.maximum(ad - 256.0, 0.0)
        c1 = p.const(f"abs1_{d+1}", a1.astype(BF))
        c2 = p.const(f"abs2_{d+1}", a2.astype(BF))
        t1 = p.sbuf(f"abs1s_{d+1}", [128, 512], BF16)
        t2 = p.sbuf(f"abs2s_{d+1}", [128, 512], BF16)
        abs1_sb.append(t1)
        abs2_sb.append(t2)
        abs_evs.append(sect.ld.put([lambda g, t=t1, c=c1: g.dma_start(out=t[:, :], in_=c.ap())]))
        abs_evs.append(sect.ld.put([lambda g, t=t2, c=c2: g.dma_start(out=t[:, :], in_=c.ap())]))

    qf_sb = [p.sbuf(f"qf{i}", [64, T], BF16) for i in range(2)]
    kf_sb = [p.sbuf(f"kf{i}", [64, T], BF16) for i in range(2)]
    vt_sb = [p.sbuf(f"vt{i}", [128, NKT, 68], BF16) for i in range(2)]
    eb_sb = [p.sbuf(f"eb{i}", [128, 64], F32) for i in range(2)]
    sid_sb = [p.sbuf(f"sid{i}", [128, 128], BF16) for i in range(2)]
    gr_sb = [p.sbuf(f"gr{i}", [1, 1024], F32) for i in range(2)]
    pair_free = [None, None]

    a_sb = [p.sbuf(f"a{i}", [128, 512], BF16) for i in range(3)]
    a_free = [None] * 3
    o_acc = [p.psum(f"oacc{i}", [128, 512], F32) for i in range(3)]
    o_free = [None, None, None]
    gb_sb = [p.sbuf(f"gb_sb{i}", [128, 512], F32) for i in range(2)]
    ga_sb = [p.sbuf(f"ga_sb{i}", [128, 512], F32) for i in range(2)]
    o_sb = [p.sbuf(f"o_sb{i}", [128, 512], F32) for i in range(2)]
    t2_sb = [p.sbuf(f"t2_sb{i}", [128, 512], F32) for i in range(2)]
    bc_sb = [p.sbuf(f"bc_sb{i}", [64, 512], F32) for i in range(2)]
    nrm_row = [p.sbuf(f"nrm{i}", [1, 512], F32) for i in range(2)]
    ost_free = [None, None]
    bc_free = [None, None]
    gbc_free = [None, None]

    def classify(slot, qc):
        q0 = qc * 512
        slope_wide = alibi_slope(min(2 * slot + 1, 15))
        out = []
        for kt in range(NKT):
            k0 = kt * 128
            if k0 + 127 < q0:
                mind = q0 - (k0 + 127)
                cls = "b"
            elif k0 > q0 + 511:
                mind = k0 - (q0 + 511)
                cls = "a"
            else:
                mind = 0
                cls = "s"
            if slope_wide * mind > ALIBI_CUT:
                continue
            out.append((kt, cls, (k0 - q0) // 128))
        return out

    for pp in range(NP):
        buf = pp % 2
        ev_q = sect.ld.put([lambda g, pp=pp, buf=buf: g.dma_start(
            out=qf_sb[buf][:, :], in_=qf_d[pp])], waits=[pair_free[buf]])
        ev_k = sect.ld.put([lambda g, pp=pp, buf=buf: g.dma_start(
            out=kf_sb[buf][:, :], in_=kf_d[pp])])
        ev_v = sect.ld.put([lambda g, pp=pp, buf=buf: g.dma_start(
            out=vt_sb[buf][:, :, :],
            in_=vt_d[pp].rearrange("(kt p) c -> p kt c", p=128))])
        ev_e = sect.ld.put([
            lambda g, pp=pp, buf=buf: g.dma_start(out=eb_sb[buf][:, :], in_=eb_d[pp]),
            lambda g, pp=pp, buf=buf: g.dma_start(out=sid_sb[buf][:, :], in_=sid_d[pp]),
            lambda g, pp=pp, buf=buf: g.dma_start(out=gr_sb[buf][:, :], in_=grow_d[pp]),
        ])
        last_on = None
        for qc in range(NQC):
            q0 = qc * 512
            qb = qc % 2
            tiles = classify(pp, qc)
            have = {c for _, c, _ in tiles}
            nt = {"b": 0, "a": 0, "s": 0}
            for _, c, _ in tiles:
                nt[c] += 1
            done = {"b": 0, "a": 0, "s": 0}
            ev_avs = {}
            for kt, cls, dist in tiles:
                k0 = kt * 128
                i_s, ps, rel = sect.pool.get()
                ev_mm = p.op("tensor", lambda t, buf=buf, k0=k0, q0=q0, ps=ps: t.matmul(
                    ps[:, :], kf_sb[buf][:, k0:k0 + 128], qf_sb[buf][:, q0:q0 + 512],
                    start=True, stop=(cls != "s")),
                    waits=[ev_q, ev_k, rel], sig=(cls != "s"))
                if cls == "s":
                    di = dist + 1
                    p.op("tensor", lambda t, buf=buf, di=di, ps=ps: t.matmul(
                        ps[:, :], sid_sb[buf][:, :], abs1_sb[di][:, :],
                        start=False, stop=False), waits=[ev_e] + abs_evs)
                    ev_mm = p.op("tensor", lambda t, buf=buf, di=di, ps=ps: t.matmul(
                        ps[:, :], sid_sb[buf][:, :], abs2_sb[di][:, :],
                        start=False, stop=True), waits=(), sig=True)
                ai = sum(done.values()) % 3
                if cls == "s":
                    ev_exp = p.op("scalar", lambda s, ai=ai, ps=ps: s.activation(
                        a_sb[ai][:, :], ps[:, :], AF.Exp, scale=scale),
                        waits=[ev_mm, a_free[ai]], sig=True)
                else:
                    dcol = dist + 32
                    ev_exp = p.op("scalar", lambda s, ai=ai, buf=buf, dcol=dcol, ps=ps: s.activation(
                        a_sb[ai][:, :], ps[:, :], AF.Exp, scale=scale,
                        bias=eb_sb[buf][:, dcol:dcol + 1]),
                        waits=[ev_mm, a_free[ai], ev_e], sig=True)
                sect.pool.release(i_s, ev_exp)
                ob = {"b": 0, "a": 1, "s": 2}[cls]
                first = done[cls] == 0
                last = done[cls] == nt[cls] - 1
                ev_av = p.op("tensor", lambda t, buf=buf, kt=kt, ai=ai, ob=ob,
                             first=first, last=last: t.matmul(
                    o_acc[ob][0:68, :], vt_sb[buf][:, kt, :], a_sb[ai][:, :],
                    start=first, stop=last),
                    waits=[ev_exp, ev_v] + ([o_free[ob]] if first else []),
                    sig=True)
                a_free[ai] = ev_av
                done[cls] += 1
                ev_avs[ob] = ev_av
            cevs = []
            if "b" in have or "a" in have:
                i_g, ps_g, rel_g = sect.pool.get()
                ev_g1 = p.op("tensor", lambda t, buf=buf, ps=ps_g: t.matmul(
                    ps[0:68, :], sect.ones_row[:, 0:68], gr_sb[buf][:, 0:512],
                    start=True, stop=True), waits=[ev_e, sect.ones_ev, rel_g], sig=True)
                ev_ge = p.op("scalar", lambda s, qb=qb, ps=ps_g: s.copy(
                    gb_sb[qb][0:68, :], ps[0:68, :]), waits=[ev_g1, gbc_free[qb]], sig=True)
                ev_g2 = p.op("tensor", lambda t, buf=buf, ps=ps_g: t.matmul(
                    ps[0:68, :], sect.ones_row[:, 0:68], gr_sb[buf][:, 512:1024],
                    start=True, stop=True), waits=[ev_ge], sig=True)
                ev_ge2 = p.op("scalar", lambda s, qb=qb, ps=ps_g: s.copy(
                    ga_sb[qb][0:68, :], ps[0:68, :]), waits=[ev_g2], sig=True)
                sect.pool.release(i_g, ev_ge2)
            seqs = []
            if "b" in have:
                seqs.append(lambda v, qb=qb: v.tensor_tensor(
                    o_sb[qb][0:68, :], o_acc[0][0:68, :], gb_sb[qb][0:68, :], ALU.mult))
                if "a" in have:
                    seqs.append(lambda v, qb=qb: v.tensor_tensor(
                        t2_sb[qb][0:68, :], o_acc[1][0:68, :], ga_sb[qb][0:68, :], ALU.mult))
                    seqs.append(lambda v, qb=qb: v.tensor_tensor(
                        o_sb[qb][0:68, :], o_sb[qb][0:68, :], t2_sb[qb][0:68, :], ALU.add))
            elif "a" in have:
                seqs.append(lambda v, qb=qb: v.tensor_tensor(
                    o_sb[qb][0:68, :], o_acc[1][0:68, :], ga_sb[qb][0:68, :], ALU.mult))
            if "s" in have:
                if "b" in have or "a" in have:
                    seqs.append(lambda v, qb=qb: v.tensor_tensor(
                        o_sb[qb][0:68, :], o_sb[qb][0:68, :], o_acc[2][0:68, :], ALU.add))
                else:
                    seqs.append(lambda v, qb=qb: v.tensor_copy(
                        o_sb[qb][0:68, :], o_acc[2][0:68, :]))
            wl = [ev_avs.get(0), ev_avs.get(1), ev_avs.get(2), ost_free[qb]]
            if "b" in have or "a" in have:
                wl.append(ev_ge2)
            ev_comb = p.seq("vector", seqs, waits=wl)
            for ob in (0, 1, 2):
                if ob in ev_avs:
                    o_free[ob] = ev_comb
            if "b" in have or "a" in have:
                gbc_free[qb] = ev_comb
            ev_r = p.op("vector", lambda v, qb=qb: v.reciprocal(
                nrm_row[qb][:, :], o_sb[qb][64:65, :]), waits=[ev_comb], sig=True)
            i_b2, ps_b2, rel_b2 = sect.pool.get()
            ev_bc = p.op("tensor", lambda t, qb=qb, ps=ps_b2: t.matmul(
                ps[0:64, :], sect.ones_row[:, 0:64], nrm_row[qb][:, :],
                start=True, stop=True), waits=[ev_r, rel_b2], sig=True)
            ev_ev = p.op("scalar", lambda s, qb=qb, ps=ps_b2: s.copy(
                bc_sb[qb][0:64, :], ps[0:64, :]), waits=[ev_bc, bc_free[qb]], sig=True)
            sect.pool.release(i_b2, ev_ev)
            ev_on = p.op("vector", lambda v, qb=qb: v.tensor_mul(
                o_sb[qb][0:64, :], o_sb[qb][0:64, :], bc_sb[qb][0:64, :]),
                waits=[ev_ev], sig=True)
            bc_free[qb] = ev_on
            sev = sect.st.put([lambda y, pp=pp, q0=q0, qb=qb: y.dma_start(
                out=of_d[pp][:, q0:q0 + 512], in_=o_sb[qb][0:64, :])], waits=[ev_on])
            sect.final_evs.append(sev)
            ost_free[qb] = sev
            last_on = sev
        pair_free[buf] = last_on
    finish(p, sect)
    return p.build()


L3A_LD = 4608
L3A_JOFF = 4095


def build_attn_L3a(NP=8, T=S, nobias=False, zerobias=False):
    p = Prog("attnL3a")
    sect = Sect(p, T, n_pool=4)
    qf_d = p.in_("qf", [NP, 64, T], F32).ap()
    kf_d = p.in_("kf", [NP, 64, T], F32).ap()
    vt_d = p.in_("vt", [NP, T, 68], F32).ap()
    relr_d = p.in_("relr", [NP, 64, L3A_LD], F32).ap()
    of_d = p.out("of", [NP, 64, T], F32).ap()
    tbuf = [p.internal(f"t3a_{i}", [T * L3A_LD], BF16) for i in range(2)]

    NKT = T // 128
    NQC = T // 512
    scale = 0.125

    tri_sb = []
    tri_evs = []
    for off in TRI_OFFS:
        c = p.const(f"tri{off}", tri_mask_np(off))
        t = p.sbuf(f"tri{off}_s", [128, 512], BF16)
        tri_sb.append(t)
        tri_evs.append(sect.ld.put([lambda g, t=t, c=c: g.dma_start(out=t[:, :], in_=c.ap())]))
    idn, id_ev = sect.ident()
    if zerobias:
        zc = p.const("zeros128", np.zeros((128, 128), BF))
        ztile = p.sbuf("ztile", [128, 128], BF16)
        zev = sect.ld.put([lambda g: g.dma_start(out=ztile[:, :], in_=zc.ap())])

    qf_sb = [p.sbuf(f"qf{i}", [64, T], BF16) for i in range(2)]
    kf_sb = [p.sbuf(f"kf{i}", [64, T], BF16) for i in range(2)]
    vt_sb = [p.sbuf(f"vt{i}", [128, NKT, 68], BF16) for i in range(2)]
    rel_sb = [p.sbuf(f"rel{i}", [64, L3A_LD], BF16) for i in range(2)]
    pair_free = [None, None]

    tw_sb = [p.sbuf(f"tw{i}", [128, 512], BF16) for i in range(2)]
    bias_sb = [p.sbuf(f"bias{i}", [128, 4, 2048], BF16) for i in range(2)]
    bias_free = [None, None]
    a_sb = [p.sbuf(f"a{i}", [128, 512], BF16) for i in range(3)]
    a_free = [None] * 3
    o_acc = [p.psum(f"oacc{i}", [128, 512], F32) for i in range(2)]
    o_free = [None, None]
    o_sb = [p.sbuf(f"o_sb{i}", [128, 512], F32) for i in range(2)]
    bc_sb = [p.sbuf(f"bc_sb{i}", [64, 512], F32) for i in range(2)]
    nrm_row = [p.sbuf(f"nrm{i}", [1, 512], F32) for i in range(2)]
    ost_free = [None, None]
    bc_free = [None, None]
    tread_done = [None, None]
    tread_done2 = [None, None]
    twr = Stream(p, "sync", "twr", 4)
    brd = Stream(p, "sync", "brd", 2)

    for pp in range(NP):
        buf = pp % 2
        tb = tbuf[pp % 2]
        ev_q = sect.ld.put([lambda g, pp=pp, buf=buf: g.dma_start(
            out=qf_sb[buf][:, :], in_=qf_d[pp])], waits=[pair_free[buf]])
        ev_k = sect.ld.put([lambda g, pp=pp, buf=buf: g.dma_start(
            out=kf_sb[buf][:, :], in_=kf_d[pp])])
        ev_v = sect.ld.put([lambda g, pp=pp, buf=buf: g.dma_start(
            out=vt_sb[buf][:, :, :],
            in_=vt_d[pp].rearrange("(kt p) c -> p kt c", p=128))])
        ev_rel = sect.ld.put([lambda g, pp=pp, buf=buf: g.dma_start(
            out=rel_sb[buf][:, :], in_=relr_d[pp])])
        tw_evs = []
        ti_free = [None, None]
        for qt in range(NKT):
            q0 = qt * 128
            jc_lo = max(0, (L3A_JOFF - q0 - 127) // 512)
            for jc in range(jc_lo, L3A_LD // 512):
                c0 = jc * 512
                i_p, ps, rel = sect.pool.get()
                ev_mm = p.op("tensor", lambda t, buf=buf, q0=q0, c0=c0, ps=ps: t.matmul(
                    ps[:, :], qf_sb[buf][:, q0:q0 + 128], rel_sb[buf][:, c0:c0 + 512],
                    start=True, stop=True),
                    waits=[ev_q, ev_rel, rel, tread_done[pp % 2],
                           tread_done2[pp % 2]], sig=True)
                wi = (qt * 9 + jc) % 2
                ev_e = p.op("scalar", lambda s, wi=wi, ps=ps: s.copy(
                    tw_sb[wi][:, :], ps[:, :]), waits=[ev_mm, ti_free[wi]], sig=True)
                sect.pool.release(i_p, ev_e)
                sev = twr.put([lambda y, q0=q0, c0=c0, wi=wi, tb=tb: y.dma_start(
                    out=bass.AP(tensor=tb.ap().tensor, offset=q0 * L3A_LD + c0,
                                ap=[[L3A_LD, 128], [1, 512]]),
                    in_=tw_sb[wi][:, :])], waits=[ev_e])
                ti_free[wi] = sev
                tw_evs.append(sev)
        tw_done = twr.drain_evs()
        last_on = None
        last_bias_read = None
        for qc in range(NQC):
            q0 = qc * 512
            ob = qc % 2
            kt_hi = min(NKT, (q0 + 512) // 128)
            khn = cdiv(kt_hi * 128, 2048)
            br_evs = []
            for kh in range(khn):
                kbase = kh * 2048
                kn = min(2048, kt_hi * 128 - kbase)
                def bld(y, q0=q0, kbase=kbase, kn=kn, kh=kh, tb=tb):
                    off = (L3A_LD - 1) * q0 + L3A_JOFF + kbase - 0
                    return y.dma_start(
                        out=bias_sb[kh][:, :, 0:kn],
                        in_=bass.AP(tensor=tb.ap().tensor, offset=off,
                                    ap=[[L3A_LD - 1, 128],
                                        [(L3A_LD - 1) * 128, 4], [1, kn]]))
                br_evs.append(brd.put([bld], waits=tw_done + [bias_free[kh]]))
            last_bias_read = br_evs[-1]
            ev_av = None
            for kt in range(kt_hi):
                k0 = kt * 128
                kh = k0 // 2048
                kloc = k0 % 2048
                i_s, ps, rel = sect.pool.get()
                ev_mm = p.op("tensor", lambda t, buf=buf, k0=k0, q0=q0, ps=ps, nobias=nobias: t.matmul(
                    ps[:, :], kf_sb[buf][:, k0:k0 + 128], qf_sb[buf][:, q0:q0 + 512],
                    start=True, stop=nobias),
                    waits=[ev_q, ev_k, rel], sig=nobias)
                ev_b = None
                subs = [] if nobias else list(range(4))
                for sub in subs:
                    qs_global = qc * 4 + sub
                    if qs_global * 128 + 127 < k0:
                        continue
                    lastb = (sub == 3) or all(
                        (qc * 4 + s2) * 128 + 127 < k0 for s2 in range(sub + 1, 4))
                    if zerobias:
                        ev_b = p.op("tensor", lambda t, kh=kh, kloc=kloc, sub=sub,
                                    lastb=lastb, ps=ps: t.matmul(
                            ps[:, sub * 128:(sub + 1) * 128],
                            ztile[:, :],
                            idn[:, :], start=False, stop=lastb),
                            waits=[br_evs[kh], id_ev, zev], sig=lastb)
                    else:
                        ev_b = p.op("tensor", lambda t, kh=kh, kloc=kloc, sub=sub,
                                    qs_global=qs_global, lastb=lastb, ps=ps: t.matmul(
                            ps[:, sub * 128:(sub + 1) * 128],
                            bias_sb[kh][:, qs_global % 4, kloc:kloc + 128],
                            idn[:, :], start=False, stop=lastb),
                            waits=[br_evs[kh], id_ev], sig=lastb)
                if not nobias:
                    ev_mm = ev_b
                ai = (qc * kt_hi + kt) % 3
                ev_exp = p.op("scalar", lambda s, ai=ai, ps=ps: s.activation(
                    a_sb[ai][:, :], ps[:, :], AF.Exp, scale=scale),
                    waits=[ev_mm, a_free[ai]], sig=True)
                sect.pool.release(i_s, ev_exp)
                if k0 >= q0:
                    ev_exp = p.op("vector", lambda v, ai=ai, off=k0 - q0: v.tensor_mul(
                        a_sb[ai][:, :], a_sb[ai][:, :],
                        tri_sb[TRI_OFFS.index(off)][:, :]),
                        waits=[ev_exp] + tri_evs, sig=True)
                ev_av = p.op("tensor", lambda t, buf=buf, kt=kt, ai=ai, ob=ob,
                             kt_hi=kt_hi: t.matmul(
                    o_acc[ob][0:68, :], vt_sb[buf][:, kt, :], a_sb[ai][:, :],
                    start=(kt == 0), stop=(kt == kt_hi - 1)),
                    waits=[ev_exp, ev_v] + ([o_free[ob]] if kt == 0 else []),
                    sig=True)
                a_free[ai] = ev_av
            for kh in range(khn):
                bias_free[kh] = ev_av
            ev_r = p.op("vector", lambda v, ob=ob: v.reciprocal(
                nrm_row[ob][:, :], o_acc[ob][64:65, :]), waits=[ev_av], sig=True)
            i_b, ps_b, rel_b = sect.pool.get()
            ev_bc = p.op("tensor", lambda t, ob=ob, ps_b=ps_b: t.matmul(
                ps_b[0:64, :], sect.ones_row[:, 0:64], nrm_row[ob][:, :],
                start=True, stop=True), waits=[ev_r, rel_b, sect.ones_ev], sig=True)
            ev_ev = p.op("scalar", lambda s, ob=ob, ps_b=ps_b: s.copy(
                bc_sb[ob][0:64, :], ps_b[0:64, :]), waits=[ev_bc, bc_free[ob]], sig=True)
            sect.pool.release(i_b, ev_ev)
            ev_on = p.op("vector", lambda v, ob=ob: v.tensor_mul(
                o_sb[ob][0:64, :], o_acc[ob][0:64, :], bc_sb[ob][0:64, :]),
                waits=[ev_ev, ost_free[ob]], sig=True)
            bc_free[ob] = ev_on
            o_free[ob] = ev_on
            sev = sect.st.put([lambda y, pp=pp, q0=q0, ob=ob: y.dma_start(
                out=of_d[pp][:, q0:q0 + 512], in_=o_sb[ob][0:64, :])], waits=[ev_on])
            sect.final_evs.append(sev)
            ost_free[ob] = sev
            last_on = sev
        _devs = brd.drain_evs()
        tread_done[pp % 2] = _devs[0] if _devs else None
        tread_done2[pp % 2] = _devs[1] if len(_devs) > 1 else None
        pair_free[buf] = last_on
    finish(p, sect)
    return p.build()



def emit_out_proj_residual(sect, o_tiles, o_ev, w_tiles, w_evs, x_tiles, x_evs,
                           bias_col=None, bias_ev=None):
    p = sect.p
    evs = [None] * 8

    def cb(oc, qc, q0, qn, i_p, ps, ev_mm):
        if bias_col is not None:
            def f(v, oc=oc, q0=q0, qn=qn, ps=ps):
                return v.scalar_tensor_tensor(
                    x_tiles[oc][:, q0:q0 + qn], ps[:, :qn], bias_col[:, oc:oc + 1],
                    x_tiles[oc][:, q0:q0 + qn], ALU.add, ALU.add)
            w = [ev_mm, x_evs[oc], bias_ev]
        else:
            def f(v, oc=oc, q0=q0, qn=qn, ps=ps):
                return v.tensor_tensor(
                    x_tiles[oc][:, q0:q0 + qn], ps[:, :qn],
                    x_tiles[oc][:, q0:q0 + qn], ALU.add)
            w = [ev_mm, x_evs[oc]]
        ev = p.op("vector", f, waits=w, sig=True)
        sect.pool.release(i_p, ev)
        evs[oc] = ev

    emit_proj_fm(sect, o_tiles, o_ev, w_tiles, w_evs, cb)
    return evs


def emit_store_fm(sect, tiles, evs, dram, nrows=8):
    out_evs = []
    for dc in range(nrows):
        sev = sect.st.put([lambda y, dc=dc: y.dma_start(
            out=dram[dc * 128:(dc + 1) * 128, :], in_=tiles[dc][:, :])],
            waits=[evs[dc] if isinstance(evs, list) else evs])
        sect.final_evs.append(sev)
        out_evs.append(sev)
    return out_evs


def emit_sign_alpha_prep(sect, w_dram, rows, cols, name):
    p = sect.p
    u = p.uid()
    nrc = rows // 128
    ncb = cols // 1024
    swt = p.internal(f"swt_{name}", [cols, rows], BF16).ap()
    alpha = p.sbuf(f"alpha{u}", [128, nrc], F32)
    acol = p.sbuf(f"acol{u}", [128, 1], F32)
    pb = sect.prep_bufs()
    wtile, stile, ttile = pb["w"], pb["s"], pb["t"]
    ps_t = sect.tpsum()
    idn, id_ev = sect.ident()
    free_w = pb["fw"]
    free_s = pb["fs"]
    ps_free = sect._tpsum_free
    tt_free = pb["ft"]
    last_store = None
    a_evs = []
    bi_ctr = 0
    for rc in range(nrc):
        for cb in range(ncb):
            bi = bi_ctr % 2
            bi_ctr += 1
            ev_l = sect.ld.put([lambda g, rc=rc, cb=cb, bi=bi: g.dma_start(
                out=wtile[bi][:, :],
                in_=w_dram[rc * 128:(rc + 1) * 128, cb * 1024:(cb + 1) * 1024])],
                waits=[free_w[bi]])
            if cb == 0:
                ev_a = p.op("vector", lambda v, rc=rc, bi=bi: v.tensor_reduce(
                    alpha[:, rc:rc + 1], wtile[bi][:, :], mybir.AxisListType.X,
                    ALU.add, apply_absolute_value=True), waits=[ev_l], sig=True)
            else:
                ev_a0 = p.op("vector", lambda v, bi=bi: v.tensor_reduce(
                    acol[:, :], wtile[bi][:, :], mybir.AxisListType.X,
                    ALU.add, apply_absolute_value=True), waits=[ev_l], sig=True)
                ev_a = p.op("vector", lambda v, rc=rc: v.tensor_add(
                    alpha[:, rc:rc + 1], alpha[:, rc:rc + 1], acol[:, :]),
                    waits=[ev_a0], sig=True)
            a_evs.append(ev_a)
            ev_s = p.op("scalar", lambda s, bi=bi: s.activation(
                stile[bi][:, :], wtile[bi][:, :], AF.Sign),
                waits=[ev_l, free_s[bi]], sig=True)
            free_w[bi] = Ev(ev_s.sem, ev_s.val)
            for dc in range(8):
                ti = dc % 2
                ev_tr = p.op("tensor", lambda t, dc=dc, bi=bi, ti=ti: t.transpose(
                    ps_t[ti][:, 0:128],
                    stile[bi][:, dc * 128:(dc + 1) * 128], idn[:, :]),
                    waits=[ev_s, id_ev, ps_free[ti]], sig=True)
                ev_e = p.op("scalar", lambda s, ti=ti: s.copy(
                    ttile[ti][:, :], ps_t[ti][:, 0:128]),
                    waits=[ev_tr, tt_free[ti]], sig=True)
                ps_free[ti] = ev_e
                sev = sect.st.put([lambda y, rc=rc, cb=cb, dc=dc, ti=ti: y.dma_start(
                    out=swt[cb * 1024 + dc * 128:cb * 1024 + (dc + 1) * 128,
                            rc * 128:(rc + 1) * 128],
                    in_=ttile[ti][:, :])], waits=[ev_e])
                tt_free[ti] = sev
                last_store = sev
            free_s[bi] = last_store
    ev_am = p.op("vector", lambda v: v.tensor_scalar_mul(
        alpha[:, :], alpha[:, :], 1.0 / cols), waits=a_evs, sig=True)
    return swt, last_store, alpha, ev_am


def emit_ffn_l0(sect, z_tiles, z_ev, x_tiles, x_evs,
                sw1t, sw1_ev, a1, a1_ev, b1col, b1_ev,
                sw2t, sw2_ev, a2, a2_ev, b2col, b2_ev):
    p, T = sect.p, sect.T
    u = p.uid()
    nsl = max(1, T // 512)
    h2 = []
    for i in range(32):
        ar, off = i // nsl, i % nsl
        if ar < len(sect.h2_arena):
            h2.append(sect.h2_arena[ar][:, off * 512:(off + 1) * 512])
        else:
            h2.append(p.sbuf(f"h2x{u}_{i}", [128, 512], BF16))
    w1buf = [p.sbuf(f"w1b{u}_{i}", [128, 8, 128], BF16) for i in range(3)]
    w2buf = [p.sbuf(f"w2b{u}_{i}", [128, 32, 128], BF16) for i in range(1)]
    f_w1 = [None] * 3
    f_w2 = [None] * 1
    f_h2 = [None] * 32
    x_out = [None] * 8
    for tc in range(cdiv(T, 512)):
        t0 = tc * 512
        h_evs = [None] * 32
        for hc in range(32):
            wi = hc % 3
            ev_w = sect.ld.put([lambda g, hc=hc, wi=wi: g.dma_start(
                out=w1buf[wi][:, :, :],
                in_=sw1t[:, hc * 128:(hc + 1) * 128].rearrange(
                    "(dc p) h -> p dc h", p=128))], waits=[f_w1[wi], sw1_ev])
            i_p, ps, rel = sect.pool.get()
            ev = None
            for dc in range(8):
                ev = p.op("tensor", lambda t, dc=dc, wi=wi, t0=t0, ps=ps: t.matmul(
                    ps[:, :], w1buf[wi][:, dc, :], z_tiles[dc][:, t0:t0 + 512],
                    start=(dc == 0), stop=(dc == 7)),
                    waits=[ev_w, z_ev] + ([rel] if dc == 0 else []),
                    sig=(dc == 7))
            f_w1[wi] = ev
            ev_g = p.op("scalar", lambda s, hc=hc, ps=ps: s.activation(
                h2[hc][:, :], ps[:, :], AF.Gelu,
                bias=b1col[:, hc:hc + 1], scale=a1[:, hc:hc + 1]),
                waits=[ev, a1_ev, b1_ev, f_h2[hc]], sig=True)
            sect.pool.release(i_p, ev_g)
            h_evs[hc] = ev_g
        for oc in range(8):
            wi = 0
            ev_w = sect.ld.put([lambda g, oc=oc, wi=wi: g.dma_start(
                out=w2buf[wi][:, :, :],
                in_=sw2t[:, oc * 128:(oc + 1) * 128].rearrange(
                    "(hc p) d -> p hc d", p=128))], waits=[f_w2[wi], sw2_ev])
            i_p, ps, rel = sect.pool.get()
            ev = None
            for hc in range(32):
                ev = p.op("tensor", lambda t, hc=hc, wi=wi, oc=oc, ps=ps: t.matmul(
                    ps[:, :], w2buf[wi][:, hc, :], h2[hc][:, :],
                    start=(hc == 0), stop=(hc == 31)),
                    waits=[ev_w, h_evs[hc]] + ([rel] if hc == 0 else []),
                    sig=(hc == 31))
            f_w2[wi] = ev
            ev_1 = p.op("vector", lambda v, oc=oc, ps=ps: v.tensor_scalar(
                sect.scr[1][:, :], ps[:, :], a2[:, oc:oc + 1], b2col[:, oc:oc + 1],
                ALU.mult, ALU.add), waits=[ev, a2_ev, b2_ev], sig=True)
            ev_2 = p.op("vector", lambda v, oc=oc, t0=t0: v.tensor_tensor(
                x_tiles[oc][:, t0:t0 + 512], sect.scr[1][:, :],
                x_tiles[oc][:, t0:t0 + 512], ALU.add),
                waits=[ev_1, x_evs[oc]], sig=True)
            sect.pool.release(i_p, ev_2)
            x_out[oc] = ev_2
        for hc in range(32):
            f_h2[hc] = x_out[7]
    return x_out


def emit_qk_plain(sect, z_tiles, z_ev, w_tiles, w_evs, out_dram,
                  bias_col=None, bias_ev=None, keep_bf16=None):
    p = sect.p
    outb = sect.scr[2]
    if not hasattr(sect, "_qk_prev"):
        sect._qk_prev = {"ev": None}
    prev = sect._qk_prev

    def cb(oc, qc, q0, qn, i_p, ps, ev_mm):
        if bias_col is not None:
            ev = p.op("scalar", lambda s, oc=oc, qn=qn, ps=ps: s.activation(
                outb[:, :qn], ps[:, :qn], AF.Identity, bias=bias_col[:, oc:oc + 1]),
                waits=[ev_mm, bias_ev, prev["ev"]], sig=True)
        else:
            ev = p.op("scalar", lambda s, qn=qn, ps=ps: s.copy(
                outb[:, :qn], ps[:, :qn]), waits=[ev_mm, prev["ev"]], sig=True)
        sect.pool.release(i_p, ev)
        sev = sect.st.put([lambda y, oc=oc, q0=q0, qn=qn: y.dma_start(
            out=out_dram[oc * 128:(oc + 1) * 128, q0:q0 + qn], in_=outb[:, :qn])],
            waits=[ev])
        sect.final_evs.append(sev)
        prev["ev"] = sev

    emit_proj_fm(sect, z_tiles, z_ev, w_tiles, w_evs, cb)


def emit_gating(sect, x_tiles, x1_evs, mean_row, rstd_row, stats_ev,
                gate_sb, gate_evl, flavor, gb_row=None, gb_ev=None):
    p, T = sect.p, sect.T
    NTT = T // 128
    l2 = flavor == "L2"
    comb_tm = p.sbuf(f"compb{p.uid()}", [128, NTT, 9], F32)
    lg = p.sbuf(f"lg{p.uid()}", [9, T], F32)
    comb_fm = lg[:, :].bitcast(BF16)[:, 0:T] if l2 else None
    comb_bf = p.sbuf(f"combbf{p.uid()}", [128, 16], BF16) if l2 else None
    cs_col = p.sbuf(f"cscol{p.uid()}", [9, 1], F32)
    idn32, id32_ev = sect.ident32()
    idn, id_ev = sect.ident()
    ps_t = sect.tpsum()
    ps_free = sect._tpsum_free

    i_c, ps_c, rel_c = sect.pool.get()
    ev = None
    for dc in range(8):
        ev = p.op("tensor", lambda t, dc=dc, ps=ps_c: t.matmul(
            ps[0:9, 0:1], gate_sb[dc][:, :], sect.ones_col[:, :],
            start=(dc == 0), stop=(dc == 7)),
            waits=list(gate_evl) + [sect.ones_col_ev] + ([rel_c] if dc == 0 else []),
            sig=(dc == 7))
    ev_cs = p.op("vector", lambda v, ps=ps_c: v.tensor_copy(
        cs_col[:, :], ps[0:9, 0:1]), waits=[ev], sig=True)
    sect.pool.release(i_c, ev_cs)

    lg_evs = []
    for qc in range(sect.NQC):
        q0 = qc * 512
        qn = min(512, T - q0)
        i_a, ps_a, rel_a = sect.pool.get()
        ev = None
        for dc in range(8):
            ev = p.op("tensor", lambda t, dc=dc, q0=q0, qn=qn, ps=ps_a: t.matmul(
                ps[0:9, :qn], gate_sb[dc][:, :], x_tiles[dc][:, q0:q0 + qn],
                start=(dc == 0), stop=(dc == 7)),
                waits=[x1_evs[dc]] + list(gate_evl) + ([rel_a] if dc == 0 else []),
                sig=(dc == 7))
        i_m, ps_m, rel_m = sect.pool.get()
        ev_bm = p.op("tensor", lambda t, q0=q0, qn=qn, ps=ps_m: t.matmul(
            ps[0:9, :qn], sect.ones_row[:, 0:9], mean_row[:, q0:q0 + qn],
            start=True, stop=True), waits=[stats_ev, sect.ones_ev, rel_m], sig=True)
        i_r, ps_r, rel_r = sect.pool.get()
        ev_br = p.op("tensor", lambda t, q0=q0, qn=qn, ps=ps_r: t.matmul(
            ps[0:9, :qn], sect.ones_row[:, 0:9], rstd_row[:, q0:q0 + qn],
            start=True, stop=True), waits=[stats_ev, rel_r], sig=True)

        ev_1 = p.op("vector", lambda v, qn=qn, ps_m=ps_m: v.tensor_scalar(
            sect.scr[2][0:9, :qn], ps_m[0:9, :qn], cs_col[:, :], None, ALU.mult),
            waits=[ev_bm, ev_cs], sig=True)
        ev_2 = p.op("vector", lambda v, qn=qn, ps_a=ps_a: v.tensor_tensor(
            sect.scr[3][0:9, :qn], ps_a[0:9, :qn], sect.scr[2][0:9, :qn],
            ALU.subtract), waits=[ev_1, ev], sig=True)
        ev_3 = p.op("vector", lambda v, q0=q0, qn=qn, ps_r=ps_r: v.tensor_tensor(
            lg[:, q0:q0 + qn], sect.scr[3][0:9, :qn], ps_r[0:9, :qn], ALU.mult),
            waits=[ev_2, ev_br], sig=True)
        sect.pool.release(i_a, ev_2)
        sect.pool.release(i_m, ev_1)
        sect.pool.release(i_r, ev_3)
        lg_evs.append(ev_3)
    lg_ev = lg_evs[-1]

    gbase = None
    ev_gbe = None
    if l2:
        i_b, ps_b, rel_b = sect.pool.get()
        ev_gb = p.op("tensor", lambda t, ps_b=ps_b: t.matmul(
            ps_b[:, 0:8], sect.ones_row[:, :], gb_row[:, :],
            start=True, stop=True), waits=[gb_ev, sect.ones_ev, rel_b], sig=True)
        gbase = p.sbuf(f"gbb{p.uid()}", [128, 8], F32)
        ev_gbe = p.op("scalar", lambda s, ps_b=ps_b: s.copy(
            gbase[:, :], ps_b[:, 0:8]), waits=[ev_gb], sig=True)
        sect.pool.release(i_b, ev_gbe)

    comb_evs = []
    fm_evs = []
    prev_gt = None
    for tt in range(NTT):
        t0 = tt * 128
        i_t, ps_tt, rel_t = sect.pool.get()
        ev_tr = p.op("tensor", lambda t, t0=t0, ps=ps_tt: t.transpose(
            ps[:, 0:9], lg[:, t0:t0 + 128], idn32[0:9, 0:9]),
            waits=[lg_ev, id32_ev, rel_t], sig=True)
        ev_g = p.op("scalar", lambda s, ps=ps_tt: s.copy(
            sect.scr[2][:, 0:9], ps[:, 0:9]),
            waits=[ev_tr, prev_gt], sig=True)
        sect.pool.release(i_t, ev_g)
        ev_x = p.op("scalar", lambda s: s.activation(
            sect.scr[2][:, 16:24], sect.scr[2][:, 0:8], AF.Exp), waits=[ev_g], sig=True)
        if flavor == "L1":
            ev_sg = p.op("scalar", lambda s, tt=tt: s.activation(
                comb_tm[:, tt, 8:9], sect.scr[2][:, 8:9], AF.Sigmoid),
                waits=[ev_g], sig=True)
        else:
            ev_sg = p.op("vector", lambda v, tt=tt: v.memset(
                comb_tm[:, tt, 8:9], 1.0), waits=[ev_g], sig=True)

        s2 = sect.scr[2]
        ev_p = p.seq("vector", [
            lambda v: v.tensor_reduce(s2[:, 24:25], s2[:, 16:24], mybir.AxisListType.X, ALU.add),
            lambda v: v.reciprocal(s2[:, 25:26], s2[:, 24:25]),
            lambda v: v.tensor_scalar_mul(s2[:, 32:40], s2[:, 16:24], s2[:, 25:26]),
            lambda v: v.tensor_copy(s2[:, 48:56], s2[:, 32:40]),
        ], waits=[ev_x])
        if l2:
            ev_p = p.op("vector", lambda v: v.tensor_add(
                sect.scr[2][:, 48:56], sect.scr[2][:, 32:40], gbase[:, :]),
                waits=[ev_p, ev_gbe], sig=True)

        ev_t = p.seq("vector", [
            lambda v: v.tensor_reduce(s2[:, 26:27], s2[:, 48:56], mybir.AxisListType.X, ALU.max),
            lambda v: v.tensor_scalar(s2[:, 56:64], s2[:, 48:56], s2[:, 26:27], None, ALU.is_ge),
            lambda v: v.scalar_tensor_tensor(s2[:, 64:72], s2[:, 56:64], -1e30,
                                             s2[:, 48:56], ALU.mult, ALU.add),
            lambda v: v.tensor_reduce(s2[:, 27:28], s2[:, 64:72], mybir.AxisListType.X, ALU.max),
            lambda v: v.tensor_scalar(s2[:, 72:80], s2[:, 48:56], s2[:, 27:28], None, ALU.is_ge),
            lambda v: v.tensor_mul(s2[:, 80:88], s2[:, 32:40], s2[:, 72:80]),
        ], waits=[ev_p])
        if flavor == "L1":
            ev_c = p.seq("vector", [
                lambda v: v.tensor_reduce(s2[:, 28:29], s2[:, 80:88], mybir.AxisListType.X, ALU.add),
                lambda v: v.reciprocal(s2[:, 29:30], s2[:, 28:29]),
                lambda v, tt=tt: v.tensor_scalar_mul(comb_tm[:, tt, 0:8], s2[:, 80:88],
                                                     s2[:, 29:30]),
            ], waits=[ev_t])
        else:
            ev_c = p.op("vector", lambda v, tt=tt: v.tensor_copy(
                comb_tm[:, tt, 0:8], sect.scr[2][:, 80:88]), waits=[ev_t], sig=True)
        prev_gt = ev_c
        comb_evs.append(ev_c)
        comb_evs.append(ev_sg)
        if l2:
            ev_cb = p.op("vector", lambda v, tt=tt: v.tensor_copy(
                comb_bf[:, 0:9], comb_tm[:, tt, :]),
                waits=[ev_c, ev_sg], sig=True)
            ti2 = tt % 2
            ev_tr2 = p.op("tensor", lambda t, tt=tt, t0=t0, ti2=ti2: t.transpose(
                ps_t[ti2][0:9, 128:256], comb_bf[:, 0:9],
                idn[:, :]), waits=[ev_cb, ps_free[ti2], id_ev], sig=True)
            ev_fme = p.op("scalar", lambda s, t0=t0, ti2=ti2: s.copy(
                comb_fm[:, t0:t0 + 128], ps_t[ti2][0:9, 128:256]),
                waits=[ev_tr2], sig=True)
            ps_free[ti2] = ev_fme
            fm_evs.append(ev_fme)
    return comb_tm, comb_evs, comb_fm, (fm_evs[-1] if fm_evs else None)


def emit_moe_experts(sect, z_tiles, z_ev, comb_tm, comb_evs, acc_dram,
                     wg_d, wu_d, wd_d, x_tiles, park_evs, o_tiles, o_dead_ev,
                     flavor, b1c_d=None, b3c_d=None, b2r_d=None,
                     comb_fm=None, comb_fm_ev=None, x_read_done=None):
    p, T = sect.p, sect.T
    u = p.uid()
    NTC = T // 512
    l2 = flavor == "L2"
    if 2 * T >= 2816:
        wg_sb = [x_tiles[dc][:, :].bitcast(BF16) for dc in range(8)]
    else:
        wg_sb = [p.sbuf(f"wgl{u}_{dc}", [128, 2816], BF16) for dc in range(8)]
    wi0, wset0, wrel0 = sect.wpool_get()
    wi1, wset1, wrel1 = sect.wpool_get()
    wd_sb = wset0 + wset1[:3]
    nsl = max(1, T // 512)
    h2 = []
    for ic in range(11):
        ar, off = ic // nsl, ic % nsl
        if ar < len(o_tiles):
            h2.append(o_tiles[ar][:, off * 512:(off + 1) * 512])
        else:
            h2.append(p.sbuf(f"h2m{u}_{ic}", [128, 512], BF16))
    sgb = p.sbuf(f"sgb{u}", [128, 512], BF16) if not l2 else None
    eo_sb = [sect.scr[0], p.sbuf(f"eo{u}_1", [128, 512], F32)]
    eo_free = [None, None]
    b1c = p.sbuf(f"b1c{u}", [128, 11], F32) if l2 else None
    b3c = p.sbuf(f"b3c{u}", [128, 11], F32) if l2 else None
    b2r = p.sbuf(f"b2r{u}", [1, 1024], BF16) if l2 else None
    accs = Stream(p, "gpsimd", f"acc{u}", 2)
    wld = Stream(p, "gpsimd", f"wld{u}", 3)

    arena_free_x = list(park_evs) + list(x_read_done or [])
    arena_free_wd = [wrel0, wrel1, o_dead_ev]
    h2_free = [o_dead_ev] * 11
    sg_free = None
    last_acc = None
    prev_e_mm = None
    prev_e_dn = None
    bias_ev = None
    for e in range(9):
        wg_evs = []
        wu_evs = []
        for dc in range(8):
            wg_evs.append(wld.put([lambda g, e=e, dc=dc: g.dma_start(
                out=wg_sb[dc][:, 0:1408], in_=wg_d[e][dc * 128:(dc + 1) * 128, :])],
                waits=arena_free_x + [prev_e_mm]))
            wu_evs.append(wld.put([lambda g, e=e, dc=dc: g.dma_start(
                out=wg_sb[dc][:, 1408:2816], in_=wu_d[e][dc * 128:(dc + 1) * 128, :])]))
        wd_evs = []
        for ic in range(11):
            wd_evs.append(wld.put([lambda g, e=e, ic=ic: g.dma_start(
                out=wd_sb[ic][:, :], in_=wd_d[e][ic * 128:(ic + 1) * 128, :])],
                waits=(arena_free_wd + [prev_e_dn]) if ic == 0 else ()))
        arena_free_x = []
        arena_free_wd = []
        if l2:
            bias_ev = sect.ld.put([
                lambda g, e=e: g.dma_start(out=b1c[:, :], in_=b1c_d[e]),
                lambda g, e=e: g.dma_start(out=b3c[:, :], in_=b3c_d[e]),
                lambda g, e=e: g.dma_start(out=b2r[:, :], in_=b2r_d[e].unsqueeze(0)),
            ], waits=[prev_e_mm, prev_e_dn])
        e_mms = []
        e_dns = []
        for tc in range(NTC):
            t0c = tc * 512
            h_evs = [None] * 11
            for ic in range(11):
                iA, psA, relA = sect.pool.get()
                iB, psB, relB = sect.pool.get()
                evg = evu = None
                for dc in range(8):
                    evg = p.op("tensor", lambda t, dc=dc, ic=ic, t0c=t0c, ps=psA: t.matmul(
                        ps[:, :], wg_sb[dc][:, ic * 128:(ic + 1) * 128],
                        z_tiles[dc][:, t0c:t0c + 512],
                        start=(dc == 0), stop=(dc == 7)),
                        waits=[wg_evs[dc], z_ev] + ([relA] if dc == 0 else []),
                        sig=(dc == 7))
                    evu = p.op("tensor", lambda t, dc=dc, ic=ic, t0c=t0c, ps=psB: t.matmul(
                        ps[:, :], wg_sb[dc][:, 1408 + ic * 128:1408 + (ic + 1) * 128],
                        z_tiles[dc][:, t0c:t0c + 512],
                        start=(dc == 0), stop=(dc == 7)),
                        waits=[wu_evs[dc]] + ([relB] if dc == 0 else []),
                        sig=(dc == 7))
                e_mms.extend([evg, evu])
                if not l2:
                    ev_s = p.op("scalar", lambda s, ps=psA: s.activation(
                        sgb[:, :], ps[:, :], AF.Silu), waits=[evg, sg_free], sig=True)
                    ev_h = p.op("vector", lambda v, ic=ic, ps=psB: v.scalar_tensor_tensor(
                        h2[ic][:, :], ps[:, :], 1.0, sgb[:, :], ALU.bypass, ALU.mult),
                        waits=[ev_s, evu, h2_free[ic]], sig=True)
                    sg_free = ev_h
                else:
                    ev_a = p.op("scalar", lambda s, ic=ic, ps=psA: s.activation(
                        sect.scr[1][:, :], ps[:, :], AF.Identity,
                        bias=b1c[:, ic:ic + 1]), waits=[evg, bias_ev, sg_free], sig=True)
                    ev_m = p.op("vector", lambda v, ic=ic, ps=psB: v.scalar_tensor_tensor(
                        sect.scr[2][:, :], ps[:, :], b3c[:, ic:ic + 1],
                        sect.scr[1][:, :], ALU.add, ALU.mult),
                        waits=[ev_a, evu, bias_ev], sig=True)
                    ev_h = p.op("scalar", lambda s, ic=ic: s.activation(
                        h2[ic][:, :], sect.scr[2][:, :], AF.Silu),
                        waits=[ev_m, h2_free[ic]], sig=True)
                    sg_free = ev_h
                sect.pool.release(iA, ev_h)
                sect.pool.release(iB, ev_h)
                h_evs[ic] = ev_h
            for half in range(2):
                for tsub in range(4):
                    tt = tc * 4 + tsub
                    t0 = tt * 128
                    i_o, ps_o, rel_o = sect.pool.get()
                    ev = None
                    for ic in range(11):
                        ev = p.op("tensor", lambda t, ic=ic, tsub=tsub, half=half, ps=ps_o: t.matmul(
                            ps[:, :], h2[ic][:, tsub * 128:(tsub + 1) * 128],
                            wd_sb[ic][:, half * 512:(half + 1) * 512],
                            start=(ic == 0), stop=(ic == 10 and not l2)),
                            waits=[h_evs[ic], wd_evs[ic]] + ([rel_o] if ic == 0 else []),
                            sig=(ic == 10 and not l2))
                        e_dns.append(ev)
                    if l2:
                        ev = p.op("tensor", lambda t, half=half, ps=ps_o: t.matmul(
                            ps[:, :], sect.ones_row_bf[:, :],
                            b2r[:, half * 512:(half + 1) * 512],
                            start=False, stop=True),
                            waits=[bias_ev, sect.ones_bf_ev], sig=True)
                    pp_ = (tc * 8 + half * 4 + tsub) % 2
                    ev_sc = p.op("vector", lambda v, tt=tt, e=e, ps=ps_o, pp_=pp_: v.tensor_scalar(
                        eo_sb[pp_][:, :], ps[:, :], comb_tm[:, tt, e:e + 1], None,
                        ALU.mult), waits=[ev, eo_free[pp_]] + comb_evs[-2:], sig=True)
                    sect.pool.release(i_o, ev_sc)
                    sev = accs.put([lambda g, tt=tt, t0=t0, half=half, e=e, pp_=pp_: g.dma_start(
                        out=acc_dram[t0:t0 + 128, half * 512:(half + 1) * 512],
                        in_=eo_sb[pp_][:, :],
                        accum_op=(ALU.bypass if e == 0 else ALU.add))],
                        waits=[ev_sc])
                    eo_free[pp_] = sev
                    last_acc = sev
            for ic in range(11):
                h2_free[ic] = e_dns[-1]
        prev_e_mm = e_mms[-1]
        prev_e_dn = e_dns[-1]
    sect.wpool_release(wi0, prev_e_dn)
    sect.wpool_release(wi1, prev_e_dn)
    return accs.drain_evs(), prev_e_mm, prev_e_dn


def emit_moe_finale(sect, acc_dram, acc_done_evs, x_tiles, x1_dram, weights_done,
                    o_tiles, h2_done, x2_o, park_evs=()):
    p, T = sect.p, sect.T
    NTT = T // 128
    idn_f32, id32_ev = sect.ident32()
    x_evs = []
    for dc in range(8):
        x_evs.append(sect.act_ld.put([lambda y, dc=dc: y.dma_start(
            out=x_tiles[dc][:, :], in_=x1_dram[dc * 128:(dc + 1) * 128, :])],
            waits=[weights_done] + list(park_evs)))
    if T >= 2048:
        acc_t = [o_tiles[i][:, :].bitcast(F32)[:, 0:1024] for i in range(2)]
    else:
        acc_t = [p.sbuf(f"accl{p.uid()}_{i}", [128, 1024], F32) for i in range(2)]
    acc_free = [h2_done, h2_done]
    x2_evs = [None] * 8
    for tt in range(NTT):
        t0 = tt * 128
        ai = tt % 2
        ev_l = sect.act_ld.put([lambda y, tt=tt, ai=ai: y.dma_start(
            out=acc_t[ai][:, :], in_=acc_dram[tt * 128:(tt + 1) * 128, :])],
            waits=list(acc_done_evs) + [acc_free[ai]])
        for dc in range(8):
            i_p, ps, rel = sect.pool.get()
            ev_tr = p.op("tensor", lambda t, ai=ai, dc=dc, ps=ps: t.transpose(
                ps[:, 0:128], acc_t[ai][:, dc * 128:(dc + 1) * 128], idn_f32[:, :]),
                waits=[ev_l, id32_ev, rel], sig=True)
            ev_add = p.op("vector", lambda v, dc=dc, t0=t0, ps=ps: v.tensor_tensor(
                x_tiles[dc][:, t0:t0 + 128], ps[:, 0:128],
                x_tiles[dc][:, t0:t0 + 128], ALU.add),
                waits=[ev_tr, x_evs[dc]], sig=True)
            sect.pool.release(i_p, ev_add)
            x2_evs[dc] = ev_add
            acc_free[ai] = ev_add
    emit_store_fm(sect, x_tiles, x2_evs, x2_o)
    return x2_evs


def build_B0(T, stage=5):
    p = Prog("B0")
    sect = Sect(p, T, n_pool=6)
    x_d = p.in_("x", [D, T], F32).ap()
    o_d = p.in_("o", [D, T], F32).ap()
    wot_d = p.in_("wot", [D, D], F32).ap()
    w1_d = p.in_("w1", [4096, D], F32).ap()
    b1c_d = p.in_("b1c", [128, 32], F32).ap()
    w2_d = p.in_("w2", [D, 4096], F32).ap()
    b2c_d = p.in_("b2c", [128, 8], F32).ap()
    wq1_d = p.in_("wq1", [D, D], F32).ap()
    wk1_d = p.in_("wk1", [D, D], F32).ap()
    wv1_d = p.in_("wv1", [D, D], F32).ap()
    bq1c_d = p.in_("bq1c", [128, 8], F32).ap()
    bk1c_d = p.in_("bk1c", [128, 8], F32).ap()
    bv1r_d = p.in_("bv1r", [1, D], F32).ap()
    x2_o = p.out("x2", [D, T], F32).ap()
    q1_o = p.out("q1", [D, T], F32).ap()
    k1_o = p.out("k1", [D, T], F32).ap()
    v1_o = p.out("v1", [T, D], F32).ap()

    x_tiles = [p.sbuf(f"xs{i}", [128, T], F32) for i in range(8)]
    o_tiles = [p.sbuf(f"os{i}", [128, T], BF16) for i in range(8)]
    z_tiles = [p.sbuf(f"zs{i}", [128, T], BF16) for i in range(8)]

    x_evs = emit_load_fm(sect, x_d, x_tiles)
    o_evs = []
    for dc in range(8):
        o_evs.append(sect.ld.put([lambda g, dc=dc: g.dma_start(
            out=o_tiles[dc][:, :], in_=o_d[dc * 128:(dc + 1) * 128, :])]))
    o_ev = o_evs[-1]

    small = Stream(p, "gpsimd", "sm", 2)
    b1c = p.sbuf("b1c_s", [128, 32], F32)
    b2c = p.sbuf("b2c_s", [128, 8], F32)
    bq1c = p.sbuf("bq1c_s", [128, 8], F32)
    bk1c = p.sbuf("bk1c_s", [128, 8], F32)
    bv1r = p.sbuf("bv1r_s", [1, D], F32)
    ev_b1 = small.put([lambda g: g.dma_start(out=b1c[:, :], in_=b1c_d)])
    ev_b2 = small.put([lambda g: g.dma_start(out=b2c[:, :], in_=b2c_d)])
    ev_bq = small.put([lambda g: g.dma_start(out=bq1c[:, :], in_=bq1c_d)])
    ev_bk = small.put([lambda g: g.dma_start(out=bk1c[:, :], in_=bk1c_d)])
    ev_bv = small.put([lambda g: g.dma_start(out=bv1r[:, :], in_=bv1r_d)])

    wo_t, wo_e, wo_i = emit_weight_load(sect, wot_d, D, D, "wo")
    x1_evs = emit_out_proj_residual(sect, o_tiles, o_ev, wo_t, wo_e, x_tiles, x_evs)
    sect.wpool_release(wo_i, Ev(p.clock_sem["tensor"], p.clock["tensor"]))
    if stage <= 1:
        emit_store_fm(sect, x_tiles, x1_evs, x2_o)
        finish(p, sect)
        return p.build()
    z_ev = emit_ln(sect, x_tiles, x1_evs, z_tiles)
    if stage <= 2:
        emit_store_fm(sect, x_tiles, x1_evs, x2_o)
        finish(p, sect)
        return p.build()
    sect.h2_arena = o_tiles
    sw1t, sw1_ev, a1, a1_ev = emit_sign_alpha_prep(sect, w1_d, 4096, 1024, "w1")
    sw2t, sw2_ev, a2, a2_ev = emit_sign_alpha_prep(sect, w2_d, 1024, 4096, "w2")
    if stage <= 3:
        emit_store_fm(sect, x_tiles, x1_evs, x2_o)
        finish(p, sect)
        return p.build()
    x2_evs = emit_ffn_l0(sect, z_tiles, z_ev, x_tiles, x1_evs,
                         sw1t, sw1_ev, a1, a1_ev, b1c, ev_b1,
                         sw2t, sw2_ev, a2, a2_ev, b2c, ev_b2)
    emit_store_fm(sect, x_tiles, x2_evs, x2_o)
    if stage <= 4:
        finish(p, sect)
        return p.build()
    z2_ev = emit_ln(sect, x_tiles, x2_evs, z_tiles)
    wq_t, wq_e, wq_i = emit_weight_load(sect, wq1_d, D, D, "wq1")
    emit_qk_plain(sect, z_tiles, z2_ev, wq_t, wq_e, q1_o, bq1c, ev_bq)
    sect.wpool_release(wq_i, Ev(p.clock_sem["tensor"], p.clock["tensor"]))
    wk_t, wk_e, wk_i = emit_weight_load(sect, wk1_d, D, D, "wk1")
    emit_qk_plain(sect, z_tiles, z2_ev, wk_t, wk_e, k1_o, bk1c, ev_bk)
    sect.wpool_release(wk_i, Ev(p.clock_sem["tensor"], p.clock["tensor"]))
    wv_t, wv_e, wv_i = emit_weight_load(sect, wv1_d, D, D, "wv1")
    emit_v_tm_to_hbm(sect, z_tiles, z2_ev, wv_t, wv_e, v1_o, bv1r, ev_bv)
    sect.wpool_release(wv_i, Ev(p.clock_sem["tensor"], p.clock["tensor"]))
    finish(p, sect)
    return p.build()


def build_B1(T, flavor="L1"):
    assert flavor == "L1"
    p = Prog("B1")
    sect = Sect(p, T, n_pool=6)
    x_d = p.in_("x", [D, T], F32).ap()
    o_d = p.in_("o", [D, T], F32).ap()
    wc_d = p.in_("wc", [D, D], F32).ap()
    bcc_d = p.in_("bcc", [128, 8], F32).ap()
    gate_d = p.in_("gate", [D, 9], F32).ap()
    wg_d = p.in_("wg", [9, D, 1408], F32).ap()
    wu_d = p.in_("wu", [9, D, 1408], F32).ap()
    wd_d = p.in_("wd", [9, 1408, D], F32).ap()
    wq2_d = p.in_("wq2", [D, D], F32).ap()
    wk2_d = p.in_("wk2", [D, D], F32).ap()
    wv2_d = p.in_("wv2", [D, D], F32).ap()
    x2_o = p.out("x2", [D, T], F32).ap()
    q2_o = p.out("q2", [D, T], F32).ap()
    k2_o = p.out("k2", [D, T], F32).ap()
    v2_o = p.out("v2", [T, D], F32).ap()
    x1_dram = p.internal("x1park", [D, T], F32).ap()
    acc_dram = p.internal("moeacc", [T, D], F32).ap()

    x_tiles = [p.sbuf(f"xs{i}", [128, T], F32) for i in range(8)]
    o_tiles = [p.sbuf(f"os{i}", [128, T], BF16) for i in range(8)]
    z_tiles = [p.sbuf(f"zs{i}", [128, T], BF16) for i in range(8)]
    gate_sb = [p.sbuf(f"gt{i}", [128, 9], F32) for i in range(8)]
    mean_row = p.sbuf("mrow", [1, T], F32)
    rstd_row = p.sbuf("rrow", [1, T], F32)

    x_evs = emit_load_fm(sect, x_d, x_tiles)
    o_evs = []
    for dc in range(8):
        o_evs.append(sect.ld.put([lambda g, dc=dc: g.dma_start(
            out=o_tiles[dc][:, :], in_=o_d[dc * 128:(dc + 1) * 128, :])]))
    o_ev = o_evs[-1]
    small = Stream(p, "gpsimd", "sm", 2)
    bcc = p.sbuf("bcc_s", [128, 8], F32)
    ev_bc = small.put([lambda g: g.dma_start(out=bcc[:, :], in_=bcc_d)])
    gate_evs = []
    for dc in range(8):
        gate_evs.append(small.put([lambda g, dc=dc: g.dma_start(
            out=gate_sb[dc][:, :], in_=gate_d[dc * 128:(dc + 1) * 128, :])]))

    wc_t, wc_e, wc_i = emit_weight_load(sect, wc_d, D, D, "wc")
    x1_evs = emit_out_proj_residual(sect, o_tiles, o_ev, wc_t, wc_e, x_tiles,
                                    x_evs, bias_col=bcc, bias_ev=ev_bc)
    sect.wpool_release(wc_i, Ev(p.clock_sem["tensor"], p.clock["tensor"]))
    park_evs = emit_store_fm(sect, x_tiles, x1_evs, x1_dram)
    z_ev = emit_ln(sect, x_tiles, x1_evs, z_tiles, keep_stats=(mean_row, rstd_row))
    comb_tm, comb_evs, comb_fm, comb_fm_ev = emit_gating(
        sect, x_tiles, x1_evs, mean_row, rstd_row, z_ev, gate_sb,
        gate_evs[-2:], "L1")
    o_dead = z_ev
    gating_done = Ev(p.clock_sem["tensor"], p.clock["tensor"])
    last_acc, wx_done, wd_done = emit_moe_experts(
        sect, z_tiles, z_ev, comb_tm, comb_evs, acc_dram,
        wg_d, wu_d, wd_d, x_tiles, park_evs, o_tiles, o_dead, "L1",
        x_read_done=[gating_done, z_ev])
    x2_evs = emit_moe_finale(sect, acc_dram, last_acc, x_tiles, x1_dram,
                             wx_done, o_tiles, wd_done, x2_o, park_evs=park_evs)
    z2_ev = emit_ln(sect, x_tiles, x2_evs, z_tiles)
    wq_t, wq_e, wq_i = emit_weight_load(sect, wq2_d, D, D, "wq2")
    emit_qk_plain(sect, z_tiles, z2_ev, wq_t, wq_e, q2_o)
    sect.wpool_release(wq_i, Ev(p.clock_sem["tensor"], p.clock["tensor"]))
    wk_t, wk_e, wk_i = emit_weight_load(sect, wk2_d, D, D, "wk2")
    emit_qk_plain(sect, z_tiles, z2_ev, wk_t, wk_e, k2_o)
    sect.wpool_release(wk_i, Ev(p.clock_sem["tensor"], p.clock["tensor"]))
    wv_t, wv_e, wv_i = emit_weight_load(sect, wv2_d, D, D, "wv2")
    emit_v_tm_to_hbm(sect, z_tiles, z2_ev, wv_t, wv_e, v2_o)
    sect.wpool_release(wv_i, Ev(p.clock_sem["tensor"], p.clock["tensor"]))
    finish(p, sect)
    return p.build()


def rope_tables_L3(positions):
    T = len(positions)
    inv = 1.0 / 10000.0 ** (np.arange(0, DK, 2, dtype=np.float64) / DK)
    fr = positions[:, None].astype(np.float64) * inv[None, :]
    cosd = np.empty((DK, T))
    sinS = np.empty((DK, T))
    for i in range(32):
        g = fr[:, i // 2]
        hh = fr[:, i // 2 + 16]
        cosd[2 * i] = np.cos(g)
        cosd[2 * i + 1] = np.cos(hh)
        sinS[2 * i] = -np.sin(g)
        sinS[2 * i + 1] = +np.sin(hh)
    return (np.concatenate([cosd, cosd], 0).astype(np.float32),
            np.concatenate([sinS, sinS], 0).astype(np.float32))


def build_B2(T):
    p = Prog("B2")
    sect = Sect(p, T, n_pool=6)
    x_d = p.in_("x", [D, T], F32).ap()
    o_d = p.in_("o", [D, T], F32).ap()
    wo_d = p.in_("wo", [D, D], F32).ap()
    boc_d = p.in_("boc", [128, 8], F32).ap()
    gate_d = p.in_("gate", [D, 9], F32).ap()
    gb_d = p.in_("gb", [1, 8], F32).ap()
    wg_d = p.in_("wg", [9, D, 1408], F32).ap()
    wu_d = p.in_("wu", [9, D, 1408], F32).ap()
    wd_d = p.in_("wd", [9, 1408, D], F32).ap()
    b1c_d = p.in_("b1c", [9, 128, 11], F32).ap()
    b3c_d = p.in_("b3c", [9, 128, 11], F32).ap()
    b2r_d = p.in_("b2r", [9, D], F32).ap()
    wq3_d = p.in_("wq3", [D, D], F32).ap()
    wk3_d = p.in_("wk3", [D, D], F32).ap()
    wv3_d = p.in_("wv3", [D, D], F32).ap()
    rtc_d = p.in_("rtc", [128, T], F32).ap()
    rts_d = p.in_("rts", [128, T], F32).ap()
    x2_o = p.out("x2", [D, T], F32).ap()
    q3_o = p.out("q3", [D, T], F32).ap()
    k3_o = p.out("k3", [D, T], F32).ap()
    v3_o = p.out("v3", [T, D], F32).ap()
    x1_dram = p.internal("x1park", [D, T], F32).ap()
    acc_dram = p.internal("moeacc", [T, D], F32).ap()

    x_tiles = [p.sbuf(f"xs{i}", [128, T], F32) for i in range(8)]
    o_tiles = [p.sbuf(f"os{i}", [128, T], BF16) for i in range(8)]
    z_tiles = [p.sbuf(f"zs{i}", [128, T], BF16) for i in range(8)]
    gate_sb = [p.sbuf(f"gt{i}", [128, 9], F32) for i in range(8)]
    mean_row = p.sbuf("mrow", [1, T], F32)
    rstd_row = p.sbuf("rrow", [1, T], F32)
    if T >= 2048:
        ncol = T // 2
        def rtc_ap(q0, qn):
            ti = 3 + q0 // ncol
            off = q0 % ncol
            return o_tiles[ti][:, :].bitcast(F32)[:, off:off + qn]
        def rts_ap(q0, qn):
            ti = 5 + q0 // ncol
            off = q0 % ncol
            return o_tiles[ti][:, :].bitcast(F32)[:, off:off + qn]
        rtab = [(3, 0), (4, 0)]
    else:
        ncol = T
        rtc_sm = p.sbuf("rtc_sm", [128, T], F32)
        rts_sm = p.sbuf("rts_sm", [128, T], F32)
        rtc_ap = lambda q0, qn: rtc_sm[:, q0:q0 + qn]
        rts_ap = lambda q0, qn: rts_sm[:, q0:q0 + qn]

    x_evs = emit_load_fm(sect, x_d, x_tiles)
    o_evs = []
    for dc in range(8):
        o_evs.append(sect.ld.put([lambda g, dc=dc: g.dma_start(
            out=o_tiles[dc][:, :], in_=o_d[dc * 128:(dc + 1) * 128, :])]))
    o_ev = o_evs[-1]
    small = Stream(p, "gpsimd", "sm", 2)
    boc = p.sbuf("boc_s", [128, 8], F32)
    gbr = p.sbuf("gbr_s", [1, 8], F32)
    ev_bo = small.put([lambda g: g.dma_start(out=boc[:, :], in_=boc_d)])
    ev_gb = small.put([lambda g: g.dma_start(out=gbr[:, :], in_=gb_d)])
    gate_evs = []
    for dc in range(8):
        gate_evs.append(small.put([lambda g, dc=dc: g.dma_start(
            out=gate_sb[dc][:, :], in_=gate_d[dc * 128:(dc + 1) * 128, :])]))

    wo_t, wo_e, wo_i = emit_weight_load(sect, wo_d, D, D, "wo2")
    x1_evs = emit_out_proj_residual(sect, o_tiles, o_ev, wo_t, wo_e, x_tiles,
                                    x_evs, bias_col=boc, bias_ev=ev_bo)
    sect.wpool_release(wo_i, Ev(p.clock_sem["tensor"], p.clock["tensor"]))
    park_evs = emit_store_fm(sect, x_tiles, x1_evs, x1_dram)
    z_ev = emit_ln(sect, x_tiles, x1_evs, z_tiles, keep_stats=(mean_row, rstd_row))
    comb_tm, comb_evs, comb_fm, comb_fm_ev = emit_gating(
        sect, x_tiles, x1_evs, mean_row, rstd_row, z_ev, gate_sb,
        gate_evs[-2:], "L2", gb_row=gbr, gb_ev=ev_gb)
    gating_done = Ev(p.clock_sem["tensor"], p.clock["tensor"])
    last_acc, wx_done, wd_done = emit_moe_experts(
        sect, z_tiles, z_ev, comb_tm, comb_evs, acc_dram,
        wg_d, wu_d, wd_d, x_tiles, park_evs, o_tiles, z_ev, "L2",
        b1c_d=b1c_d, b3c_d=b3c_d, b2r_d=b2r_d,
        x_read_done=[gating_done, z_ev])
    x2_evs = emit_moe_finale(sect, acc_dram, last_acc, x_tiles, x1_dram,
                             wx_done, o_tiles, wd_done, x2_o, park_evs=park_evs)
    z2_ev = emit_ln(sect, x_tiles, x2_evs, z_tiles)
    if T >= 2048:
        ev_rt = sect.act_ld.put([
            lambda y: y.dma_start(out=o_tiles[3][:, :].bitcast(F32)[:, 0:ncol],
                                  in_=rtc_d[:, 0:ncol]),
            lambda y: y.dma_start(out=o_tiles[4][:, :].bitcast(F32)[:, 0:T - ncol],
                                  in_=rtc_d[:, ncol:T]),
        ], waits=[z_ev])
        ev_rt2 = sect.act_ld.put([
            lambda y: y.dma_start(out=o_tiles[5][:, :].bitcast(F32)[:, 0:ncol],
                                  in_=rts_d[:, 0:ncol]),
            lambda y: y.dma_start(out=o_tiles[6][:, :].bitcast(F32)[:, 0:T - ncol],
                                  in_=rts_d[:, ncol:T]),
        ], waits=[z_ev])
    else:
        ev_rt = sect.act_ld.put([lambda y: y.dma_start(out=rtc_sm[:, :], in_=rtc_d)])
        ev_rt2 = sect.act_ld.put([lambda y: y.dma_start(out=rts_sm[:, :], in_=rts_d)])
    wq_t, wq_e, wq_i = emit_weight_load(sect, wq3_d, D, D, "wq3")
    emit_qk_rope_to_hbm(sect, z_tiles, z2_ev, wq_t, wq_e, rtc_ap, rts_ap,
                        [ev_rt, ev_rt2], q3_o)
    sect.wpool_release(wq_i, Ev(p.clock_sem["tensor"], p.clock["tensor"]))
    wk_t, wk_e, wk_i = emit_weight_load(sect, wk3_d, D, D, "wk3")
    emit_qk_rope_to_hbm(sect, z_tiles, z2_ev, wk_t, wk_e, rtc_ap, rts_ap,
                        [ev_rt, ev_rt2], k3_o)
    sect.wpool_release(wk_i, Ev(p.clock_sem["tensor"], p.clock["tensor"]))
    wv_t, wv_e, wv_i = emit_weight_load(sect, wv3_d, D, D, "wv3")
    emit_v_tm_to_hbm(sect, z_tiles, z2_ev, wv_t, wv_e, v3_o)
    sect.wpool_release(wv_i, Ev(p.clock_sem["tensor"], p.clock["tensor"]))
    finish(p, sect)
    return p.build()


def build_B3(T):
    p = Prog("B3")
    sect = Sect(p, T, n_pool=6)
    x_d = p.in_("x", [D, T], F32).ap()
    o_d = p.in_("o", [D, T], F32).ap()
    wo_d = p.in_("wo", [D, D], F32).ap()
    wqf_d = p.in_("wqf", [D, D], F32).ap()
    wkf_d = p.in_("wkf", [D, D], F32).ap()
    wvf_d = p.in_("wvf", [D, D], F32).ap()
    x2_o = p.out("x2", [D, T], F32).ap()
    qf_o = p.out("qf", [D, T], F32).ap()
    kf_o = p.out("kf", [D, T], F32).ap()
    vf_o = p.out("vf", [T, D], F32).ap()

    x_tiles = [p.sbuf(f"xs{i}", [128, T], F32) for i in range(8)]
    o_tiles = [p.sbuf(f"os{i}", [128, T], BF16) for i in range(8)]
    z_tiles = [p.sbuf(f"zs{i}", [128, T], BF16) for i in range(8)]

    x_evs = emit_load_fm(sect, x_d, x_tiles)
    o_evs = []
    for dc in range(8):
        o_evs.append(sect.ld.put([lambda g, dc=dc: g.dma_start(
            out=o_tiles[dc][:, :], in_=o_d[dc * 128:(dc + 1) * 128, :])]))
    o_ev = o_evs[-1]
    wo_t, wo_e, wo_i = emit_weight_load(sect, wo_d, D, D, "wo3")
    x1_evs = emit_out_proj_residual(sect, o_tiles, o_ev, wo_t, wo_e, x_tiles, x_evs)
    sect.wpool_release(wo_i, Ev(p.clock_sem["tensor"], p.clock["tensor"]))
    emit_store_fm(sect, x_tiles, x1_evs, x2_o)
    z_ev = emit_ln(sect, x_tiles, x1_evs, z_tiles)
    wq_t, wq_e, wq_i = emit_weight_load(sect, wqf_d, D, D, "wqf")
    emit_qk_plain(sect, z_tiles, z_ev, wq_t, wq_e, qf_o)
    sect.wpool_release(wq_i, Ev(p.clock_sem["tensor"], p.clock["tensor"]))
    wk_t, wk_e, wk_i = emit_weight_load(sect, wkf_d, D, D, "wkf")
    emit_qk_plain(sect, z_tiles, z_ev, wk_t, wk_e, kf_o)
    sect.wpool_release(wk_i, Ev(p.clock_sem["tensor"], p.clock["tensor"]))
    wv_t, wv_e, wv_i = emit_weight_load(sect, wvf_d, D, D, "wvf")
    emit_v_tm_to_hbm(sect, z_tiles, z_ev, wv_t, wv_e, vf_o)
    sect.wpool_release(wv_i, Ev(p.clock_sem["tensor"], p.clock["tensor"]))
    finish(p, sect)
    return p.build()


def emit_l2norm_pass(sect, in_tiles, in_evs, out_tiles):
    p, T = sect.p, sect.T
    xsq = sect.scr[0]
    stat_sq, _, arow, _ = sect.ln_rows()
    last = None
    for qc in range(sect.NQC):
        q0 = qc * 512
        qn = min(512, T - q0)
        i_q, ps_q, rel_q = sect.pool.get()
        ev_mm2 = None
        for dc in range(8):
            ev_sq = p.op("scalar", lambda s, dc=dc, q0=q0, qn=qn: s.activation(
                xsq[:, :qn], in_tiles[dc][:, q0:q0 + qn], AF.Square),
                waits=[in_evs[dc] if isinstance(in_evs, list) else in_evs, ev_mm2],
                sig=True)
            ev_mm2 = p.op("tensor", lambda t, dc=dc, qn=qn, ps=ps_q: t.matmul(
                ps[0:1, :qn], sect.ones_col[:, :], xsq[:, :qn],
                start=(dc == 0), stop=(dc == 7)),
                waits=[ev_sq, sect.ones_col_ev] + ([rel_q] if dc == 0 else []),
                sig=True)
        ev_c = p.op("vector", lambda v, qn=qn, ps=ps_q: v.tensor_copy(
            stat_sq[:, :qn], ps[0:1, :qn]), waits=[ev_mm2], sig=True)
        sect.pool.release(i_q, ev_c)
        ev_sq2 = p.op("scalar", lambda s, qn=qn: s.activation(
            arow[:, :qn], stat_sq[:, :qn], AF.Sqrt), waits=[ev_c], sig=True)
        ev_r = p.seq("vector", [
            lambda v, qn=qn: v.tensor_scalar_max(arow[:, :qn], arow[:, :qn], 1e-12),
            lambda v, qn=qn: v.reciprocal(arow[:, :qn], arow[:, :qn]),
        ], waits=[ev_sq2])
        i_a, ps_a, rel_a = sect.pool.get()
        ev_ba = p.op("tensor", lambda t, qn=qn, ps=ps_a: t.matmul(
            ps[:, :qn], sect.ones_row[:, :], arow[:, :qn],
            start=True, stop=True), waits=[ev_r, rel_a, sect.ones_ev], sig=True)
        for dc in range(8):
            last = p.op("vector", lambda v, dc=dc, q0=q0, qn=qn, ps_a=ps_a: v.tensor_tensor(
                out_tiles[dc][:, q0:q0 + qn], in_tiles[dc][:, q0:q0 + qn],
                ps_a[:, :qn], ALU.mult),
                waits=[ev_ba], sig=True)
        sect.pool.release(i_a, last)
    return last


def build_B4(T):
    p = Prog("B4")
    sect = Sect(p, T, n_pool=6)
    x_d = p.in_("x", [D, T], F32).ap()
    o_d = p.in_("o", [D, T], F32).ap()
    wo_d = p.in_("wo", [D, D], F32).ap()
    wff_d = p.in_("wff", [D, D], F32).ap()
    xo_o = p.out("xout", [D, T], F32).ap()

    x_tiles = [p.sbuf(f"xs{i}", [128, T], F32) for i in range(8)]
    o_tiles = [p.sbuf(f"os{i}", [128, T], BF16) for i in range(8)]
    z_tiles = [p.sbuf(f"zs{i}", [128, T], BF16) for i in range(8)]
    z2_tiles = o_tiles

    x_evs = emit_load_fm(sect, x_d, x_tiles)
    o_evs = []
    for dc in range(8):
        o_evs.append(sect.ld.put([lambda g, dc=dc: g.dma_start(
            out=o_tiles[dc][:, :], in_=o_d[dc * 128:(dc + 1) * 128, :])]))
    o_ev = o_evs[-1]
    wo_t, wo_e, wo_i = emit_weight_load(sect, wo_d, D, D, "wo4")
    zz_evs = [None] * 8

    def cb1(oc, qc, q0, qn, i_p, ps, ev_mm):
        ev = p.op("scalar", lambda s, oc=oc, q0=q0, qn=qn, ps=ps: s.copy(
            z_tiles[oc][:, q0:q0 + qn], ps[:, :qn]), waits=[ev_mm], sig=True)
        sect.pool.release(i_p, ev)
        zz_evs[oc] = ev
    emit_proj_fm(sect, o_tiles, o_ev, wo_t, wo_e, cb1)
    sect.wpool_release(wo_i, Ev(p.clock_sem["tensor"], p.clock["tensor"]))
    n1_ev = emit_l2norm_pass(sect, z_tiles, zz_evs, z2_tiles)
    wf_t, wf_e, wf_i = emit_weight_load(sect, wff_d, D, D, "wff")
    zz2_evs = [None] * 8

    def cb2(oc, qc, q0, qn, i_p, ps, ev_mm):
        ev = p.op("scalar", lambda s, oc=oc, q0=q0, qn=qn, ps=ps: s.copy(
            z_tiles[oc][:, q0:q0 + qn], ps[:, :qn]), waits=[ev_mm, zz_evs[oc]], sig=True)
        sect.pool.release(i_p, ev)
        zz2_evs[oc] = ev
    emit_proj_fm(sect, z2_tiles, n1_ev, wf_t, wf_e, cb2)
    sect.wpool_release(wf_i, Ev(p.clock_sem["tensor"], p.clock["tensor"]))
    n2_ev = emit_l2norm_pass(sect, z_tiles, zz2_evs, z2_tiles)
    x2_evs = [None] * 8
    prev_a = None
    for dc in range(8):
        for qc in range(sect.NQC):
            q0 = qc * 512
            qn = min(512, T - q0)
            ev_g = p.op("scalar", lambda s, dc=dc, q0=q0, qn=qn: s.activation(
                sect.scr[1][:, :qn], z2_tiles[dc][:, q0:q0 + qn], AF.Gelu),
                waits=[n2_ev, prev_a], sig=True)
            ev_a = p.op("vector", lambda v, dc=dc, q0=q0, qn=qn: v.tensor_tensor(
                x_tiles[dc][:, q0:q0 + qn], sect.scr[1][:, :qn],
                x_tiles[dc][:, q0:q0 + qn], ALU.add),
                waits=[ev_g, x_evs[dc]], sig=True)
            prev_a = ev_a
            x2_evs[dc] = ev_a
    emit_store_fm(sect, x_tiles, x2_evs, xo_o)
    finish(p, sect)
    return p.build()


def _t(a):
    return np.ascontiguousarray(np.asarray(a).T.astype(np.float32))


def _col(b, n=128):
    b = np.asarray(b).reshape(-1)
    return np.ascontiguousarray(b.reshape(-1, n).T.astype(np.float32))


_PROG_CACHE = {}


def _prog(name, builder):
    if name not in _PROG_CACHE:
        _PROG_CACHE[name] = builder()
    return _PROG_CACHE[name]


TT = 2 * S // NCORES * 4 // 4
TOK = B * S // NCORES


def _gather_heads(per_core_fm, batch):
    return np.concatenate([per_core_fm[2 * batch], per_core_fm[2 * batch + 1]], axis=1)


def _attn_in_rows(Qc, pairs):
    out = np.empty((8, 64, S), np.float32)
    for i, (b, h) in enumerate(pairs):
        out[i] = Qc[b][h * 64:(h + 1) * 64]
    return out


def _attn_in_vt(Vc, pairs, aug):
    w = 68 if aug else 64
    out = np.zeros((8, S, w), np.float32)
    for i, (b, h) in enumerate(pairs):
        out[i, :, :64] = Vc[b][:, h * 64:(h + 1) * 64]
        if aug:
            out[i, :, 64] = 1.0
    return out


def _scatter_heads(ofs, pairs_of_core):
    O = {b: np.empty((D, S), np.float32) for b in range(B)}
    for c in range(NCORES):
        for i, (b, h) in enumerate(pairs_of_core[c]):
            O[b][h * 64:(h + 1) * 64] = ofs[c][i]
    return O


def _shard_fm(O):
    return [np.ascontiguousarray(O[c // 2][:, (c % 2) * TOK:(c % 2 + 1) * TOK])
            for c in range(NCORES)]


def _run(nc, in_maps):
    res = run_bass_kernel_spmd(nc, in_maps, core_ids=list(range(NCORES)))
    return res.results


def kernel(**inputs):
    ii = {k: np.asarray(v) for k, v in inputs.items()}
    X = ii["L0_attn_x"].astype(np.float32)
    Xf = X.reshape(-1, D)

    pairs_std = [[(c // 2, (c % 2) * 8 + j) for j in range(8)] for c in range(NCORES)]
    pairs_l2 = [[(c // 2, 2 * j + (c % 2)) for j in range(8)] for c in range(NCORES)]

    nc = _prog("pre", lambda: build_pre(TOK))
    maps = []
    for c in range(NCORES):
        rows = slice(c * TOK, (c + 1) * TOK)
        pos = np.arange(c * TOK, (c + 1) * TOK) % S
        rtc, rts = rope_tables_L0(pos)
        maps.append({
            "x": _t(Xf[rows]),
            "wqt": _t(ii["L0_attn_WQ"]), "wkt": _t(ii["L0_attn_WK"]),
            "wvt": _t(ii["L0_attn_WV"]),
            "rtc": rtc, "rts": rts,
        })
    r = _run(nc, maps)
    Qc = {b: _gather_heads([r[c]["q"] for c in range(8)], b) for b in range(B)}
    Kc = {b: _gather_heads([r[c]["k"] for c in range(8)], b) for b in range(B)}
    Vc = {b: np.concatenate([r[2 * b]["v"], r[2 * b + 1]["v"]], axis=0) for b in range(B)}

    nc = _prog("attnL0", lambda: build_attn_L0())
    maps = [{"qf": _attn_in_rows(Qc, pairs_std[c]),
             "kf": _attn_in_rows(Kc, pairs_std[c]),
             "vt": _attn_in_vt(Vc, pairs_std[c], True)} for c in range(NCORES)]
    r = _run(nc, maps)
    O = _scatter_heads([r[c]["of"] for c in range(8)], pairs_std)
    o_sh = _shard_fm(O)

    nc = _prog("B0", lambda: build_B0(TOK))
    Wq1 = ii["L1_attn_Wq"].transpose(1, 0, 2).reshape(D, D)
    Wk1 = ii["L1_attn_Wk"].transpose(1, 0, 2).reshape(D, D)
    Wv1 = ii["L1_attn_Wv"].transpose(1, 0, 2).reshape(D, D)
    maps = []
    for c in range(NCORES):
        rows = slice(c * TOK, (c + 1) * TOK)
        maps.append({
            "x": _t(Xf[rows]), "o": o_sh[c],
            "wot": _t(ii["L0_attn_WO"]),
            "w1": ii["L0_ffn_w1"].astype(np.float32),
            "b1c": _col(ii["L0_ffn_b1"]),
            "w2": ii["L0_ffn_w2"].astype(np.float32),
            "b2c": _col(ii["L0_ffn_b2"]),
            "wq1": np.ascontiguousarray(Wq1.astype(np.float32)),
            "wk1": np.ascontiguousarray(Wk1.astype(np.float32)),
            "wv1": np.ascontiguousarray(Wv1.astype(np.float32)),
            "bq1c": _col(ii["L1_attn_bq"]), "bk1c": _col(ii["L1_attn_bk"]),
            "bv1r": ii["L1_attn_bv"].reshape(1, D).astype(np.float32),
        })
    r = _run(nc, maps)
    x_sh = [r[c]["x2"] for c in range(8)]
    Qc = {b: _gather_heads([r[c]["q1"] for c in range(8)], b) for b in range(B)}
    Kc = {b: _gather_heads([r[c]["k1"] for c in range(8)], b) for b in range(B)}
    Vc = {b: np.concatenate([r[2 * b]["v1"], r[2 * b + 1]["v1"]], axis=0) for b in range(B)}

    nc = _prog("attnL1", lambda: build_attn_col("L1"))
    maps = [{"qf": _attn_in_rows(Qc, pairs_std[c]),
             "kf": _attn_in_rows(Kc, pairs_std[c]),
             "vt": _attn_in_vt(Vc, pairs_std[c], False)} for c in range(NCORES)]
    r = _run(nc, maps)
    O = _scatter_heads([r[c]["of"] for c in range(8)], pairs_std)
    o_sh = _shard_fm(O)

    nc = _prog("B1", lambda: build_B1(TOK))
    gate_ext = np.concatenate([ii["L1_ffn_gate_w"].T,
                               ii["L1_ffn_shared_gate_w"].T], axis=1)
    wg = np.concatenate([ii["L1_ffn_w_gate"], ii["L1_ffn_sw_gate"][None]], 0)
    wu = np.concatenate([ii["L1_ffn_w_up"], ii["L1_ffn_sw_up"][None]], 0)
    wd = np.concatenate([ii["L1_ffn_w_down"], ii["L1_ffn_sw_down"][None]], 0)
    maps = []
    for c in range(NCORES):
        maps.append({
            "x": x_sh[c], "o": o_sh[c],
            "wc": ii["L1_attn_Wc"].astype(np.float32),
            "bcc": _col(ii["L1_attn_bc"]),
            "gate": np.ascontiguousarray(gate_ext.astype(np.float32)),
            "wg": wg.astype(np.float32), "wu": wu.astype(np.float32),
            "wd": wd.astype(np.float32),
            "wq2": _t(ii["L2_attn_Wq"]), "wk2": _t(ii["L2_attn_Wk"]),
            "wv2": _t(ii["L2_attn_Wv"]),
        })
    r = _run(nc, maps)
    x_sh = [r[c]["x2"] for c in range(8)]
    Qc = {b: _gather_heads([r[c]["q2"] for c in range(8)], b) for b in range(B)}
    Kc = {b: _gather_heads([r[c]["k2"] for c in range(8)], b) for b in range(B)}
    Vc = {b: np.concatenate([r[2 * b]["v2"], r[2 * b + 1]["v2"]], axis=0) for b in range(B)}

    nc = _prog("attnL2", lambda: build_attn_L2())
    ebs, sids, grs = [], [], []
    iarr = np.arange(128)
    jarr = np.arange(512)
    for c in range(NCORES):
        eb = np.zeros((8, 128, 64), np.float32)
        sid = np.zeros((8, 128, 128), np.float32)
        gr = np.zeros((8, 1, 1024), np.float32)
        for i, (b, h) in enumerate(pairs_l2[c]):
            sl = alibi_slope(h)
            for dd in range(-32, 32):
                if dd <= -1:
                    eb[i, :, dd + 32] = sl * (dd * 128 + iarr)
                elif dd >= 4:
                    eb[i, :, dd + 32] = sl * (511 - dd * 128 - iarr)
            sid[i] = 8.0 * sl * np.eye(128)
            gr[i, 0, 0:512] = np.exp(-sl * jarr)
            gr[i, 0, 512:1024] = np.exp(-sl * (511 - jarr))
        ebs.append(eb)
        sids.append(sid)
        grs.append(gr)
    maps = [{"qf": _attn_in_rows(Qc, pairs_l2[c]),
             "kf": _attn_in_rows(Kc, pairs_l2[c]),
             "vt": _attn_in_vt(Vc, pairs_l2[c], True),
             "eb": ebs[c], "sid": sids[c], "grow": grs[c]} for c in range(NCORES)]
    r = _run(nc, maps)
    O = _scatter_heads([r[c]["of"] for c in range(8)], pairs_l2)
    o_sh = _shard_fm(O)

    nc = _prog("B2", lambda: build_B2(TOK))
    gate_ext = np.concatenate([ii["L2_ffn_gate_w"].T, np.zeros((D, 1), np.float32)],
                              axis=1)
    wg = np.concatenate([ii["L2_ffn_w1"], ii["L2_ffn_sw1"][None]], 0)
    wu = np.concatenate([ii["L2_ffn_w3"], ii["L2_ffn_sw3"][None]], 0)
    wd = np.concatenate([ii["L2_ffn_w2"], ii["L2_ffn_sw2"][None]], 0)
    b1 = np.concatenate([ii["L2_ffn_b1"], ii["L2_ffn_sb1"][None]], 0)
    b3 = np.concatenate([ii["L2_ffn_b3"], ii["L2_ffn_sb3"][None]], 0)
    b2 = np.concatenate([ii["L2_ffn_b2"], ii["L2_ffn_sb2"][None]], 0)
    b1c = np.ascontiguousarray(b1.reshape(9, 11, 128).transpose(0, 2, 1)).astype(np.float32)
    b3c = np.ascontiguousarray(b3.reshape(9, 11, 128).transpose(0, 2, 1)).astype(np.float32)
    perm = np.empty(64, np.int64)
    perm[0::2] = np.arange(32)
    perm[1::2] = np.arange(32) + 32
    Wq3 = ii["L3_attn_Wq"].reshape(D, 16, 64)[:, :, perm].reshape(D, D)
    Wk3 = ii["L3_attn_Wk"].reshape(D, 16, 64)[:, :, perm].reshape(D, D)
    maps = []
    for c in range(NCORES):
        pos = np.arange(c * TOK, (c + 1) * TOK) % S
        rtc, rts = rope_tables_L3(pos)
        maps.append({
            "x": x_sh[c], "o": o_sh[c],
            "wo": _t(ii["L2_attn_Wo"]),
            "boc": _col(ii["L2_attn_bo"]),
            "gate": np.ascontiguousarray(gate_ext.astype(np.float32)),
            "gb": ii["L2_ffn_gate_b"].reshape(1, 8).astype(np.float32),
            "wg": wg.astype(np.float32), "wu": wu.astype(np.float32),
            "wd": wd.astype(np.float32),
            "b1c": b1c, "b3c": b3c, "b2r": b2.astype(np.float32),
            "wq3": np.ascontiguousarray(Wq3.astype(np.float32)),
            "wk3": np.ascontiguousarray(Wk3.astype(np.float32)),
            "wv3": ii["L3_attn_Wv"].astype(np.float32),
            "rtc": rtc, "rts": rts,
        })
    r = _run(nc, maps)
    x_sh = [r[c]["x2"] for c in range(8)]
    Qc = {b: _gather_heads([r[c]["q3"] for c in range(8)], b) for b in range(B)}
    Kc = {b: _gather_heads([r[c]["k3"] for c in range(8)], b) for b in range(B)}
    Vc = {b: np.concatenate([r[2 * b]["v3"], r[2 * b + 1]["v3"]], axis=0) for b in range(B)}

    nc = _prog("attnL3a", lambda: build_attn_L3a())
    relp = ii["L3_attn_rel_emb"].astype(np.float32)[:, :, perm]
    jj = np.arange(S)
    bkj = _bucket_np(jj)
    ccols = np.arange(L3A_LD)
    jc = L3A_JOFF - ccols
    valid = (jc >= 0) & (jc < S)
    relr_h = np.zeros((16, 64, L3A_LD), np.float32)
    for h in range(16):
        tmp = np.zeros((L3A_LD, 64), np.float32)
        tmp[valid] = relp[bkj[jc[valid]], h, :]
        relr_h[h] = tmp.T
    maps = []
    for c in range(NCORES):
        relr = np.stack([relr_h[h] for (_, h) in pairs_std[c]])
        maps.append({"qf": _attn_in_rows(Qc, pairs_std[c]),
                     "kf": _attn_in_rows(Kc, pairs_std[c]),
                     "vt": _attn_in_vt(Vc, pairs_std[c], True),
                     "relr": relr})
    r = _run(nc, maps)
    O = _scatter_heads([r[c]["of"] for c in range(8)], pairs_std)
    o_sh = _shard_fm(O)

    nc = _prog("B3", lambda: build_B3(TOK))
    maps = []
    for c in range(NCORES):
        maps.append({
            "x": x_sh[c], "o": o_sh[c],
            "wo": ii["L3_attn_Wo"].astype(np.float32),
            "wqf": ii["L3_ffn_Wq"].astype(np.float32),
            "wkf": ii["L3_ffn_Wk"].astype(np.float32),
            "wvf": ii["L3_ffn_Wv"].astype(np.float32),
        })
    r = _run(nc, maps)
    x_sh = [r[c]["x2"] for c in range(8)]
    Qc = {b: _gather_heads([r[c]["qf"] for c in range(8)], b) for b in range(B)}
    Kc = {b: _gather_heads([r[c]["kf"] for c in range(8)], b) for b in range(B)}
    Vc = {b: np.concatenate([r[2 * b]["vf"], r[2 * b + 1]["vf"]], axis=0) for b in range(B)}

    nc = _prog("attnL3f", lambda: build_attn_col("L3f"))
    maps = [{"qf": _attn_in_rows(Qc, pairs_std[c]),
             "kf": _attn_in_rows(Kc, pairs_std[c]),
             "vt": _attn_in_vt(Vc, pairs_std[c], False)} for c in range(NCORES)]
    r = _run(nc, maps)
    O = _scatter_heads([r[c]["of"] for c in range(8)], pairs_std)
    o_sh = _shard_fm(O)

    nc = _prog("B4", lambda: build_B4(TOK))
    maps = []
    for c in range(NCORES):
        maps.append({
            "x": x_sh[c], "o": o_sh[c],
            "wo": ii["L3_ffn_Wo"].astype(np.float32),
            "wff": ii["L3_ffn_Wff"].astype(np.float32),
        })
    r = _run(nc, maps)
    out = np.empty((B * S, D), np.float32)
    for c in range(NCORES):
        out[c * TOK:(c + 1) * TOK] = r[c]["xout"].T
    return out.reshape(B, S, D)


def _bucket_np(relv, NB=256, MAXD=16384):
    ab = np.abs(relv)
    me, half = NB // 4, NB // 2
    small = ab < me
    safe = np.maximum(ab, 1).astype(np.float32)
    expr = (np.log(safe / np.float32(me)) / np.float32(math.log(MAXD / me))
            * np.float32(half - me))
    large = me + expr.astype(np.int64)
    large = np.minimum(large, half - 1)
    bk = np.where(small, ab, large)
    return np.where(relv < 0, half + bk, bk).astype(np.int64)


# version 03 (final)
# speedup vs baseline: 1.0054x; 1.0054x over previous
import math
from contextlib import ExitStack

import numpy as np
import ml_dtypes

import concourse.bass as bass
import concourse.mybir as mybir
from concourse.bass_utils import run_bass_kernel_spmd

F32 = mybir.dt.float32
BF16 = mybir.dt.bfloat16
AF = mybir.ActivationFunctionType
ALU = mybir.AluOpType
BF = ml_dtypes.bfloat16

B, S, D = 4, 4096, 1024
NH, DK = 16, 64
NCORES = 8

ENGINES = ("tensor", "scalar", "vector", "gpsimd", "sync")

PAIR_SWAP_MASK = []
for _i in range(0, 32, 2):
    PAIR_SWAP_MASK += [_i + 1, _i]


class Ev:
    __slots__ = ("sem", "val")

    def __init__(self, sem, val):
        self.sem = sem
        self.val = val

    def __repr__(self):
        return f"Ev({self.sem},{self.val})"


class Prog:

    def __init__(self, name="prog"):
        self.nc = bass.Bass("TRN2", target_bir_lowering=False)
        self.ops = {e: [] for e in ENGINES}
        self.sem_names = []
        self.clock_sem = {e: self._sem(f"clk_{e}") for e in ENGINES}
        self.clock = {e: 0 for e in ENGINES}
        self.sem_handles = {}
        self._uid = 0

    def _sem(self, name):
        self.sem_names.append(name)
        return name

    def stream_sem(self, name):
        return self._sem(name)

    def uid(self):
        self._uid += 1
        return self._uid

    def in_(self, name, shape, dtype):
        return self.nc.dram_tensor(name, list(shape), dtype, kind="ExternalInput")

    def out(self, name, shape, dtype):
        return self.nc.dram_tensor(name, list(shape), dtype, kind="ExternalOutput")

    def internal(self, name, shape, dtype):
        return self.nc.dram_tensor(name, list(shape), dtype, kind="Internal")

    def const(self, name, arr):
        return self.nc.inline_tensor(np.ascontiguousarray(arr), name=name)

    def sbuf(self, name, shape, dtype):
        return self.nc.alloc_sbuf_tensor(name, list(shape), dtype)

    def psum(self, name, shape, dtype=F32):
        return self.nc.alloc_psum_tensor(name, list(shape), dtype)

    def op(self, eng, builder, waits=(), sig=False):
        waits = [w for w in waits if w is not None]
        sig_val = None
        if sig:
            self.clock[eng] += 1
            sig_val = self.clock[eng]
        self.ops[eng].append(("op", builder, tuple(waits), sig_val))
        if sig:
            return Ev(self.clock_sem[eng], sig_val)
        return None

    def seq(self, eng, builders, waits=()):
        ev = None
        first = True
        for b in builders:
            ev = self.op(eng, b, waits=(list(waits) + [ev]) if first or ev else [ev],
                         sig=True)
            first = False
        return ev

    def dma(self, eng, sem, cum, builder, waits=()):
        waits = [w for w in waits if w is not None]
        self.ops[eng].append(("dma", builder, tuple(waits), (sem, 16)))
        return Ev(sem, cum)

    def build(self, sim_serial=False):
        nc = self.nc
        aux = {}
        if sim_serial:
            for e in ENGINES:
                aux[e] = self._sem(f"aux_{e}")
        with ExitStack() as ctx:
            for s in self.sem_names:
                self.sem_handles[s] = ctx.enter_context(nc.semaphore(s))
            block = ctx.enter_context(nc.Block())
            for eng_name in ENGINES:
                ops = self.ops[eng_name]
                if not ops:
                    continue
                clk = self.sem_handles[self.clock_sem[eng_name]]
                auxs = self.sem_handles.get(aux.get(eng_name))

                def replay(engine, ops=ops, clk=clk, eng_name=eng_name, auxs=auxs):
                    seen = {}
                    naux = 0
                    marker = None
                    for kind, builder, waits, sig in ops:
                        for w in waits:
                            if seen.get(w.sem, -1) >= w.val:
                                continue
                            engine.wait_ge(self.sem_handles[w.sem], w.val)
                            seen[w.sem] = w.val
                        if auxs is not None and marker is not None and kind != "dma":
                            engine.wait_ge(marker[0], marker[1])
                        ins = builder(engine)
                        if ins is None:
                            continue
                        if kind == "dma":
                            sem, inc = sig
                            ins.then_inc(self.sem_handles[sem], inc)
                        elif sig is not None:
                            ins.then_inc(clk, 1)
                            marker = (clk, sig)
                        elif auxs is not None:
                            ins.then_inc(auxs, 1)
                            naux += 1
                            marker = (auxs, naux)

                getattr(block, eng_name)(replay)
        return nc


class Stream:

    def __init__(self, prog, eng, name, nslots=2):
        self.p = prog
        self.eng = eng
        self.nslots = nslots
        self.sems = [prog.stream_sem(f"{name}{i}") for i in range(nslots)]
        self.cum = [0] * nslots
        self.i = 0

    def drain_evs(self):
        return [Ev(self.sems[i], self.cum[i]) for i in range(self.nslots)
                if self.cum[i]]

    def put(self, builders, waits=()):
        slot = self.i % self.nslots
        self.i += 1
        sem = self.sems[slot]
        gate = Ev(sem, self.cum[slot]) if self.cum[slot] else None
        ev = None
        for k, b in enumerate(builders):
            self.cum[slot] += 16
            ev = self.p.dma(self.eng, sem, self.cum[slot], b,
                            waits=(list(waits) + [gate]) if k == 0 else ())
        return ev


class PsumPool:

    def __init__(self, prog, n_f32=7):
        self.slots = []
        for i in range(n_f32):
            t = prog.psum(f"pp{i}", [128, 512], F32)
            self.slots.append([t, None])
        self.rr = 0

    def get(self):
        idx = self.rr % len(self.slots)
        self.rr += 1
        t, rel = self.slots[idx]
        return idx, t, rel

    def release(self, idx, ev):
        self.slots[idx][1] = ev


def cdiv(a, b):
    return (a + b - 1) // b



class Sect:
    def __init__(self, p: Prog, T, n_pool=7):
        self.p = p
        self.T = T
        self.NQC = cdiv(T, 512)
        self.ld = Stream(p, "gpsimd", "ld", 4)
        self.act_ld = Stream(p, "sync", "ald", 4)
        self.st = Stream(p, "sync", "st", 6)
        self.pool = PsumPool(p, n_f32=n_pool)
        self.scr = [p.sbuf(f"scr{i}", [128, 512], F32) for i in range(4)]
        self.ones_col = p.sbuf("ones_col", [128, 1], F32)
        self.ones_row = p.sbuf("ones_row", [1, 128], F32)
        c1 = p.const("c_ones_col", np.ones((128, 1), np.float32))
        c2 = p.const("c_ones_row", np.ones((1, 128), np.float32))
        c3 = p.const("c_ones_row_bf", np.ones((1, 128), BF))
        self.ones_row_bf = p.sbuf("ones_row_bf", [1, 128], BF16)
        e1 = self.ld.put([lambda g: g.dma_start(out=self.ones_col[:, :], in_=c1.ap())])
        e2 = self.ld.put([lambda g: g.dma_start(out=self.ones_row[:, :], in_=c2.ap())])
        e3 = self.ld.put([lambda g: g.dma_start(out=self.ones_row_bf[:, :], in_=c3.ap())])
        self.ones_bf_ev = e3
        self.ones_ev = e2
        self.ones_col_ev = e1
        self.ident_sb = None
        self.final_evs = []

    def wpool_get(self):
        if not hasattr(self, "_wpool"):
            self._wpool = []
            for si in range(2):
                tiles = [self.p.sbuf(f"wp{si}_{i}", [128, 1024], BF16)
                         for i in range(8)]
                self._wpool.append([tiles, None])
            self._wpool_rr = 0
        idx = self._wpool_rr % 2
        self._wpool_rr += 1
        tiles, rel = self._wpool[idx]
        return idx, tiles, rel

    def wpool_release(self, idx, ev):
        self._wpool[idx][1] = ev

    def ln_rows(self):
        if not hasattr(self, "_ln_rows"):
            p = self.p
            self._ln_rows = [p.sbuf(f"lnr{i}", [1, 512], F32) for i in range(4)]
        return self._ln_rows

    def prep_bufs(self):
        if not hasattr(self, "_prep"):
            p = self.p
            self._prep = {
                "w": [p.sbuf(f"prw{i}", [128, 1024], BF16) for i in range(2)],
                "s": [p.sbuf(f"prs{i}", [128, 1024], BF16) for i in range(2)],
                "t": [p.sbuf(f"prt{i}", [128, 128], BF16) for i in range(2)],
                "fw": [None, None], "fs": [None, None], "ft": [None, None],
            }
        return self._prep

    def tpsum(self):
        if not hasattr(self, "_tpsum"):
            self._tpsum = [self.p.psum("ps_tr0", [128, 1024], BF16),
                           self.p.psum("ps_tr1", [128, 1024], BF16)]
            self._tpsum_free = [None, None]
        return self._tpsum

    def ident32(self):
        if not hasattr(self, "_id32"):
            p = self.p
            c = p.const("c_ident_f32", np.eye(128, dtype=np.float32))
            self._id32 = p.sbuf("ident32_sb", [128, 128], F32)
            h = self._id32
            self._id32_ev = self.ld.put(
                [lambda g: g.dma_start(out=h[:, :], in_=c.ap())])
        return self._id32, self._id32_ev

    def ident(self):
        if self.ident_sb is None:
            p = self.p
            c = p.const("c_ident_bf", np.eye(128, dtype=BF))
            self.ident_sb = p.sbuf("ident_sb", [128, 128], BF16)
            h = self.ident_sb
            self.ident_ev = self.ld.put(
                [lambda g: g.dma_start(out=h[:, :], in_=c.ap())])
        return self.ident_sb, self.ident_ev



def emit_load_fm(sect, dram_ap, sb_tiles, waits=()):
    evs = []
    for dc in range(len(sb_tiles)):
        def b(g, dc=dc, sb=sb_tiles[dc]):
            return g.dma_start(out=sb[:, :],
                               in_=dram_ap[dc * 128:(dc + 1) * 128, :])
        evs.append(sect.act_ld.put([b], waits=waits))
    return evs


def emit_ln(sect, x_tiles, x_evs, z_tiles, keep_stats=None):
    p, T = sect.p, sect.T
    xsq = sect.scr[0]
    stat_sum, stat_sq, arow, brow = sect.ln_rows()

    last_z = None
    row_free = None
    for qc in range(sect.NQC):
        q0 = qc * 512
        qn = min(512, T - q0)
        sl = slice(0, qn)
        i_s, ps_s, rel_s = sect.pool.get()
        i_q, ps_q, rel_q = sect.pool.get()
        ev_mm = ev_mm2 = None
        for dc in range(8):
            ev_sq = p.op("scalar", lambda s, dc=dc, q0=q0, qn=qn: s.activation(
                xsq[:, :qn], x_tiles[dc][:, q0:q0 + qn], AF.Square),
                waits=[x_evs[dc], ev_mm2], sig=True)
            ev_mm = p.op("tensor", lambda t, dc=dc, q0=q0, qn=qn, ps=ps_s: t.matmul(
                ps[0:1, :qn], sect.ones_col[:, :], x_tiles[dc][:, q0:q0 + qn],
                start=(dc == 0), stop=(dc == 7)),
                waits=[x_evs[dc], sect.ones_col_ev] + ([rel_s] if dc == 0 else []),
                sig=(dc == 7))
            ev_mm2 = p.op("tensor", lambda t, dc=dc, qn=qn, ps=ps_q: t.matmul(
                ps[0:1, :qn], sect.ones_col[:, :], xsq[:, :qn],
                start=(dc == 0), stop=(dc == 7)),
                waits=[ev_sq] + ([rel_q] if dc == 0 else []), sig=True)
        ev_c1 = p.op("vector", lambda v, qn=qn, ps=ps_s: v.tensor_copy(
            stat_sum[:, :qn], ps[0:1, :qn]), waits=[ev_mm, row_free], sig=True)
        ev_c2 = p.op("vector", lambda v, qn=qn, ps=ps_q: v.tensor_copy(
            stat_sq[:, :qn], ps[0:1, :qn]), waits=[ev_mm2], sig=True)
        sect.pool.release(i_s, ev_c1)
        sect.pool.release(i_q, ev_c2)

        ev_r1 = p.seq("vector", [
            lambda v, sl=sl: v.tensor_scalar_mul(stat_sum[:, sl], stat_sum[:, sl], 1.0 / 1024.0),
            lambda v, sl=sl: v.tensor_scalar_mul(stat_sq[:, sl], stat_sq[:, sl], 1.0 / 1024.0),
            lambda v, sl=sl: v.tensor_tensor(arow[:, sl], stat_sum[:, sl], stat_sum[:, sl], ALU.mult),
            lambda v, sl=sl: v.tensor_sub(stat_sq[:, sl], stat_sq[:, sl], arow[:, sl]),
            lambda v, sl=sl: v.tensor_scalar_add(stat_sq[:, sl], stat_sq[:, sl], 1e-5),
        ], waits=[ev_c1, ev_c2])
        ev_sqrt = p.op("scalar", lambda s, qn=qn: s.activation(
            arow[:, :qn], stat_sq[:, :qn], AF.Sqrt),
            waits=[ev_r1], sig=True)

        seq2 = [lambda v, sl=sl: v.reciprocal(arow[:, sl], arow[:, sl])]
        if keep_stats is not None:
            seq2.append(lambda v, sl=sl, q0=q0, qn=qn: v.tensor_copy(
                keep_stats[0][:, q0:q0 + qn], stat_sum[:, sl]))
            seq2.append(lambda v, sl=sl, q0=q0, qn=qn: v.tensor_copy(
                keep_stats[1][:, q0:q0 + qn], arow[:, sl]))
        seq2.append(lambda v, sl=sl: v.tensor_tensor(brow[:, sl], stat_sum[:, sl], arow[:, sl], ALU.mult))
        seq2.append(lambda v, sl=sl: v.tensor_scalar_mul(brow[:, sl], brow[:, sl], -1.0))
        ev_r2 = p.seq("vector", seq2, waits=[ev_sqrt])

        i_a, ps_a, rel_a = sect.pool.get()
        i_b, ps_b, rel_b = sect.pool.get()
        ev_ba = p.op("tensor", lambda t, qn=qn, ps=ps_a: t.matmul(
            ps[:, :qn], sect.ones_row[:, :], arow[:, :qn],
            start=True, stop=True), waits=[ev_r2, rel_a, sect.ones_ev], sig=True)
        ev_bb = p.op("tensor", lambda t, qn=qn, ps=ps_b: t.matmul(
            ps[:, :qn], sect.ones_row[:, :], brow[:, :qn],
            start=True, stop=True), waits=[ev_r2, rel_b], sig=True)
        row_free = ev_bb
        for dc in range(8):
            last_z = p.seq("vector", [
                lambda v, dc=dc, q0=q0, qn=qn, ps_a=ps_a: v.tensor_tensor(
                    sect.scr[1][:, :qn], x_tiles[dc][:, q0:q0 + qn],
                    ps_a[:, :qn], ALU.mult),
                lambda v, dc=dc, q0=q0, qn=qn, ps_b=ps_b: v.tensor_tensor(
                    z_tiles[dc][:, q0:q0 + qn],
                    sect.scr[1][:, :qn], ps_b[:, :qn], ALU.add),
            ], waits=[ev_ba, ev_bb, x_evs[dc]])
        sect.pool.release(i_a, last_z)
        sect.pool.release(i_b, last_z)
    return last_z


def emit_weight_load(sect, dram, rows, cols, name, cast=True, waits=()):
    assert rows <= 1024 and cols <= 1024 and cast
    idx, tiles, rel = sect.wpool_get()
    evs = []
    for rc in range(cdiv(rows, 128)):
        def b(g, t=tiles[rc], rc=rc):
            return g.dma_start(out=t[:, 0:cols],
                               in_=dram[rc * 128:(rc + 1) * 128, :])
        evs.append(sect.ld.put([b], waits=list(waits) + [rel]))
    return tiles, evs, idx


def emit_proj_fm(sect, z_tiles, z_ev, w_tiles, w_evs, out_cb,
                 out_cols=1024, kchunks=8):
    p, T = sect.p, sect.T
    for oc in range(cdiv(out_cols, 128)):
        on = min(128, out_cols - oc * 128)
        for qc in range(sect.NQC):
            q0 = qc * 512
            qn = min(512, T - q0)
            i_p, ps, rel = sect.pool.get()
            ev = None
            for dc in range(kchunks):
                ev = p.op("tensor", lambda t, dc=dc, oc=oc, on=on, q0=q0, qn=qn, ps=ps: t.matmul(
                    ps[:on, :qn], w_tiles[dc][:, oc * 128:oc * 128 + on],
                    z_tiles[dc][:, q0:q0 + qn],
                    start=(dc == 0), stop=(dc == kchunks - 1)),
                    waits=[z_ev, w_evs[dc]] + ([rel] if dc == 0 else []),
                    sig=(dc == kchunks - 1))
            out_cb(oc, qc, q0, qn, i_p, ps, ev)


def emit_qk_rope_to_hbm(sect, z_tiles, z_ev, w_tiles, w_evs, rt_cos, rt_sin,
                        rt_evs, out_dram, out_f32_tiles=None):
    p = sect.p
    if not callable(rt_cos):
        _rc, _rs = rt_cos, rt_sin
        rt_cos = lambda q0, qn: _rc[:, q0:q0 + qn]
        rt_sin = lambda q0, qn: _rs[:, q0:q0 + qn]

    if not hasattr(sect, "_scr3_prev"):
        sect._scr3_prev = {"ev": None}
    prev = sect._scr3_prev

    def cb2(oc, qc, q0, qn, i_p, ps, ev_mm):
        e1 = p.op("vector", lambda v, qn=qn, ps=ps: v.stream_shuffle(
            sect.scr[2][:, :qn], ps[:, :qn], PAIR_SWAP_MASK),
            waits=[ev_mm, prev["ev"]] + list(rt_evs), sig=True)
        e2 = p.op("vector", lambda v, q0=q0, qn=qn, ps=ps: v.tensor_tensor(
            sect.scr[1][:, :qn], ps[:, :qn], rt_cos(q0, qn), ALU.mult),
            waits=[e1], sig=True)
        e3 = p.op("vector", lambda v, q0=q0, qn=qn: v.tensor_tensor(
            sect.scr[2][:, :qn], sect.scr[2][:, :qn], rt_sin(q0, qn),
            ALU.mult), waits=[e1], sig=True)
        ev = p.op("vector", lambda v, qn=qn: v.tensor_tensor(
            sect.scr[3][:, :qn], sect.scr[1][:, :qn], sect.scr[2][:, :qn],
            ALU.add), waits=[e2, e3], sig=True)
        sect.pool.release(i_p, ev)

        def st(y, oc=oc, q0=q0, qn=qn):
            return y.dma_start(out=out_dram[oc * 128:(oc + 1) * 128, q0:q0 + qn],
                               in_=sect.scr[3][:, :qn])
        sev = sect.st.put([st], waits=[ev])
        sect.final_evs.append(sev)
        prev["ev"] = sev

    emit_proj_fm(sect, z_tiles, z_ev, w_tiles, w_evs, cb2)


def emit_v_tm_to_hbm(sect, z_tiles, z_ev, w_tiles, w_evs, out_dram,
                     bias_row=None, bias_ev=None):
    p, T = sect.p, sect.T
    if not hasattr(sect, "_scr3_prev"):
        sect._scr3_prev = {"ev": None}
    prev = sect._scr3_prev
    for tc in range(cdiv(T, 128)):
        t0 = tc * 128
        for half in range(2):
            i_p, ps, rel = sect.pool.get()
            ev = None
            for dc in range(8):
                last = (dc == 7 and bias_row is None)
                ev = p.op("tensor", lambda t, dc=dc, t0=t0, half=half, ps=ps, last=last: t.matmul(
                    ps[:, :], z_tiles[dc][:, t0:t0 + 128],
                    w_tiles[dc][:, half * 512:(half + 1) * 512],
                    start=(dc == 0), stop=last),
                    waits=[z_ev, w_evs[dc]] + ([rel] if dc == 0 else []),
                    sig=last)
            if bias_row is not None:
                ev = p.op("tensor", lambda t, half=half, ps=ps: t.matmul(
                    ps[:, :], sect.ones_row[:, :],
                    bias_row[:, half * 512:(half + 1) * 512],
                    start=False, stop=True),
                    waits=[bias_ev, sect.ones_ev], sig=True)
            ev_e = p.op("scalar", lambda s, ps=ps: s.copy(
                sect.scr[3][:, :], ps[:, :]),
                waits=[ev, prev["ev"]], sig=True)
            sect.pool.release(i_p, ev_e)
            sev = sect.st.put([lambda y, t0=t0, half=half: y.dma_start(
                out=out_dram[t0:t0 + 128, half * 512:(half + 1) * 512],
                in_=sect.scr[3][:, :])], waits=[ev_e])
            sect.final_evs.append(sev)
            prev["ev"] = sev


def finish(p, sect):
    p.op("sync", lambda y: None, waits=sect.final_evs, sig=False)



def rope_tables_L0(positions):
    T = len(positions)
    dims = np.arange(DK // 2, dtype=np.float64)
    th = positions[:, None].astype(np.float64) / (10000.0 ** (2.0 * dims / DK))[None, :]
    cos = np.repeat(np.cos(th), 2, axis=-1).T
    sin = np.repeat(np.sin(th), 2, axis=-1).T
    sgn = np.tile(np.array([-1.0, 1.0]), DK // 2)[:, None]
    sinS = sin * sgn
    return (np.concatenate([cos, cos], 0).astype(np.float32),
            np.concatenate([sinS, sinS], 0).astype(np.float32))


def build_pre(T):
    p = Prog("pre")
    sect = Sect(p, T)
    x_d = p.in_("x", [D, T], F32).ap()
    wq_d = p.in_("wqt", [D, D], F32).ap()
    wk_d = p.in_("wkt", [D, D], F32).ap()
    wv_d = p.in_("wvt", [D, D], F32).ap()
    rtc_d = p.in_("rtc", [128, T], F32).ap()
    rts_d = p.in_("rts", [128, T], F32).ap()
    q_o = p.out("q", [D, T], F32).ap()
    k_o = p.out("k", [D, T], F32).ap()
    v_o = p.out("v", [T, D], F32).ap()

    x_tiles = [p.sbuf(f"x{i}", [128, T], F32) for i in range(8)]
    z_tiles = [p.sbuf(f"z{i}", [128, T], BF16) for i in range(8)]
    rtc = p.sbuf("rtc_s", [128, T], F32)
    rts = p.sbuf("rts_s", [128, T], F32)

    x_evs = emit_load_fm(sect, x_d, x_tiles)
    ev_rt = sect.act_ld.put([lambda g: g.dma_start(out=rtc[:, :], in_=rtc_d)])
    ev_rt2 = sect.act_ld.put([lambda g: g.dma_start(out=rts[:, :], in_=rts_d)])

    z_ev = emit_ln(sect, x_tiles, x_evs, z_tiles)

    wq_t, wq_e, wq_i = emit_weight_load(sect, wq_d, D, D, "wq")
    emit_qk_rope_to_hbm(sect, z_tiles, z_ev, wq_t, wq_e, rtc, rts, [ev_rt, ev_rt2], q_o)
    sect.wpool_release(wq_i, Ev(p.clock_sem["tensor"], p.clock["tensor"]))
    wk_t, wk_e, wk_i = emit_weight_load(sect, wk_d, D, D, "wk")
    emit_qk_rope_to_hbm(sect, z_tiles, z_ev, wk_t, wk_e, rtc, rts, [ev_rt, ev_rt2], k_o)
    sect.wpool_release(wk_i, Ev(p.clock_sem["tensor"], p.clock["tensor"]))
    wv_t, wv_e, wv_i = emit_weight_load(sect, wv_d, D, D, "wv")
    emit_v_tm_to_hbm(sect, z_tiles, z_ev, wv_t, wv_e, v_o)
    sect.wpool_release(wv_i, Ev(p.clock_sem["tensor"], p.clock["tensor"]))

    finish(p, sect)
    return p.build()




TRI_OFFS = (0, 128, 256, 384)


def tri_mask_np(off):
    i = np.arange(128)[:, None]
    j = np.arange(512)[None, :]
    return (j >= off + i).astype(BF)


def build_attn_L0(NP=8, T=S):
    p = Prog("attnL0")
    sect = Sect(p, T, n_pool=5)
    qf_d = p.in_("qf", [NP, 64, T], F32).ap()
    kf_d = p.in_("kf", [NP, 64, T], F32).ap()
    vt_d = p.in_("vt", [NP, T, 68], F32).ap()
    of_d = p.out("of", [NP, 64, T], F32).ap()

    NKT = T // 128
    NQC = T // 512
    scale = 1.0 / 8.0

    tri_sb = []
    tri_evs = []
    for off in TRI_OFFS:
        c = p.const(f"tri{off}", tri_mask_np(off))
        t = p.sbuf(f"tri{off}_s", [128, 512], BF16)
        tri_sb.append(t)
        tri_evs.append(sect.ld.put([lambda g, t=t, c=c: g.dma_start(out=t[:, :], in_=c.ap())]))

    qf_sb = [p.sbuf(f"qf{i}", [64, T], BF16) for i in range(2)]
    kf_sb = [p.sbuf(f"kf{i}", [64, T], BF16) for i in range(2)]
    vt_sb = [p.sbuf(f"vt{i}", [128, NKT, 68], BF16) for i in range(2)]
    pair_free = [None, None]

    a_sb = [p.sbuf(f"a{i}", [128, 512], BF16) for i in range(3)]
    a_free = [None] * 3
    o_acc = [p.psum(f"oacc{i}", [128, 512], F32) for i in range(2)]
    o_free = [None, None]
    o_sb = [p.sbuf(f"o_sb{i}", [128, 512], F32) for i in range(2)]
    bc_sb = [p.sbuf(f"bc_sb{i}", [64, 512], F32) for i in range(2)]
    nrm_row = [p.sbuf(f"nrm_row{i}", [1, 512], F32) for i in range(2)]
    ost_free = [None, None]
    bc_free = [None, None]

    for pp in range(NP):
        buf = pp % 2
        ev_q = sect.ld.put([lambda g, pp=pp, buf=buf: g.dma_start(
            out=qf_sb[buf][:, :], in_=qf_d[pp])], waits=[pair_free[buf]])
        ev_k = sect.ld.put([lambda g, pp=pp, buf=buf: g.dma_start(
            out=kf_sb[buf][:, :], in_=kf_d[pp])])
        ev_v = sect.ld.put([lambda g, pp=pp, buf=buf: g.dma_start(
            out=vt_sb[buf][:, :, :],
            in_=vt_d[pp].rearrange("(kt p) c -> p kt c", p=128))])
        last_compute = None
        for qc in range(NQC):
            q0 = qc * 512
            ob = qc % 2
            kt_hi = min(NKT, (q0 + 512) // 128)
            ev_av = None
            for kt in range(kt_hi):
                k0 = kt * 128
                i_s, ps, rel = sect.pool.get()
                ev_mm = p.op("tensor", lambda t, buf=buf, k0=k0, q0=q0, ps=ps: t.matmul(
                    ps[:, :], kf_sb[buf][:, k0:k0 + 128], qf_sb[buf][:, q0:q0 + 512],
                    start=True, stop=True),
                    waits=[ev_q, ev_k, rel], sig=True)
                ai = (qc * kt_hi + kt) % 3
                ev_exp = p.op("scalar", lambda s, ai=ai, ps=ps: s.activation(
                    a_sb[ai][:, :], ps[:, :], AF.Exp, scale=scale),
                    waits=[ev_mm, a_free[ai]], sig=True)
                sect.pool.release(i_s, ev_exp)
                if k0 >= q0:
                    ev_exp = p.op("vector", lambda v, ai=ai, off=k0 - q0: v.tensor_mul(
                        a_sb[ai][:, :], a_sb[ai][:, :], tri_sb[TRI_OFFS.index(off)][:, :]),
                        waits=[ev_exp] + tri_evs, sig=True)
                ev_av = p.op("tensor", lambda t, buf=buf, kt=kt, ai=ai, ob=ob: t.matmul(
                    o_acc[ob][0:68, :], vt_sb[buf][:, kt, :], a_sb[ai][:, :],
                    start=(kt == 0), stop=(kt == kt_hi - 1)),
                    waits=[ev_exp, ev_v] + ([o_free[ob]] if kt == 0 else []), sig=True)
                a_free[ai] = ev_av
            ev_r = p.op("vector", lambda v, ob=ob: v.reciprocal(
                nrm_row[ob][:, :], o_acc[ob][64:65, :]), waits=[ev_av], sig=True)
            i_b, ps_b, rel_b = sect.pool.get()
            ev_bc = p.op("tensor", lambda t, ob=ob, ps_b=ps_b: t.matmul(
                ps_b[0:64, :], sect.ones_row[:, 0:64], nrm_row[ob][:, :],
                start=True, stop=True), waits=[ev_r, rel_b, sect.ones_ev], sig=True)
            ev_ev = p.op("scalar", lambda s, ob=ob, ps_b=ps_b: s.copy(
                bc_sb[ob][0:64, :], ps_b[0:64, :]), waits=[ev_bc, bc_free[ob]], sig=True)
            sect.pool.release(i_b, ev_ev)
            ev_on = p.op("vector", lambda v, ob=ob: v.tensor_mul(
                o_sb[ob][0:64, :], o_acc[ob][0:64, :], bc_sb[ob][0:64, :]),
                waits=[ev_ev, ost_free[ob]], sig=True)
            bc_free[ob] = ev_on
            o_free[ob] = ev_on
            sev = sect.st.put([lambda y, pp=pp, q0=q0, ob=ob: y.dma_start(
                out=of_d[pp][:, q0:q0 + 512], in_=o_sb[ob][0:64, :])], waits=[ev_on])
            sect.final_evs.append(sev)
            ost_free[ob] = sev
            last_compute = ev_on
        pair_free[buf] = last_compute
    finish(p, sect)
    return p.build()




def build_attn_col(flavor, NP=8, T=S):
    causal = flavor == "L1"
    scale = 0.125 if flavor == "L1" else 1.0 / 32.0
    p = Prog("attn_" + flavor)
    sect = Sect(p, T, n_pool=4)
    qf_d = p.in_("qf", [NP, 64, T], F32).ap()
    kf_d = p.in_("kf", [NP, 64, T], F32).ap()
    vt_d = p.in_("vt", [NP, T, 64], F32).ap()
    of_d = p.out("of", [NP, 64, T], F32).ap()

    NKT = T // 128
    NQC = T // 512

    tri_sb = []
    tri_evs = []
    for off in TRI_OFFS:
        c = p.const(f"tri{off}", tri_mask_np(off))
        t = p.sbuf(f"tri{off}_s", [128, 512], BF16)
        tri_sb.append(t)
        tri_evs.append(sect.ld.put([lambda g, t=t, c=c: g.dma_start(out=t[:, :], in_=c.ap())]))

    qf_sb = [p.sbuf(f"qf{i}", [64, T], BF16) for i in range(2)]
    kf_sb = [p.sbuf(f"kf{i}", [64, T], BF16) for i in range(2)]
    vt_sb = [p.sbuf(f"vt{i}", [128, NKT, 64], BF16) for i in range(2)]
    pair_free = [None, None]

    a_sb = [p.sbuf(f"a{i}", [128, T], BF16) for i in range(2)]
    a_last_use = [dict(), dict()]
    rs = [p.sbuf(f"rs{i}", [128, NQC], F32) for i in range(2)]
    zr = p.sbuf("zr", [128, 1], F32)
    vs_sb = [p.sbuf(f"vs{i}", [128, 64], BF16) for i in range(2)]
    vs_free = [None, None]
    o_acc = [p.psum(f"oacc{i}", [128, 512], F32) for i in range(4)]
    o_free = [None] * 4
    o_sb = [p.sbuf(f"o_sb{i}", [128, 512], F32) for i in range(2)]
    ost_free = [[None, None], [None, None]]
    prev_z = [None, None]

    def last_kt(qc):
        return (NKT - 1) if not causal else min(NKT, (qc * 512 + 512) // 128) - 1

    for pp in range(NP):
        buf = pp % 2
        ev_q = sect.ld.put([lambda g, pp=pp, buf=buf: g.dma_start(
            out=qf_sb[buf][:, :], in_=qf_d[pp])], waits=[pair_free[buf]])
        ev_k = sect.ld.put([lambda g, pp=pp, buf=buf: g.dma_start(
            out=kf_sb[buf][:, :], in_=kf_d[pp])])
        ev_v = sect.ld.put([lambda g, pp=pp, buf=buf: g.dma_start(
            out=vt_sb[buf][:, :, :],
            in_=vt_d[pp].rearrange("(kt p) c -> p kt c", p=128))])
        last_av = None
        for kt in range(NKT):
            k0 = kt * 128
            ab = kt % 2
            zb = kt % 2
            qc_lo = (k0 // 512) if causal else 0
            a_ready = {}
            rs_evs = []
            for qc in range(qc_lo, NQC):
                q0 = qc * 512
                i_s, ps, rel = sect.pool.get()
                ev_mm = p.op("tensor", lambda t, buf=buf, k0=k0, q0=q0, ps=ps: t.matmul(
                    ps[:, :], kf_sb[buf][:, k0:k0 + 128], qf_sb[buf][:, q0:q0 + 512],
                    start=True, stop=True),
                    waits=[ev_q, ev_k, rel], sig=True)
                war = [a_last_use[ab].get(qc), prev_z[zb]]
                boundary = causal and qc == qc_lo
                if boundary:
                    ev_exp = p.op("scalar", lambda s, ab=ab, qc=qc, ps=ps: s.activation(
                        a_sb[ab][:, qc * 512:(qc + 1) * 512], ps[:, :], AF.Exp,
                        scale=scale),
                        waits=[ev_mm] + war, sig=True)
                    off = k0 - q0
                    ev_m = p.op("vector", lambda v, ab=ab, qc=qc, off=off: v.tensor_mul(
                        a_sb[ab][:, qc * 512:(qc + 1) * 512],
                        a_sb[ab][:, qc * 512:(qc + 1) * 512],
                        tri_sb[TRI_OFFS.index(off)][:, :]),
                        waits=[ev_exp] + tri_evs, sig=True)
                    ev_rs = p.op("vector", lambda v, ab=ab, qc=qc, zb=zb: v.tensor_reduce(
                        rs[zb][:, qc:qc + 1], a_sb[ab][:, qc * 512:(qc + 1) * 512],
                        mybir.AxisListType.X, ALU.add), waits=[ev_m], sig=True)
                    a_ready[qc] = ev_rs
                    rs_evs.append(ev_rs)
                    ev_done = ev_rs
                else:
                    ev_exp = p.op("scalar", lambda s, ab=ab, qc=qc, zb=zb, ps=ps: s.activation(
                        a_sb[ab][:, qc * 512:(qc + 1) * 512], ps[:, :], AF.Exp,
                        scale=scale, accum_out=rs[zb][:, qc:qc + 1]),
                        waits=[ev_mm] + war, sig=True)
                    a_ready[qc] = ev_exp
                    rs_evs.append(ev_exp)
                    ev_done = ev_exp
                sect.pool.release(i_s, ev_done)
            nqc = NQC - qc_lo
            ev_z = p.op("vector", lambda v, qc_lo=qc_lo, nqc=nqc, zb=zb: v.tensor_reduce(
                zr[:, :], rs[zb][:, qc_lo:qc_lo + nqc], mybir.AxisListType.X, ALU.add),
                waits=rs_evs, sig=True)
            prev_z[zb] = ev_z
            ev_zr = p.op("vector", lambda v: v.reciprocal(zr[:, :], zr[:, :]),
                         waits=[ev_z], sig=True)
            vb = kt % 2
            ev_vs = p.op("vector", lambda v, buf=buf, kt=kt, vb=vb: v.tensor_scalar_mul(
                vs_sb[vb][:, :], vt_sb[buf][:, kt, :], zr[:, :]),
                waits=[ev_zr, ev_v, vs_free[vb]], sig=True)
            ev_av = None
            for qc in range(qc_lo, NQC):
                bank, half = qc // 2, qc % 2
                is_first = (kt == 0)
                is_last = (kt == last_kt(qc))
                ev_av = p.op("tensor", lambda t, ab=ab, qc=qc, bank=bank, half=half,
                             vb=vb, is_first=is_first, is_last=is_last: t.matmul(
                    o_acc[bank][half * 64:(half + 1) * 64, :], vs_sb[vb][:, :],
                    a_sb[ab][:, qc * 512:(qc + 1) * 512],
                    start=is_first, stop=is_last),
                    waits=[ev_vs, a_ready[qc]] +
                          ([o_free[bank]] if is_first and half == 0 else []),
                    sig=True)
                a_last_use[ab][qc] = ev_av
                last_av = ev_av
                bank_done = is_last and (qc == 2 * bank + 1 or qc == NQC - 1 or
                                         last_kt(2 * bank + 1) == last_kt(qc) and half == 1)
                if is_last and half == 1:
                    sb_i = bank % 2
                    for hh, qq in ((0, qc - 1), (1, qc)):
                        ev_e = p.op("scalar", lambda s, bank=bank, hh=hh, sb_i=sb_i: s.copy(
                            o_sb[sb_i][hh * 64:(hh + 1) * 64, :],
                            o_acc[bank][hh * 64:(hh + 1) * 64, :]),
                            waits=[ev_av, ost_free[sb_i][hh]], sig=True)
                        sev = sect.st.put([lambda y, pp=pp, qq=qq, hh=hh, sb_i=sb_i: y.dma_start(
                            out=of_d[pp][:, qq * 512:(qq + 1) * 512],
                            in_=o_sb[sb_i][hh * 64:(hh + 1) * 64, :])], waits=[ev_e])
                        sect.final_evs.append(sev)
                        ost_free[sb_i][hh] = sev
                        o_free[bank] = ev_e
            vs_free[vb] = ev_av
        pair_free[buf] = last_av
    finish(p, sect)
    return p.build()


ALIBI_CUT = 36.0


def alibi_slope(h):
    start = 2.0 ** (-2.0 ** (-(np.log2(16) - 3)))
    return start * start ** h


def build_attn_L2(NP=8, T=S):
    p = Prog("attnL2")
    sect = Sect(p, T, n_pool=3)
    qf_d = p.in_("qf", [NP, 64, T], F32).ap()
    kf_d = p.in_("kf", [NP, 64, T], F32).ap()
    vt_d = p.in_("vt", [NP, T, 68], F32).ap()
    eb_d = p.in_("eb", [NP, 128, 64], F32).ap()
    sid_d = p.in_("sid", [NP, 128, 128], F32).ap()
    grow_d = p.in_("grow", [NP, 1, 1024], F32).ap()
    of_d = p.out("of", [NP, 64, T], F32).ap()

    NKT = T // 128
    NQC = T // 512
    scale = 0.125

    abs1_sb, abs2_sb = [], []
    abs_evs = []
    for d in range(-1, 4):
        i = np.arange(128)[None, :, None]
        j = np.arange(512)[None, None, :]
        ad = np.abs(j - d * 128 - i).astype(np.float64)[0]
        a1 = -np.minimum(ad, 256.0)
        a2 = -np.maximum(ad - 256.0, 0.0)
        c1 = p.const(f"abs1_{d+1}", a1.astype(BF))
        c2 = p.const(f"abs2_{d+1}", a2.astype(BF))
        t1 = p.sbuf(f"abs1s_{d+1}", [128, 512], BF16)
        t2 = p.sbuf(f"abs2s_{d+1}", [128, 512], BF16)
        abs1_sb.append(t1)
        abs2_sb.append(t2)
        abs_evs.append(sect.ld.put([lambda g, t=t1, c=c1: g.dma_start(out=t[:, :], in_=c.ap())]))
        abs_evs.append(sect.ld.put([lambda g, t=t2, c=c2: g.dma_start(out=t[:, :], in_=c.ap())]))

    qf_sb = [p.sbuf(f"qf{i}", [64, T], BF16) for i in range(2)]
    kf_sb = [p.sbuf(f"kf{i}", [64, T], BF16) for i in range(2)]
    vt_sb = [p.sbuf(f"vt{i}", [128, NKT, 68], BF16) for i in range(2)]
    eb_sb = [p.sbuf(f"eb{i}", [128, 64], F32) for i in range(2)]
    sid_sb = [p.sbuf(f"sid{i}", [128, 128], BF16) for i in range(2)]
    gr_sb = [p.sbuf(f"gr{i}", [1, 1024], F32) for i in range(2)]
    pair_free = [None, None]

    a_sb = [p.sbuf(f"a{i}", [128, 512], BF16) for i in range(3)]
    a_free = [None] * 3
    o_acc = [p.psum(f"oacc{i}", [128, 512], F32) for i in range(3)]
    o_free = [None, None, None]
    gb_sb = [p.sbuf(f"gb_sb{i}", [128, 512], F32) for i in range(2)]
    ga_sb = [p.sbuf(f"ga_sb{i}", [128, 512], F32) for i in range(2)]
    o_sb = [p.sbuf(f"o_sb{i}", [128, 512], F32) for i in range(2)]
    t2_sb = [p.sbuf(f"t2_sb{i}", [128, 512], F32) for i in range(2)]
    bc_sb = [p.sbuf(f"bc_sb{i}", [64, 512], F32) for i in range(2)]
    nrm_row = [p.sbuf(f"nrm{i}", [1, 512], F32) for i in range(2)]
    ost_free = [None, None]
    bc_free = [None, None]
    gbc_free = [None, None]

    def classify(slot, qc):
        q0 = qc * 512
        slope_wide = alibi_slope(min(2 * slot + 1, 15))
        out = []
        for kt in range(NKT):
            k0 = kt * 128
            if k0 + 127 < q0:
                mind = q0 - (k0 + 127)
                cls = "b"
            elif k0 > q0 + 511:
                mind = k0 - (q0 + 511)
                cls = "a"
            else:
                mind = 0
                cls = "s"
            if slope_wide * mind > ALIBI_CUT:
                continue
            out.append((kt, cls, (k0 - q0) // 128))
        return out

    for pp in range(NP):
        buf = pp % 2
        ev_q = sect.ld.put([lambda g, pp=pp, buf=buf: g.dma_start(
            out=qf_sb[buf][:, :], in_=qf_d[pp])], waits=[pair_free[buf]])
        ev_k = sect.ld.put([lambda g, pp=pp, buf=buf: g.dma_start(
            out=kf_sb[buf][:, :], in_=kf_d[pp])])
        ev_v = sect.ld.put([lambda g, pp=pp, buf=buf: g.dma_start(
            out=vt_sb[buf][:, :, :],
            in_=vt_d[pp].rearrange("(kt p) c -> p kt c", p=128))])
        ev_e = sect.ld.put([
            lambda g, pp=pp, buf=buf: g.dma_start(out=eb_sb[buf][:, :], in_=eb_d[pp]),
            lambda g, pp=pp, buf=buf: g.dma_start(out=sid_sb[buf][:, :], in_=sid_d[pp]),
            lambda g, pp=pp, buf=buf: g.dma_start(out=gr_sb[buf][:, :], in_=grow_d[pp]),
        ])
        last_on = None
        for qc in range(NQC):
            q0 = qc * 512
            qb = qc % 2
            tiles = classify(pp, qc)
            have = {c for _, c, _ in tiles}
            nt = {"b": 0, "a": 0, "s": 0}
            for _, c, _ in tiles:
                nt[c] += 1
            done = {"b": 0, "a": 0, "s": 0}
            ev_avs = {}
            for kt, cls, dist in tiles:
                k0 = kt * 128
                i_s, ps, rel = sect.pool.get()
                ev_mm = p.op("tensor", lambda t, buf=buf, k0=k0, q0=q0, ps=ps: t.matmul(
                    ps[:, :], kf_sb[buf][:, k0:k0 + 128], qf_sb[buf][:, q0:q0 + 512],
                    start=True, stop=(cls != "s")),
                    waits=[ev_q, ev_k, rel], sig=(cls != "s"))
                if cls == "s":
                    di = dist + 1
                    p.op("tensor", lambda t, buf=buf, di=di, ps=ps: t.matmul(
                        ps[:, :], sid_sb[buf][:, :], abs1_sb[di][:, :],
                        start=False, stop=False), waits=[ev_e] + abs_evs)
                    ev_mm = p.op("tensor", lambda t, buf=buf, di=di, ps=ps: t.matmul(
                        ps[:, :], sid_sb[buf][:, :], abs2_sb[di][:, :],
                        start=False, stop=True), waits=(), sig=True)
                ai = sum(done.values()) % 3
                if cls == "s":
                    ev_exp = p.op("scalar", lambda s, ai=ai, ps=ps: s.activation(
                        a_sb[ai][:, :], ps[:, :], AF.Exp, scale=scale),
                        waits=[ev_mm, a_free[ai]], sig=True)
                else:
                    dcol = dist + 32
                    ev_exp = p.op("scalar", lambda s, ai=ai, buf=buf, dcol=dcol, ps=ps: s.activation(
                        a_sb[ai][:, :], ps[:, :], AF.Exp, scale=scale,
                        bias=eb_sb[buf][:, dcol:dcol + 1]),
                        waits=[ev_mm, a_free[ai], ev_e], sig=True)
                sect.pool.release(i_s, ev_exp)
                ob = {"b": 0, "a": 1, "s": 2}[cls]
                first = done[cls] == 0
                last = done[cls] == nt[cls] - 1
                ev_av = p.op("tensor", lambda t, buf=buf, kt=kt, ai=ai, ob=ob,
                             first=first, last=last: t.matmul(
                    o_acc[ob][0:68, :], vt_sb[buf][:, kt, :], a_sb[ai][:, :],
                    start=first, stop=last),
                    waits=[ev_exp, ev_v] + ([o_free[ob]] if first else []),
                    sig=True)
                a_free[ai] = ev_av
                done[cls] += 1
                ev_avs[ob] = ev_av
            cevs = []
            if "b" in have or "a" in have:
                i_g, ps_g, rel_g = sect.pool.get()
                ev_g1 = p.op("tensor", lambda t, buf=buf, ps=ps_g: t.matmul(
                    ps[0:68, :], sect.ones_row[:, 0:68], gr_sb[buf][:, 0:512],
                    start=True, stop=True), waits=[ev_e, sect.ones_ev, rel_g], sig=True)
                ev_ge = p.op("scalar", lambda s, qb=qb, ps=ps_g: s.copy(
                    gb_sb[qb][0:68, :], ps[0:68, :]), waits=[ev_g1, gbc_free[qb]], sig=True)
                ev_g2 = p.op("tensor", lambda t, buf=buf, ps=ps_g: t.matmul(
                    ps[0:68, :], sect.ones_row[:, 0:68], gr_sb[buf][:, 512:1024],
                    start=True, stop=True), waits=[ev_ge], sig=True)
                ev_ge2 = p.op("scalar", lambda s, qb=qb, ps=ps_g: s.copy(
                    ga_sb[qb][0:68, :], ps[0:68, :]), waits=[ev_g2], sig=True)
                sect.pool.release(i_g, ev_ge2)
            seqs = []
            if "b" in have:
                seqs.append(lambda v, qb=qb: v.tensor_tensor(
                    o_sb[qb][0:68, :], o_acc[0][0:68, :], gb_sb[qb][0:68, :], ALU.mult))
                if "a" in have:
                    seqs.append(lambda v, qb=qb: v.tensor_tensor(
                        t2_sb[qb][0:68, :], o_acc[1][0:68, :], ga_sb[qb][0:68, :], ALU.mult))
                    seqs.append(lambda v, qb=qb: v.tensor_tensor(
                        o_sb[qb][0:68, :], o_sb[qb][0:68, :], t2_sb[qb][0:68, :], ALU.add))
            elif "a" in have:
                seqs.append(lambda v, qb=qb: v.tensor_tensor(
                    o_sb[qb][0:68, :], o_acc[1][0:68, :], ga_sb[qb][0:68, :], ALU.mult))
            if "s" in have:
                if "b" in have or "a" in have:
                    seqs.append(lambda v, qb=qb: v.tensor_tensor(
                        o_sb[qb][0:68, :], o_sb[qb][0:68, :], o_acc[2][0:68, :], ALU.add))
                else:
                    seqs.append(lambda v, qb=qb: v.tensor_copy(
                        o_sb[qb][0:68, :], o_acc[2][0:68, :]))
            wl = [ev_avs.get(0), ev_avs.get(1), ev_avs.get(2), ost_free[qb]]
            if "b" in have or "a" in have:
                wl.append(ev_ge2)
            ev_comb = p.seq("vector", seqs, waits=wl)
            for ob in (0, 1, 2):
                if ob in ev_avs:
                    o_free[ob] = ev_comb
            if "b" in have or "a" in have:
                gbc_free[qb] = ev_comb
            ev_r = p.op("vector", lambda v, qb=qb: v.reciprocal(
                nrm_row[qb][:, :], o_sb[qb][64:65, :]), waits=[ev_comb], sig=True)
            i_b2, ps_b2, rel_b2 = sect.pool.get()
            ev_bc = p.op("tensor", lambda t, qb=qb, ps=ps_b2: t.matmul(
                ps[0:64, :], sect.ones_row[:, 0:64], nrm_row[qb][:, :],
                start=True, stop=True), waits=[ev_r, rel_b2], sig=True)
            ev_ev = p.op("scalar", lambda s, qb=qb, ps=ps_b2: s.copy(
                bc_sb[qb][0:64, :], ps[0:64, :]), waits=[ev_bc, bc_free[qb]], sig=True)
            sect.pool.release(i_b2, ev_ev)
            ev_on = p.op("vector", lambda v, qb=qb: v.tensor_mul(
                o_sb[qb][0:64, :], o_sb[qb][0:64, :], bc_sb[qb][0:64, :]),
                waits=[ev_ev], sig=True)
            bc_free[qb] = ev_on
            sev = sect.st.put([lambda y, pp=pp, q0=q0, qb=qb: y.dma_start(
                out=of_d[pp][:, q0:q0 + 512], in_=o_sb[qb][0:64, :])], waits=[ev_on])
            sect.final_evs.append(sev)
            ost_free[qb] = sev
            last_on = sev
        pair_free[buf] = last_on
    finish(p, sect)
    return p.build()


L3A_LD = 4608
L3A_JOFF = 4095


def build_attn_L3a(NP=8, T=S, nobias=False, zerobias=False):
    p = Prog("attnL3a")
    sect = Sect(p, T, n_pool=4)
    qf_d = p.in_("qf", [NP, 64, T], F32).ap()
    kf_d = p.in_("kf", [NP, 64, T], F32).ap()
    vt_d = p.in_("vt", [NP, T, 68], F32).ap()
    relr_d = p.in_("relr", [NP, 64, L3A_LD], F32).ap()
    of_d = p.out("of", [NP, 64, T], F32).ap()
    tbuf = [p.internal(f"t3a_{i}", [T * L3A_LD], BF16) for i in range(2)]

    NKT = T // 128
    NQC = T // 512
    scale = 0.125

    tri_sb = []
    tri_evs = []
    for off in TRI_OFFS:
        c = p.const(f"tri{off}", tri_mask_np(off))
        t = p.sbuf(f"tri{off}_s", [128, 512], BF16)
        tri_sb.append(t)
        tri_evs.append(sect.ld.put([lambda g, t=t, c=c: g.dma_start(out=t[:, :], in_=c.ap())]))
    idn, id_ev = sect.ident()
    if zerobias:
        zc = p.const("zeros128", np.zeros((128, 128), BF))
        ztile = p.sbuf("ztile", [128, 128], BF16)
        zev = sect.ld.put([lambda g: g.dma_start(out=ztile[:, :], in_=zc.ap())])

    qf_sb = [p.sbuf(f"qf{i}", [64, T], BF16) for i in range(2)]
    kf_sb = [p.sbuf(f"kf{i}", [64, T], BF16) for i in range(2)]
    vt_sb = [p.sbuf(f"vt{i}", [128, NKT, 68], BF16) for i in range(2)]
    rel_sb = [p.sbuf(f"rel{i}", [64, L3A_LD], BF16) for i in range(2)]
    pair_free = [None, None]

    tw_sb = [p.sbuf(f"tw{i}", [128, 512], BF16) for i in range(2)]
    bias_sb = [p.sbuf(f"bias{i}", [128, 4, 2048], BF16) for i in range(2)]
    bias_free = [None, None]
    a_sb = [p.sbuf(f"a{i}", [128, 512], BF16) for i in range(3)]
    a_free = [None] * 3
    o_acc = [p.psum(f"oacc{i}", [128, 512], F32) for i in range(2)]
    o_free = [None, None]
    o_sb = [p.sbuf(f"o_sb{i}", [128, 512], F32) for i in range(2)]
    bc_sb = [p.sbuf(f"bc_sb{i}", [64, 512], F32) for i in range(2)]
    nrm_row = [p.sbuf(f"nrm{i}", [1, 512], F32) for i in range(2)]
    ost_free = [None, None]
    bc_free = [None, None]
    tread_done = [None, None]
    tread_done2 = [None, None]
    twr = Stream(p, "sync", "twr", 4)
    brd = Stream(p, "sync", "brd", 2)

    for pp in range(NP):
        buf = pp % 2
        tb = tbuf[pp % 2]
        ev_q = sect.ld.put([lambda g, pp=pp, buf=buf: g.dma_start(
            out=qf_sb[buf][:, :], in_=qf_d[pp])], waits=[pair_free[buf]])
        ev_k = sect.ld.put([lambda g, pp=pp, buf=buf: g.dma_start(
            out=kf_sb[buf][:, :], in_=kf_d[pp])])
        ev_v = sect.ld.put([lambda g, pp=pp, buf=buf: g.dma_start(
            out=vt_sb[buf][:, :, :],
            in_=vt_d[pp].rearrange("(kt p) c -> p kt c", p=128))])
        ev_rel = sect.ld.put([lambda g, pp=pp, buf=buf: g.dma_start(
            out=rel_sb[buf][:, :], in_=relr_d[pp])])
        tw_evs = []
        ti_free = [None, None]
        for qt in range(NKT):
            q0 = qt * 128
            jc_lo = max(0, (L3A_JOFF - q0 - 127) // 512)
            for jc in range(jc_lo, L3A_LD // 512):
                c0 = jc * 512
                i_p, ps, rel = sect.pool.get()
                ev_mm = p.op("tensor", lambda t, buf=buf, q0=q0, c0=c0, ps=ps: t.matmul(
                    ps[:, :], qf_sb[buf][:, q0:q0 + 128], rel_sb[buf][:, c0:c0 + 512],
                    start=True, stop=True),
                    waits=[ev_q, ev_rel, rel, tread_done[pp % 2],
                           tread_done2[pp % 2]], sig=True)
                wi = (qt * 9 + jc) % 2
                ev_e = p.op("scalar", lambda s, wi=wi, ps=ps: s.copy(
                    tw_sb[wi][:, :], ps[:, :]), waits=[ev_mm, ti_free[wi]], sig=True)
                sect.pool.release(i_p, ev_e)
                sev = twr.put([lambda y, q0=q0, c0=c0, wi=wi, tb=tb: y.dma_start(
                    out=bass.AP(tensor=tb.ap().tensor, offset=q0 * L3A_LD + c0,
                                ap=[[L3A_LD, 128], [1, 512]]),
                    in_=tw_sb[wi][:, :])], waits=[ev_e])
                ti_free[wi] = sev
                tw_evs.append(sev)
        tw_done = twr.drain_evs()
        last_on = None
        last_bias_read = None
        for qc in range(NQC):
            q0 = qc * 512
            ob = qc % 2
            kt_hi = min(NKT, (q0 + 512) // 128)
            khn = cdiv(kt_hi * 128, 2048)
            br_evs = []
            for kh in range(khn):
                kbase = kh * 2048
                kn = min(2048, kt_hi * 128 - kbase)
                def bld(y, q0=q0, kbase=kbase, kn=kn, kh=kh, tb=tb):
                    off = (L3A_LD - 1) * q0 + L3A_JOFF + kbase - 0
                    return y.dma_start(
                        out=bias_sb[kh][:, :, 0:kn],
                        in_=bass.AP(tensor=tb.ap().tensor, offset=off,
                                    ap=[[L3A_LD - 1, 128],
                                        [(L3A_LD - 1) * 128, 4], [1, kn]]))
                br_evs.append(brd.put([bld], waits=tw_done + [bias_free[kh]]))
            last_bias_read = br_evs[-1]
            ev_av = None
            for kt in range(kt_hi):
                k0 = kt * 128
                kh = k0 // 2048
                kloc = k0 % 2048
                i_s, ps, rel = sect.pool.get()
                ev_mm = p.op("tensor", lambda t, buf=buf, k0=k0, q0=q0, ps=ps, nobias=nobias: t.matmul(
                    ps[:, :], kf_sb[buf][:, k0:k0 + 128], qf_sb[buf][:, q0:q0 + 512],
                    start=True, stop=nobias),
                    waits=[ev_q, ev_k, rel], sig=nobias)
                ev_b = None
                subs = [] if nobias else list(range(4))
                for sub in subs:
                    qs_global = qc * 4 + sub
                    if qs_global * 128 + 127 < k0:
                        continue
                    lastb = (sub == 3) or all(
                        (qc * 4 + s2) * 128 + 127 < k0 for s2 in range(sub + 1, 4))
                    if zerobias:
                        ev_b = p.op("tensor", lambda t, kh=kh, kloc=kloc, sub=sub,
                                    lastb=lastb, ps=ps: t.matmul(
                            ps[:, sub * 128:(sub + 1) * 128],
                            ztile[:, :],
                            idn[:, :], start=False, stop=lastb),
                            waits=[br_evs[kh], id_ev, zev], sig=lastb)
                    else:
                        ev_b = p.op("tensor", lambda t, kh=kh, kloc=kloc, sub=sub,
                                    qs_global=qs_global, lastb=lastb, ps=ps: t.matmul(
                            ps[:, sub * 128:(sub + 1) * 128],
                            bias_sb[kh][:, qs_global % 4, kloc:kloc + 128],
                            idn[:, :], start=False, stop=lastb),
                            waits=[br_evs[kh], id_ev], sig=lastb)
                if not nobias:
                    ev_mm = ev_b
                ai = (qc * kt_hi + kt) % 3
                ev_exp = p.op("scalar", lambda s, ai=ai, ps=ps: s.activation(
                    a_sb[ai][:, :], ps[:, :], AF.Exp, scale=scale),
                    waits=[ev_mm, a_free[ai]], sig=True)
                sect.pool.release(i_s, ev_exp)
                if k0 >= q0:
                    ev_exp = p.op("vector", lambda v, ai=ai, off=k0 - q0: v.tensor_mul(
                        a_sb[ai][:, :], a_sb[ai][:, :],
                        tri_sb[TRI_OFFS.index(off)][:, :]),
                        waits=[ev_exp] + tri_evs, sig=True)
                ev_av = p.op("tensor", lambda t, buf=buf, kt=kt, ai=ai, ob=ob,
                             kt_hi=kt_hi: t.matmul(
                    o_acc[ob][0:68, :], vt_sb[buf][:, kt, :], a_sb[ai][:, :],
                    start=(kt == 0), stop=(kt == kt_hi - 1)),
                    waits=[ev_exp, ev_v] + ([o_free[ob]] if kt == 0 else []),
                    sig=True)
                a_free[ai] = ev_av
            for kh in range(khn):
                bias_free[kh] = ev_av
            ev_r = p.op("vector", lambda v, ob=ob: v.reciprocal(
                nrm_row[ob][:, :], o_acc[ob][64:65, :]), waits=[ev_av], sig=True)
            i_b, ps_b, rel_b = sect.pool.get()
            ev_bc = p.op("tensor", lambda t, ob=ob, ps_b=ps_b: t.matmul(
                ps_b[0:64, :], sect.ones_row[:, 0:64], nrm_row[ob][:, :],
                start=True, stop=True), waits=[ev_r, rel_b, sect.ones_ev], sig=True)
            ev_ev = p.op("scalar", lambda s, ob=ob, ps_b=ps_b: s.copy(
                bc_sb[ob][0:64, :], ps_b[0:64, :]), waits=[ev_bc, bc_free[ob]], sig=True)
            sect.pool.release(i_b, ev_ev)
            ev_on = p.op("vector", lambda v, ob=ob: v.tensor_mul(
                o_sb[ob][0:64, :], o_acc[ob][0:64, :], bc_sb[ob][0:64, :]),
                waits=[ev_ev, ost_free[ob]], sig=True)
            bc_free[ob] = ev_on
            o_free[ob] = ev_on
            sev = sect.st.put([lambda y, pp=pp, q0=q0, ob=ob: y.dma_start(
                out=of_d[pp][:, q0:q0 + 512], in_=o_sb[ob][0:64, :])], waits=[ev_on])
            sect.final_evs.append(sev)
            ost_free[ob] = sev
            last_on = sev
        _devs = brd.drain_evs()
        tread_done[pp % 2] = _devs[0] if _devs else None
        tread_done2[pp % 2] = _devs[1] if len(_devs) > 1 else None
        pair_free[buf] = last_on
    finish(p, sect)
    return p.build()



def emit_out_proj_residual(sect, o_tiles, o_ev, w_tiles, w_evs, x_tiles, x_evs,
                           bias_col=None, bias_ev=None):
    p = sect.p
    evs = [None] * 8

    def cb(oc, qc, q0, qn, i_p, ps, ev_mm):
        if bias_col is not None:
            def f(v, oc=oc, q0=q0, qn=qn, ps=ps):
                return v.scalar_tensor_tensor(
                    x_tiles[oc][:, q0:q0 + qn], ps[:, :qn], bias_col[:, oc:oc + 1],
                    x_tiles[oc][:, q0:q0 + qn], ALU.add, ALU.add)
            w = [ev_mm, x_evs[oc], bias_ev]
        else:
            def f(v, oc=oc, q0=q0, qn=qn, ps=ps):
                return v.tensor_tensor(
                    x_tiles[oc][:, q0:q0 + qn], ps[:, :qn],
                    x_tiles[oc][:, q0:q0 + qn], ALU.add)
            w = [ev_mm, x_evs[oc]]
        ev = p.op("vector", f, waits=w, sig=True)
        sect.pool.release(i_p, ev)
        evs[oc] = ev

    emit_proj_fm(sect, o_tiles, o_ev, w_tiles, w_evs, cb)
    return evs


def emit_store_fm(sect, tiles, evs, dram, nrows=8):
    out_evs = []
    for dc in range(nrows):
        sev = sect.st.put([lambda y, dc=dc: y.dma_start(
            out=dram[dc * 128:(dc + 1) * 128, :], in_=tiles[dc][:, :])],
            waits=[evs[dc] if isinstance(evs, list) else evs])
        sect.final_evs.append(sev)
        out_evs.append(sev)
    return out_evs


def emit_sign_alpha_prep(sect, w_dram, rows, cols, name):
    p = sect.p
    u = p.uid()
    nrc = rows // 128
    ncb = cols // 1024
    swt = p.internal(f"swt_{name}", [cols, rows], BF16).ap()
    alpha = p.sbuf(f"alpha{u}", [128, nrc], F32)
    acol = p.sbuf(f"acol{u}", [128, 1], F32)
    pb = sect.prep_bufs()
    wtile, stile, ttile = pb["w"], pb["s"], pb["t"]
    ps_t = sect.tpsum()
    idn, id_ev = sect.ident()
    free_w = pb["fw"]
    free_s = pb["fs"]
    ps_free = sect._tpsum_free
    tt_free = pb["ft"]
    last_store = None
    a_evs = []
    bi_ctr = 0
    for rc in range(nrc):
        for cb in range(ncb):
            bi = bi_ctr % 2
            bi_ctr += 1
            ev_l = sect.ld.put([lambda g, rc=rc, cb=cb, bi=bi: g.dma_start(
                out=wtile[bi][:, :],
                in_=w_dram[rc * 128:(rc + 1) * 128, cb * 1024:(cb + 1) * 1024])],
                waits=[free_w[bi]])
            if cb == 0:
                ev_a = p.op("vector", lambda v, rc=rc, bi=bi: v.tensor_reduce(
                    alpha[:, rc:rc + 1], wtile[bi][:, :], mybir.AxisListType.X,
                    ALU.add, apply_absolute_value=True), waits=[ev_l], sig=True)
            else:
                ev_a0 = p.op("vector", lambda v, bi=bi: v.tensor_reduce(
                    acol[:, :], wtile[bi][:, :], mybir.AxisListType.X,
                    ALU.add, apply_absolute_value=True), waits=[ev_l], sig=True)
                ev_a = p.op("vector", lambda v, rc=rc: v.tensor_add(
                    alpha[:, rc:rc + 1], alpha[:, rc:rc + 1], acol[:, :]),
                    waits=[ev_a0], sig=True)
            a_evs.append(ev_a)
            ev_s = p.op("scalar", lambda s, bi=bi: s.activation(
                stile[bi][:, :], wtile[bi][:, :], AF.Sign),
                waits=[ev_l, free_s[bi]], sig=True)
            free_w[bi] = Ev(ev_s.sem, ev_s.val)
            for dc in range(8):
                ti = dc % 2
                ev_tr = p.op("tensor", lambda t, dc=dc, bi=bi, ti=ti: t.transpose(
                    ps_t[ti][:, 0:128],
                    stile[bi][:, dc * 128:(dc + 1) * 128], idn[:, :]),
                    waits=[ev_s, id_ev, ps_free[ti]], sig=True)
                ev_e = p.op("scalar", lambda s, ti=ti: s.copy(
                    ttile[ti][:, :], ps_t[ti][:, 0:128]),
                    waits=[ev_tr, tt_free[ti]], sig=True)
                ps_free[ti] = ev_e
                sev = sect.st.put([lambda y, rc=rc, cb=cb, dc=dc, ti=ti: y.dma_start(
                    out=swt[cb * 1024 + dc * 128:cb * 1024 + (dc + 1) * 128,
                            rc * 128:(rc + 1) * 128],
                    in_=ttile[ti][:, :])], waits=[ev_e])
                tt_free[ti] = sev
                last_store = sev
            free_s[bi] = last_store
    ev_am = p.op("vector", lambda v: v.tensor_scalar_mul(
        alpha[:, :], alpha[:, :], 1.0 / cols), waits=a_evs, sig=True)
    return swt, last_store, alpha, ev_am


def emit_ffn_l0(sect, z_tiles, z_ev, x_tiles, x_evs,
                sw1t, sw1_ev, a1, a1_ev, b1col, b1_ev,
                sw2t, sw2_ev, a2, a2_ev, b2col, b2_ev):
    p, T = sect.p, sect.T
    u = p.uid()
    nsl = max(1, T // 512)
    h2 = []
    for i in range(32):
        ar, off = i // nsl, i % nsl
        if ar < len(sect.h2_arena):
            h2.append(sect.h2_arena[ar][:, off * 512:(off + 1) * 512])
        else:
            h2.append(p.sbuf(f"h2x{u}_{i}", [128, 512], BF16))
    w1buf = [p.sbuf(f"w1b{u}_{i}", [128, 8, 128], BF16) for i in range(3)]
    w2buf = [p.sbuf(f"w2b{u}_{i}", [128, 32, 128], BF16) for i in range(1)]
    f_w1 = [None] * 3
    f_w2 = [None] * 1
    f_h2 = [None] * 32
    x_out = [None] * 8
    for tc in range(cdiv(T, 512)):
        t0 = tc * 512
        h_evs = [None] * 32
        for hc in range(32):
            wi = hc % 3
            ev_w = sect.ld.put([lambda g, hc=hc, wi=wi: g.dma_start(
                out=w1buf[wi][:, :, :],
                in_=sw1t[:, hc * 128:(hc + 1) * 128].rearrange(
                    "(dc p) h -> p dc h", p=128))], waits=[f_w1[wi], sw1_ev])
            i_p, ps, rel = sect.pool.get()
            ev = None
            for dc in range(8):
                ev = p.op("tensor", lambda t, dc=dc, wi=wi, t0=t0, ps=ps: t.matmul(
                    ps[:, :], w1buf[wi][:, dc, :], z_tiles[dc][:, t0:t0 + 512],
                    start=(dc == 0), stop=(dc == 7)),
                    waits=[ev_w, z_ev] + ([rel] if dc == 0 else []),
                    sig=(dc == 7))
            f_w1[wi] = ev
            ev_g = p.op("scalar", lambda s, hc=hc, ps=ps: s.activation(
                h2[hc][:, :], ps[:, :], AF.Gelu,
                bias=b1col[:, hc:hc + 1], scale=a1[:, hc:hc + 1]),
                waits=[ev, a1_ev, b1_ev, f_h2[hc]], sig=True)
            sect.pool.release(i_p, ev_g)
            h_evs[hc] = ev_g
        for oc in range(8):
            wi = 0
            ev_w = sect.ld.put([lambda g, oc=oc, wi=wi: g.dma_start(
                out=w2buf[wi][:, :, :],
                in_=sw2t[:, oc * 128:(oc + 1) * 128].rearrange(
                    "(hc p) d -> p hc d", p=128))], waits=[f_w2[wi], sw2_ev])
            i_p, ps, rel = sect.pool.get()
            ev = None
            for hc in range(32):
                ev = p.op("tensor", lambda t, hc=hc, wi=wi, oc=oc, ps=ps: t.matmul(
                    ps[:, :], w2buf[wi][:, hc, :], h2[hc][:, :],
                    start=(hc == 0), stop=(hc == 31)),
                    waits=[ev_w, h_evs[hc]] + ([rel] if hc == 0 else []),
                    sig=(hc == 31))
            f_w2[wi] = ev
            ev_1 = p.op("vector", lambda v, oc=oc, ps=ps: v.tensor_scalar(
                sect.scr[1][:, :], ps[:, :], a2[:, oc:oc + 1], b2col[:, oc:oc + 1],
                ALU.mult, ALU.add), waits=[ev, a2_ev, b2_ev], sig=True)
            ev_2 = p.op("vector", lambda v, oc=oc, t0=t0: v.tensor_tensor(
                x_tiles[oc][:, t0:t0 + 512], sect.scr[1][:, :],
                x_tiles[oc][:, t0:t0 + 512], ALU.add),
                waits=[ev_1, x_evs[oc]], sig=True)
            sect.pool.release(i_p, ev_2)
            x_out[oc] = ev_2
        for hc in range(32):
            f_h2[hc] = x_out[7]
    return x_out


def emit_qk_plain(sect, z_tiles, z_ev, w_tiles, w_evs, out_dram,
                  bias_col=None, bias_ev=None, keep_bf16=None):
    p = sect.p
    outb = sect.scr[2]
    if not hasattr(sect, "_qk_prev"):
        sect._qk_prev = {"ev": None}
    prev = sect._qk_prev

    def cb(oc, qc, q0, qn, i_p, ps, ev_mm):
        if bias_col is not None:
            ev = p.op("scalar", lambda s, oc=oc, qn=qn, ps=ps: s.activation(
                outb[:, :qn], ps[:, :qn], AF.Identity, bias=bias_col[:, oc:oc + 1]),
                waits=[ev_mm, bias_ev, prev["ev"]], sig=True)
        else:
            ev = p.op("scalar", lambda s, qn=qn, ps=ps: s.copy(
                outb[:, :qn], ps[:, :qn]), waits=[ev_mm, prev["ev"]], sig=True)
        sect.pool.release(i_p, ev)
        sev = sect.st.put([lambda y, oc=oc, q0=q0, qn=qn: y.dma_start(
            out=out_dram[oc * 128:(oc + 1) * 128, q0:q0 + qn], in_=outb[:, :qn])],
            waits=[ev])
        sect.final_evs.append(sev)
        prev["ev"] = sev

    emit_proj_fm(sect, z_tiles, z_ev, w_tiles, w_evs, cb)


def emit_gating(sect, x_tiles, x1_evs, mean_row, rstd_row, stats_ev,
                gate_sb, gate_evl, flavor, gb_row=None, gb_ev=None):
    p, T = sect.p, sect.T
    NTT = T // 128
    l2 = flavor == "L2"
    comb_tm = p.sbuf(f"compb{p.uid()}", [128, NTT, 9], F32)
    lg = p.sbuf(f"lg{p.uid()}", [9, T], F32)
    comb_fm = lg[:, :].bitcast(BF16)[:, 0:T] if l2 else None
    comb_bf = p.sbuf(f"combbf{p.uid()}", [128, 16], BF16) if l2 else None
    cs_col = p.sbuf(f"cscol{p.uid()}", [9, 1], F32)
    idn32, id32_ev = sect.ident32()
    idn, id_ev = sect.ident()
    ps_t = sect.tpsum()
    ps_free = sect._tpsum_free

    i_c, ps_c, rel_c = sect.pool.get()
    ev = None
    for dc in range(8):
        ev = p.op("tensor", lambda t, dc=dc, ps=ps_c: t.matmul(
            ps[0:9, 0:1], gate_sb[dc][:, :], sect.ones_col[:, :],
            start=(dc == 0), stop=(dc == 7)),
            waits=list(gate_evl) + [sect.ones_col_ev] + ([rel_c] if dc == 0 else []),
            sig=(dc == 7))
    ev_cs = p.op("vector", lambda v, ps=ps_c: v.tensor_copy(
        cs_col[:, :], ps[0:9, 0:1]), waits=[ev], sig=True)
    sect.pool.release(i_c, ev_cs)

    lg_evs = []
    for qc in range(sect.NQC):
        q0 = qc * 512
        qn = min(512, T - q0)
        i_a, ps_a, rel_a = sect.pool.get()
        ev = None
        for dc in range(8):
            ev = p.op("tensor", lambda t, dc=dc, q0=q0, qn=qn, ps=ps_a: t.matmul(
                ps[0:9, :qn], gate_sb[dc][:, :], x_tiles[dc][:, q0:q0 + qn],
                start=(dc == 0), stop=(dc == 7)),
                waits=[x1_evs[dc]] + list(gate_evl) + ([rel_a] if dc == 0 else []),
                sig=(dc == 7))
        i_m, ps_m, rel_m = sect.pool.get()
        ev_bm = p.op("tensor", lambda t, q0=q0, qn=qn, ps=ps_m: t.matmul(
            ps[0:9, :qn], sect.ones_row[:, 0:9], mean_row[:, q0:q0 + qn],
            start=True, stop=True), waits=[stats_ev, sect.ones_ev, rel_m], sig=True)
        i_r, ps_r, rel_r = sect.pool.get()
        ev_br = p.op("tensor", lambda t, q0=q0, qn=qn, ps=ps_r: t.matmul(
            ps[0:9, :qn], sect.ones_row[:, 0:9], rstd_row[:, q0:q0 + qn],
            start=True, stop=True), waits=[stats_ev, rel_r], sig=True)

        ev_1 = p.op("vector", lambda v, qn=qn, ps_m=ps_m: v.tensor_scalar(
            sect.scr[2][0:9, :qn], ps_m[0:9, :qn], cs_col[:, :], None, ALU.mult),
            waits=[ev_bm, ev_cs], sig=True)
        ev_2 = p.op("vector", lambda v, qn=qn, ps_a=ps_a: v.tensor_tensor(
            sect.scr[3][0:9, :qn], ps_a[0:9, :qn], sect.scr[2][0:9, :qn],
            ALU.subtract), waits=[ev_1, ev], sig=True)
        ev_3 = p.op("vector", lambda v, q0=q0, qn=qn, ps_r=ps_r: v.tensor_tensor(
            lg[:, q0:q0 + qn], sect.scr[3][0:9, :qn], ps_r[0:9, :qn], ALU.mult),
            waits=[ev_2, ev_br], sig=True)
        sect.pool.release(i_a, ev_2)
        sect.pool.release(i_m, ev_1)
        sect.pool.release(i_r, ev_3)
        lg_evs.append(ev_3)
    lg_ev = lg_evs[-1]

    gbase = None
    ev_gbe = None
    if l2:
        i_b, ps_b, rel_b = sect.pool.get()
        ev_gb = p.op("tensor", lambda t, ps_b=ps_b: t.matmul(
            ps_b[:, 0:8], sect.ones_row[:, :], gb_row[:, :],
            start=True, stop=True), waits=[gb_ev, sect.ones_ev, rel_b], sig=True)
        gbase = p.sbuf(f"gbb{p.uid()}", [128, 8], F32)
        ev_gbe = p.op("scalar", lambda s, ps_b=ps_b: s.copy(
            gbase[:, :], ps_b[:, 0:8]), waits=[ev_gb], sig=True)
        sect.pool.release(i_b, ev_gbe)

    comb_evs = []
    fm_evs = []
    prev_gt = None
    for tt in range(NTT):
        t0 = tt * 128
        i_t, ps_tt, rel_t = sect.pool.get()
        ev_tr = p.op("tensor", lambda t, t0=t0, ps=ps_tt: t.transpose(
            ps[:, 0:9], lg[:, t0:t0 + 128], idn32[0:9, 0:9]),
            waits=[lg_ev, id32_ev, rel_t], sig=True)
        ev_g = p.op("scalar", lambda s, ps=ps_tt: s.copy(
            sect.scr[2][:, 0:9], ps[:, 0:9]),
            waits=[ev_tr, prev_gt], sig=True)
        sect.pool.release(i_t, ev_g)
        ev_x = p.op("scalar", lambda s: s.activation(
            sect.scr[2][:, 16:24], sect.scr[2][:, 0:8], AF.Exp), waits=[ev_g], sig=True)
        if flavor == "L1":
            ev_sg = p.op("scalar", lambda s, tt=tt: s.activation(
                comb_tm[:, tt, 8:9], sect.scr[2][:, 8:9], AF.Sigmoid),
                waits=[ev_g], sig=True)
        else:
            ev_sg = p.op("vector", lambda v, tt=tt: v.memset(
                comb_tm[:, tt, 8:9], 1.0), waits=[ev_g], sig=True)

        s2 = sect.scr[2]
        ev_p = p.seq("vector", [
            lambda v: v.tensor_reduce(s2[:, 24:25], s2[:, 16:24], mybir.AxisListType.X, ALU.add),
            lambda v: v.reciprocal(s2[:, 25:26], s2[:, 24:25]),
            lambda v: v.tensor_scalar_mul(s2[:, 32:40], s2[:, 16:24], s2[:, 25:26]),
            lambda v: v.tensor_copy(s2[:, 48:56], s2[:, 32:40]),
        ], waits=[ev_x])
        if l2:
            ev_p = p.op("vector", lambda v: v.tensor_add(
                sect.scr[2][:, 48:56], sect.scr[2][:, 32:40], gbase[:, :]),
                waits=[ev_p, ev_gbe], sig=True)

        ev_t = p.seq("vector", [
            lambda v: v.tensor_reduce(s2[:, 26:27], s2[:, 48:56], mybir.AxisListType.X, ALU.max),
            lambda v: v.tensor_scalar(s2[:, 56:64], s2[:, 48:56], s2[:, 26:27], None, ALU.is_ge),
            lambda v: v.scalar_tensor_tensor(s2[:, 64:72], s2[:, 56:64], -1e30,
                                             s2[:, 48:56], ALU.mult, ALU.add),
            lambda v: v.tensor_reduce(s2[:, 27:28], s2[:, 64:72], mybir.AxisListType.X, ALU.max),
            lambda v: v.tensor_scalar(s2[:, 72:80], s2[:, 48:56], s2[:, 27:28], None, ALU.is_ge),
            lambda v: v.tensor_mul(s2[:, 80:88], s2[:, 32:40], s2[:, 72:80]),
        ], waits=[ev_p])
        if flavor == "L1":
            ev_c = p.seq("vector", [
                lambda v: v.tensor_reduce(s2[:, 28:29], s2[:, 80:88], mybir.AxisListType.X, ALU.add),
                lambda v: v.reciprocal(s2[:, 29:30], s2[:, 28:29]),
                lambda v, tt=tt: v.tensor_scalar_mul(comb_tm[:, tt, 0:8], s2[:, 80:88],
                                                     s2[:, 29:30]),
            ], waits=[ev_t])
        else:
            ev_c = p.op("vector", lambda v, tt=tt: v.tensor_copy(
                comb_tm[:, tt, 0:8], sect.scr[2][:, 80:88]), waits=[ev_t], sig=True)
        prev_gt = ev_c
        comb_evs.append(ev_c)
        comb_evs.append(ev_sg)
        if l2:
            ev_cb = p.op("vector", lambda v, tt=tt: v.tensor_copy(
                comb_bf[:, 0:9], comb_tm[:, tt, :]),
                waits=[ev_c, ev_sg], sig=True)
            ti2 = tt % 2
            ev_tr2 = p.op("tensor", lambda t, tt=tt, t0=t0, ti2=ti2: t.transpose(
                ps_t[ti2][0:9, 128:256], comb_bf[:, 0:9],
                idn[:, :]), waits=[ev_cb, ps_free[ti2], id_ev], sig=True)
            ev_fme = p.op("scalar", lambda s, t0=t0, ti2=ti2: s.copy(
                comb_fm[:, t0:t0 + 128], ps_t[ti2][0:9, 128:256]),
                waits=[ev_tr2], sig=True)
            ps_free[ti2] = ev_fme
            fm_evs.append(ev_fme)
    return comb_tm, comb_evs, comb_fm, (fm_evs[-1] if fm_evs else None)


def emit_moe_experts(sect, z_tiles, z_ev, comb_tm, comb_evs, acc_dram,
                     wg_d, wu_d, wd_d, x_tiles, park_evs, o_tiles, o_dead_ev,
                     flavor, b1c_d=None, b3c_d=None, b2r_d=None,
                     comb_fm=None, comb_fm_ev=None, x_read_done=None):
    p, T = sect.p, sect.T
    u = p.uid()
    NTC = T // 512
    l2 = flavor == "L2"
    if 2 * T >= 2816:
        wg_sb = [x_tiles[dc][:, :].bitcast(BF16) for dc in range(8)]
    else:
        wg_sb = [p.sbuf(f"wgl{u}_{dc}", [128, 2816], BF16) for dc in range(8)]
    wi0, wset0, wrel0 = sect.wpool_get()
    wi1, wset1, wrel1 = sect.wpool_get()
    wd_sb = wset0 + wset1[:3]
    nsl = max(1, T // 512)
    h2 = []
    for ic in range(11):
        ar, off = ic // nsl, ic % nsl
        if ar < len(o_tiles):
            h2.append(o_tiles[ar][:, off * 512:(off + 1) * 512])
        else:
            h2.append(p.sbuf(f"h2m{u}_{ic}", [128, 512], BF16))
    sgb = p.sbuf(f"sgb{u}", [128, 512], BF16) if not l2 else None
    eo_sb = [sect.scr[0], p.sbuf(f"eo{u}_1", [128, 512], F32)]
    eo_free = [None, None]
    b1c = p.sbuf(f"b1c{u}", [128, 11], F32) if l2 else None
    b3c = p.sbuf(f"b3c{u}", [128, 11], F32) if l2 else None
    b2r = p.sbuf(f"b2r{u}", [1, 1024], BF16) if l2 else None
    accs = Stream(p, "gpsimd", f"acc{u}", 2)
    wld = Stream(p, "gpsimd", f"wld{u}", 3)

    arena_free_x = list(park_evs) + list(x_read_done or [])
    arena_free_wd = [wrel0, wrel1, o_dead_ev]
    h2_free = [o_dead_ev] * 11
    sg_free = None
    last_acc = None
    prev_e_mm = None
    prev_e_dn = None
    bias_ev = None
    for e in range(9):
        wg_evs = []
        wu_evs = []
        for dc in range(8):
            wg_evs.append(wld.put([lambda g, e=e, dc=dc: g.dma_start(
                out=wg_sb[dc][:, 0:1408], in_=wg_d[e][dc * 128:(dc + 1) * 128, :])],
                waits=arena_free_x + [prev_e_mm]))
            wu_evs.append(wld.put([lambda g, e=e, dc=dc: g.dma_start(
                out=wg_sb[dc][:, 1408:2816], in_=wu_d[e][dc * 128:(dc + 1) * 128, :])]))
        wd_evs = []
        for ic in range(11):
            wd_evs.append(wld.put([lambda g, e=e, ic=ic: g.dma_start(
                out=wd_sb[ic][:, :], in_=wd_d[e][ic * 128:(ic + 1) * 128, :])],
                waits=(arena_free_wd + [prev_e_dn]) if ic == 0 else ()))
        arena_free_x = []
        arena_free_wd = []
        if l2:
            bias_ev = sect.ld.put([
                lambda g, e=e: g.dma_start(out=b1c[:, :], in_=b1c_d[e]),
                lambda g, e=e: g.dma_start(out=b3c[:, :], in_=b3c_d[e]),
                lambda g, e=e: g.dma_start(out=b2r[:, :], in_=b2r_d[e].unsqueeze(0)),
            ], waits=[prev_e_mm, prev_e_dn])
        e_mms = []
        e_dns = []
        for tc in range(NTC):
            t0c = tc * 512
            h_evs = [None] * 11
            for ic in range(11):
                iA, psA, relA = sect.pool.get()
                iB, psB, relB = sect.pool.get()
                evg = evu = None
                for dc in range(8):
                    evg = p.op("tensor", lambda t, dc=dc, ic=ic, t0c=t0c, ps=psA: t.matmul(
                        ps[:, :], wg_sb[dc][:, ic * 128:(ic + 1) * 128],
                        z_tiles[dc][:, t0c:t0c + 512],
                        start=(dc == 0), stop=(dc == 7)),
                        waits=[wg_evs[dc], z_ev] + ([relA] if dc == 0 else []),
                        sig=(dc == 7))
                    evu = p.op("tensor", lambda t, dc=dc, ic=ic, t0c=t0c, ps=psB: t.matmul(
                        ps[:, :], wg_sb[dc][:, 1408 + ic * 128:1408 + (ic + 1) * 128],
                        z_tiles[dc][:, t0c:t0c + 512],
                        start=(dc == 0), stop=(dc == 7)),
                        waits=[wu_evs[dc]] + ([relB] if dc == 0 else []),
                        sig=(dc == 7))
                e_mms.extend([evg, evu])
                if not l2:
                    ev_s = p.op("scalar", lambda s, ps=psA: s.activation(
                        sgb[:, :], ps[:, :], AF.Silu), waits=[evg, sg_free], sig=True)
                    ev_h = p.op("vector", lambda v, ic=ic, ps=psB: v.scalar_tensor_tensor(
                        h2[ic][:, :], ps[:, :], 1.0, sgb[:, :], ALU.bypass, ALU.mult),
                        waits=[ev_s, evu, h2_free[ic]], sig=True)
                    sg_free = ev_h
                else:
                    ev_a = p.op("scalar", lambda s, ic=ic, ps=psA: s.activation(
                        sect.scr[1][:, :], ps[:, :], AF.Identity,
                        bias=b1c[:, ic:ic + 1]), waits=[evg, bias_ev, sg_free], sig=True)
                    ev_m = p.op("vector", lambda v, ic=ic, ps=psB: v.scalar_tensor_tensor(
                        sect.scr[2][:, :], ps[:, :], b3c[:, ic:ic + 1],
                        sect.scr[1][:, :], ALU.add, ALU.mult),
                        waits=[ev_a, evu, bias_ev], sig=True)
                    ev_h = p.op("scalar", lambda s, ic=ic: s.activation(
                        h2[ic][:, :], sect.scr[2][:, :], AF.Silu),
                        waits=[ev_m, h2_free[ic]], sig=True)
                    sg_free = ev_h
                sect.pool.release(iA, ev_h)
                sect.pool.release(iB, ev_h)
                h_evs[ic] = ev_h
            for half in range(2):
                for tsub in range(4):
                    tt = tc * 4 + tsub
                    t0 = tt * 128
                    i_o, ps_o, rel_o = sect.pool.get()
                    ev = None
                    for ic in range(11):
                        ev = p.op("tensor", lambda t, ic=ic, tsub=tsub, half=half, ps=ps_o: t.matmul(
                            ps[:, :], h2[ic][:, tsub * 128:(tsub + 1) * 128],
                            wd_sb[ic][:, half * 512:(half + 1) * 512],
                            start=(ic == 0), stop=(ic == 10 and not l2)),
                            waits=[h_evs[ic], wd_evs[ic]] + ([rel_o] if ic == 0 else []),
                            sig=(ic == 10 and not l2))
                        e_dns.append(ev)
                    if l2:
                        ev = p.op("tensor", lambda t, half=half, ps=ps_o: t.matmul(
                            ps[:, :], sect.ones_row_bf[:, :],
                            b2r[:, half * 512:(half + 1) * 512],
                            start=False, stop=True),
                            waits=[bias_ev, sect.ones_bf_ev], sig=True)
                    pp_ = (tc * 8 + half * 4 + tsub) % 2
                    ev_sc = p.op("vector", lambda v, tt=tt, e=e, ps=ps_o, pp_=pp_: v.tensor_scalar(
                        eo_sb[pp_][:, :], ps[:, :], comb_tm[:, tt, e:e + 1], None,
                        ALU.mult), waits=[ev, eo_free[pp_]] + comb_evs[-2:], sig=True)
                    sect.pool.release(i_o, ev_sc)
                    sev = accs.put([lambda g, tt=tt, t0=t0, half=half, e=e, pp_=pp_: g.dma_start(
                        out=acc_dram[t0:t0 + 128, half * 512:(half + 1) * 512],
                        in_=eo_sb[pp_][:, :],
                        accum_op=(ALU.bypass if e == 0 else ALU.add))],
                        waits=[ev_sc])
                    eo_free[pp_] = sev
                    last_acc = sev
            for ic in range(11):
                h2_free[ic] = e_dns[-1]
        prev_e_mm = e_mms[-1]
        prev_e_dn = e_dns[-1]
    sect.wpool_release(wi0, prev_e_dn)
    sect.wpool_release(wi1, prev_e_dn)
    return accs.drain_evs(), prev_e_mm, prev_e_dn


def emit_moe_finale(sect, acc_dram, acc_done_evs, x_tiles, x1_dram, weights_done,
                    o_tiles, h2_done, x2_o, park_evs=()):
    p, T = sect.p, sect.T
    NTT = T // 128
    idn_f32, id32_ev = sect.ident32()
    x_evs = []
    for dc in range(8):
        x_evs.append(sect.act_ld.put([lambda y, dc=dc: y.dma_start(
            out=x_tiles[dc][:, :], in_=x1_dram[dc * 128:(dc + 1) * 128, :])],
            waits=[weights_done] + list(park_evs)))
    if T >= 2048:
        acc_t = [o_tiles[i][:, :].bitcast(F32)[:, 0:1024] for i in range(2)]
    else:
        acc_t = [p.sbuf(f"accl{p.uid()}_{i}", [128, 1024], F32) for i in range(2)]
    acc_free = [h2_done, h2_done]
    x2_evs = [None] * 8
    for tt in range(NTT):
        t0 = tt * 128
        ai = tt % 2
        ev_l = sect.act_ld.put([lambda y, tt=tt, ai=ai: y.dma_start(
            out=acc_t[ai][:, :], in_=acc_dram[tt * 128:(tt + 1) * 128, :])],
            waits=list(acc_done_evs) + [acc_free[ai]])
        for dc in range(8):
            i_p, ps, rel = sect.pool.get()
            ev_tr = p.op("tensor", lambda t, ai=ai, dc=dc, ps=ps: t.transpose(
                ps[:, 0:128], acc_t[ai][:, dc * 128:(dc + 1) * 128], idn_f32[:, :]),
                waits=[ev_l, id32_ev, rel], sig=True)
            ev_add = p.op("vector", lambda v, dc=dc, t0=t0, ps=ps: v.tensor_tensor(
                x_tiles[dc][:, t0:t0 + 128], ps[:, 0:128],
                x_tiles[dc][:, t0:t0 + 128], ALU.add),
                waits=[ev_tr, x_evs[dc]], sig=True)
            sect.pool.release(i_p, ev_add)
            x2_evs[dc] = ev_add
            acc_free[ai] = ev_add
    emit_store_fm(sect, x_tiles, x2_evs, x2_o)
    return x2_evs


def build_B0(T, stage=5):
    p = Prog("B0")
    sect = Sect(p, T, n_pool=6)
    x_d = p.in_("x", [D, T], F32).ap()
    o_d = p.in_("o", [D, T], F32).ap()
    wot_d = p.in_("wot", [D, D], F32).ap()
    w1_d = p.in_("w1", [4096, D], F32).ap()
    b1c_d = p.in_("b1c", [128, 32], F32).ap()
    w2_d = p.in_("w2", [D, 4096], F32).ap()
    b2c_d = p.in_("b2c", [128, 8], F32).ap()
    wq1_d = p.in_("wq1", [D, D], F32).ap()
    wk1_d = p.in_("wk1", [D, D], F32).ap()
    wv1_d = p.in_("wv1", [D, D], F32).ap()
    bq1c_d = p.in_("bq1c", [128, 8], F32).ap()
    bk1c_d = p.in_("bk1c", [128, 8], F32).ap()
    bv1r_d = p.in_("bv1r", [1, D], F32).ap()
    x2_o = p.out("x2", [D, T], F32).ap()
    q1_o = p.out("q1", [D, T], F32).ap()
    k1_o = p.out("k1", [D, T], F32).ap()
    v1_o = p.out("v1", [T, D], F32).ap()

    x_tiles = [p.sbuf(f"xs{i}", [128, T], F32) for i in range(8)]
    o_tiles = [p.sbuf(f"os{i}", [128, T], BF16) for i in range(8)]
    z_tiles = [p.sbuf(f"zs{i}", [128, T], BF16) for i in range(8)]

    x_evs = emit_load_fm(sect, x_d, x_tiles)
    o_evs = []
    for dc in range(8):
        o_evs.append(sect.ld.put([lambda g, dc=dc: g.dma_start(
            out=o_tiles[dc][:, :], in_=o_d[dc * 128:(dc + 1) * 128, :])]))
    o_ev = o_evs[-1]

    small = Stream(p, "gpsimd", "sm", 2)
    b1c = p.sbuf("b1c_s", [128, 32], F32)
    b2c = p.sbuf("b2c_s", [128, 8], F32)
    bq1c = p.sbuf("bq1c_s", [128, 8], F32)
    bk1c = p.sbuf("bk1c_s", [128, 8], F32)
    bv1r = p.sbuf("bv1r_s", [1, D], F32)
    ev_b1 = small.put([lambda g: g.dma_start(out=b1c[:, :], in_=b1c_d)])
    ev_b2 = small.put([lambda g: g.dma_start(out=b2c[:, :], in_=b2c_d)])
    ev_bq = small.put([lambda g: g.dma_start(out=bq1c[:, :], in_=bq1c_d)])
    ev_bk = small.put([lambda g: g.dma_start(out=bk1c[:, :], in_=bk1c_d)])
    ev_bv = small.put([lambda g: g.dma_start(out=bv1r[:, :], in_=bv1r_d)])

    wo_t, wo_e, wo_i = emit_weight_load(sect, wot_d, D, D, "wo")
    x1_evs = emit_out_proj_residual(sect, o_tiles, o_ev, wo_t, wo_e, x_tiles, x_evs)
    sect.wpool_release(wo_i, Ev(p.clock_sem["tensor"], p.clock["tensor"]))
    if stage <= 1:
        emit_store_fm(sect, x_tiles, x1_evs, x2_o)
        finish(p, sect)
        return p.build()
    z_ev = emit_ln(sect, x_tiles, x1_evs, z_tiles)
    if stage <= 2:
        emit_store_fm(sect, x_tiles, x1_evs, x2_o)
        finish(p, sect)
        return p.build()
    sect.h2_arena = o_tiles
    sw1t, sw1_ev, a1, a1_ev = emit_sign_alpha_prep(sect, w1_d, 4096, 1024, "w1")
    sw2t, sw2_ev, a2, a2_ev = emit_sign_alpha_prep(sect, w2_d, 1024, 4096, "w2")
    if stage <= 3:
        emit_store_fm(sect, x_tiles, x1_evs, x2_o)
        finish(p, sect)
        return p.build()
    x2_evs = emit_ffn_l0(sect, z_tiles, z_ev, x_tiles, x1_evs,
                         sw1t, sw1_ev, a1, a1_ev, b1c, ev_b1,
                         sw2t, sw2_ev, a2, a2_ev, b2c, ev_b2)
    emit_store_fm(sect, x_tiles, x2_evs, x2_o)
    if stage <= 4:
        finish(p, sect)
        return p.build()
    z2_ev = emit_ln(sect, x_tiles, x2_evs, z_tiles)
    wq_t, wq_e, wq_i = emit_weight_load(sect, wq1_d, D, D, "wq1")
    emit_qk_plain(sect, z_tiles, z2_ev, wq_t, wq_e, q1_o, bq1c, ev_bq)
    sect.wpool_release(wq_i, Ev(p.clock_sem["tensor"], p.clock["tensor"]))
    wk_t, wk_e, wk_i = emit_weight_load(sect, wk1_d, D, D, "wk1")
    emit_qk_plain(sect, z_tiles, z2_ev, wk_t, wk_e, k1_o, bk1c, ev_bk)
    sect.wpool_release(wk_i, Ev(p.clock_sem["tensor"], p.clock["tensor"]))
    wv_t, wv_e, wv_i = emit_weight_load(sect, wv1_d, D, D, "wv1")
    emit_v_tm_to_hbm(sect, z_tiles, z2_ev, wv_t, wv_e, v1_o, bv1r, ev_bv)
    sect.wpool_release(wv_i, Ev(p.clock_sem["tensor"], p.clock["tensor"]))
    finish(p, sect)
    return p.build()


def build_B1(T, flavor="L1"):
    assert flavor == "L1"
    p = Prog("B1")
    sect = Sect(p, T, n_pool=6)
    x_d = p.in_("x", [D, T], F32).ap()
    o_d = p.in_("o", [D, T], F32).ap()
    wc_d = p.in_("wc", [D, D], F32).ap()
    bcc_d = p.in_("bcc", [128, 8], F32).ap()
    gate_d = p.in_("gate", [D, 9], F32).ap()
    wg_d = p.in_("wg", [9, D, 1408], F32).ap()
    wu_d = p.in_("wu", [9, D, 1408], F32).ap()
    wd_d = p.in_("wd", [9, 1408, D], F32).ap()
    wq2_d = p.in_("wq2", [D, D], F32).ap()
    wk2_d = p.in_("wk2", [D, D], F32).ap()
    wv2_d = p.in_("wv2", [D, D], F32).ap()
    x2_o = p.out("x2", [D, T], F32).ap()
    q2_o = p.out("q2", [D, T], F32).ap()
    k2_o = p.out("k2", [D, T], F32).ap()
    v2_o = p.out("v2", [T, D], F32).ap()
    x1_dram = p.internal("x1park", [D, T], F32).ap()
    acc_dram = p.internal("moeacc", [T, D], F32).ap()

    x_tiles = [p.sbuf(f"xs{i}", [128, T], F32) for i in range(8)]
    o_tiles = [p.sbuf(f"os{i}", [128, T], BF16) for i in range(8)]
    z_tiles = [p.sbuf(f"zs{i}", [128, T], BF16) for i in range(8)]
    gate_sb = [p.sbuf(f"gt{i}", [128, 9], F32) for i in range(8)]
    mean_row = p.sbuf("mrow", [1, T], F32)
    rstd_row = p.sbuf("rrow", [1, T], F32)

    x_evs = emit_load_fm(sect, x_d, x_tiles)
    o_evs = []
    for dc in range(8):
        o_evs.append(sect.ld.put([lambda g, dc=dc: g.dma_start(
            out=o_tiles[dc][:, :], in_=o_d[dc * 128:(dc + 1) * 128, :])]))
    o_ev = o_evs[-1]
    small = Stream(p, "gpsimd", "sm", 2)
    bcc = p.sbuf("bcc_s", [128, 8], F32)
    ev_bc = small.put([lambda g: g.dma_start(out=bcc[:, :], in_=bcc_d)])
    gate_evs = []
    for dc in range(8):
        gate_evs.append(small.put([lambda g, dc=dc: g.dma_start(
            out=gate_sb[dc][:, :], in_=gate_d[dc * 128:(dc + 1) * 128, :])]))

    wc_t, wc_e, wc_i = emit_weight_load(sect, wc_d, D, D, "wc")
    x1_evs = emit_out_proj_residual(sect, o_tiles, o_ev, wc_t, wc_e, x_tiles,
                                    x_evs, bias_col=bcc, bias_ev=ev_bc)
    sect.wpool_release(wc_i, Ev(p.clock_sem["tensor"], p.clock["tensor"]))
    park_evs = emit_store_fm(sect, x_tiles, x1_evs, x1_dram)
    z_ev = emit_ln(sect, x_tiles, x1_evs, z_tiles, keep_stats=(mean_row, rstd_row))
    comb_tm, comb_evs, comb_fm, comb_fm_ev = emit_gating(
        sect, x_tiles, x1_evs, mean_row, rstd_row, z_ev, gate_sb,
        gate_evs[-2:], "L1")
    o_dead = z_ev
    gating_done = Ev(p.clock_sem["tensor"], p.clock["tensor"])
    last_acc, wx_done, wd_done = emit_moe_experts(
        sect, z_tiles, z_ev, comb_tm, comb_evs, acc_dram,
        wg_d, wu_d, wd_d, x_tiles, park_evs, o_tiles, o_dead, "L1",
        x_read_done=[gating_done, z_ev])
    x2_evs = emit_moe_finale(sect, acc_dram, last_acc, x_tiles, x1_dram,
                             wx_done, o_tiles, wd_done, x2_o, park_evs=park_evs)
    z2_ev = emit_ln(sect, x_tiles, x2_evs, z_tiles)
    wq_t, wq_e, wq_i = emit_weight_load(sect, wq2_d, D, D, "wq2")
    emit_qk_plain(sect, z_tiles, z2_ev, wq_t, wq_e, q2_o)
    sect.wpool_release(wq_i, Ev(p.clock_sem["tensor"], p.clock["tensor"]))
    wk_t, wk_e, wk_i = emit_weight_load(sect, wk2_d, D, D, "wk2")
    emit_qk_plain(sect, z_tiles, z2_ev, wk_t, wk_e, k2_o)
    sect.wpool_release(wk_i, Ev(p.clock_sem["tensor"], p.clock["tensor"]))
    wv_t, wv_e, wv_i = emit_weight_load(sect, wv2_d, D, D, "wv2")
    emit_v_tm_to_hbm(sect, z_tiles, z2_ev, wv_t, wv_e, v2_o)
    sect.wpool_release(wv_i, Ev(p.clock_sem["tensor"], p.clock["tensor"]))
    finish(p, sect)
    return p.build()


def rope_tables_L3(positions):
    T = len(positions)
    inv = 1.0 / 10000.0 ** (np.arange(0, DK, 2, dtype=np.float64) / DK)
    fr = positions[:, None].astype(np.float64) * inv[None, :]
    cosd = np.empty((DK, T))
    sinS = np.empty((DK, T))
    for i in range(32):
        g = fr[:, i // 2]
        hh = fr[:, i // 2 + 16]
        cosd[2 * i] = np.cos(g)
        cosd[2 * i + 1] = np.cos(hh)
        sinS[2 * i] = -np.sin(g)
        sinS[2 * i + 1] = +np.sin(hh)
    return (np.concatenate([cosd, cosd], 0).astype(np.float32),
            np.concatenate([sinS, sinS], 0).astype(np.float32))


def build_B2(T):
    p = Prog("B2")
    sect = Sect(p, T, n_pool=6)
    x_d = p.in_("x", [D, T], F32).ap()
    o_d = p.in_("o", [D, T], F32).ap()
    wo_d = p.in_("wo", [D, D], F32).ap()
    boc_d = p.in_("boc", [128, 8], F32).ap()
    gate_d = p.in_("gate", [D, 9], F32).ap()
    gb_d = p.in_("gb", [1, 8], F32).ap()
    wg_d = p.in_("wg", [9, D, 1408], F32).ap()
    wu_d = p.in_("wu", [9, D, 1408], F32).ap()
    wd_d = p.in_("wd", [9, 1408, D], F32).ap()
    b1c_d = p.in_("b1c", [9, 128, 11], F32).ap()
    b3c_d = p.in_("b3c", [9, 128, 11], F32).ap()
    b2r_d = p.in_("b2r", [9, D], F32).ap()
    wq3_d = p.in_("wq3", [D, D], F32).ap()
    wk3_d = p.in_("wk3", [D, D], F32).ap()
    wv3_d = p.in_("wv3", [D, D], F32).ap()
    rtc_d = p.in_("rtc", [128, T], F32).ap()
    rts_d = p.in_("rts", [128, T], F32).ap()
    x2_o = p.out("x2", [D, T], F32).ap()
    q3_o = p.out("q3", [D, T], F32).ap()
    k3_o = p.out("k3", [D, T], F32).ap()
    v3_o = p.out("v3", [T, D], F32).ap()
    x1_dram = p.internal("x1park", [D, T], F32).ap()
    acc_dram = p.internal("moeacc", [T, D], F32).ap()

    x_tiles = [p.sbuf(f"xs{i}", [128, T], F32) for i in range(8)]
    o_tiles = [p.sbuf(f"os{i}", [128, T], BF16) for i in range(8)]
    z_tiles = [p.sbuf(f"zs{i}", [128, T], BF16) for i in range(8)]
    gate_sb = [p.sbuf(f"gt{i}", [128, 9], F32) for i in range(8)]
    mean_row = p.sbuf("mrow", [1, T], F32)
    rstd_row = p.sbuf("rrow", [1, T], F32)
    if T >= 2048:
        ncol = T // 2
        def rtc_ap(q0, qn):
            ti = 3 + q0 // ncol
            off = q0 % ncol
            return o_tiles[ti][:, :].bitcast(F32)[:, off:off + qn]
        def rts_ap(q0, qn):
            ti = 5 + q0 // ncol
            off = q0 % ncol
            return o_tiles[ti][:, :].bitcast(F32)[:, off:off + qn]
        rtab = [(3, 0), (4, 0)]
    else:
        ncol = T
        rtc_sm = p.sbuf("rtc_sm", [128, T], F32)
        rts_sm = p.sbuf("rts_sm", [128, T], F32)
        rtc_ap = lambda q0, qn: rtc_sm[:, q0:q0 + qn]
        rts_ap = lambda q0, qn: rts_sm[:, q0:q0 + qn]

    x_evs = emit_load_fm(sect, x_d, x_tiles)
    o_evs = []
    for dc in range(8):
        o_evs.append(sect.ld.put([lambda g, dc=dc: g.dma_start(
            out=o_tiles[dc][:, :], in_=o_d[dc * 128:(dc + 1) * 128, :])]))
    o_ev = o_evs[-1]
    small = Stream(p, "gpsimd", "sm", 2)
    boc = p.sbuf("boc_s", [128, 8], F32)
    gbr = p.sbuf("gbr_s", [1, 8], F32)
    ev_bo = small.put([lambda g: g.dma_start(out=boc[:, :], in_=boc_d)])
    ev_gb = small.put([lambda g: g.dma_start(out=gbr[:, :], in_=gb_d)])
    gate_evs = []
    for dc in range(8):
        gate_evs.append(small.put([lambda g, dc=dc: g.dma_start(
            out=gate_sb[dc][:, :], in_=gate_d[dc * 128:(dc + 1) * 128, :])]))

    wo_t, wo_e, wo_i = emit_weight_load(sect, wo_d, D, D, "wo2")
    x1_evs = emit_out_proj_residual(sect, o_tiles, o_ev, wo_t, wo_e, x_tiles,
                                    x_evs, bias_col=boc, bias_ev=ev_bo)
    sect.wpool_release(wo_i, Ev(p.clock_sem["tensor"], p.clock["tensor"]))
    park_evs = emit_store_fm(sect, x_tiles, x1_evs, x1_dram)
    z_ev = emit_ln(sect, x_tiles, x1_evs, z_tiles, keep_stats=(mean_row, rstd_row))
    comb_tm, comb_evs, comb_fm, comb_fm_ev = emit_gating(
        sect, x_tiles, x1_evs, mean_row, rstd_row, z_ev, gate_sb,
        gate_evs[-2:], "L2", gb_row=gbr, gb_ev=ev_gb)
    gating_done = Ev(p.clock_sem["tensor"], p.clock["tensor"])
    last_acc, wx_done, wd_done = emit_moe_experts(
        sect, z_tiles, z_ev, comb_tm, comb_evs, acc_dram,
        wg_d, wu_d, wd_d, x_tiles, park_evs, o_tiles, z_ev, "L2",
        b1c_d=b1c_d, b3c_d=b3c_d, b2r_d=b2r_d,
        x_read_done=[gating_done, z_ev])
    x2_evs = emit_moe_finale(sect, acc_dram, last_acc, x_tiles, x1_dram,
                             wx_done, o_tiles, wd_done, x2_o, park_evs=park_evs)
    z2_ev = emit_ln(sect, x_tiles, x2_evs, z_tiles)
    if T >= 2048:
        ev_rt = sect.act_ld.put([
            lambda y: y.dma_start(out=o_tiles[3][:, :].bitcast(F32)[:, 0:ncol],
                                  in_=rtc_d[:, 0:ncol]),
            lambda y: y.dma_start(out=o_tiles[4][:, :].bitcast(F32)[:, 0:T - ncol],
                                  in_=rtc_d[:, ncol:T]),
        ], waits=[z_ev])
        ev_rt2 = sect.act_ld.put([
            lambda y: y.dma_start(out=o_tiles[5][:, :].bitcast(F32)[:, 0:ncol],
                                  in_=rts_d[:, 0:ncol]),
            lambda y: y.dma_start(out=o_tiles[6][:, :].bitcast(F32)[:, 0:T - ncol],
                                  in_=rts_d[:, ncol:T]),
        ], waits=[z_ev])
    else:
        ev_rt = sect.act_ld.put([lambda y: y.dma_start(out=rtc_sm[:, :], in_=rtc_d)])
        ev_rt2 = sect.act_ld.put([lambda y: y.dma_start(out=rts_sm[:, :], in_=rts_d)])
    wq_t, wq_e, wq_i = emit_weight_load(sect, wq3_d, D, D, "wq3")
    emit_qk_rope_to_hbm(sect, z_tiles, z2_ev, wq_t, wq_e, rtc_ap, rts_ap,
                        [ev_rt, ev_rt2], q3_o)
    sect.wpool_release(wq_i, Ev(p.clock_sem["tensor"], p.clock["tensor"]))
    wk_t, wk_e, wk_i = emit_weight_load(sect, wk3_d, D, D, "wk3")
    emit_qk_rope_to_hbm(sect, z_tiles, z2_ev, wk_t, wk_e, rtc_ap, rts_ap,
                        [ev_rt, ev_rt2], k3_o)
    sect.wpool_release(wk_i, Ev(p.clock_sem["tensor"], p.clock["tensor"]))
    wv_t, wv_e, wv_i = emit_weight_load(sect, wv3_d, D, D, "wv3")
    emit_v_tm_to_hbm(sect, z_tiles, z2_ev, wv_t, wv_e, v3_o)
    sect.wpool_release(wv_i, Ev(p.clock_sem["tensor"], p.clock["tensor"]))
    finish(p, sect)
    return p.build()


def build_B3(T):
    p = Prog("B3")
    sect = Sect(p, T, n_pool=6)
    x_d = p.in_("x", [D, T], F32).ap()
    o_d = p.in_("o", [D, T], F32).ap()
    wo_d = p.in_("wo", [D, D], F32).ap()
    wqf_d = p.in_("wqf", [D, D], F32).ap()
    wkf_d = p.in_("wkf", [D, D], F32).ap()
    wvf_d = p.in_("wvf", [D, D], F32).ap()
    x2_o = p.out("x2", [D, T], F32).ap()
    qf_o = p.out("qf", [D, T], F32).ap()
    kf_o = p.out("kf", [D, T], F32).ap()
    vf_o = p.out("vf", [T, D], F32).ap()

    x_tiles = [p.sbuf(f"xs{i}", [128, T], F32) for i in range(8)]
    o_tiles = [p.sbuf(f"os{i}", [128, T], BF16) for i in range(8)]
    z_tiles = [p.sbuf(f"zs{i}", [128, T], BF16) for i in range(8)]

    x_evs = emit_load_fm(sect, x_d, x_tiles)
    o_evs = []
    for dc in range(8):
        o_evs.append(sect.ld.put([lambda g, dc=dc: g.dma_start(
            out=o_tiles[dc][:, :], in_=o_d[dc * 128:(dc + 1) * 128, :])]))
    o_ev = o_evs[-1]
    wo_t, wo_e, wo_i = emit_weight_load(sect, wo_d, D, D, "wo3")
    x1_evs = emit_out_proj_residual(sect, o_tiles, o_ev, wo_t, wo_e, x_tiles, x_evs)
    sect.wpool_release(wo_i, Ev(p.clock_sem["tensor"], p.clock["tensor"]))
    emit_store_fm(sect, x_tiles, x1_evs, x2_o)
    z_ev = emit_ln(sect, x_tiles, x1_evs, z_tiles)
    wq_t, wq_e, wq_i = emit_weight_load(sect, wqf_d, D, D, "wqf")
    emit_qk_plain(sect, z_tiles, z_ev, wq_t, wq_e, qf_o)
    sect.wpool_release(wq_i, Ev(p.clock_sem["tensor"], p.clock["tensor"]))
    wk_t, wk_e, wk_i = emit_weight_load(sect, wkf_d, D, D, "wkf")
    emit_qk_plain(sect, z_tiles, z_ev, wk_t, wk_e, kf_o)
    sect.wpool_release(wk_i, Ev(p.clock_sem["tensor"], p.clock["tensor"]))
    wv_t, wv_e, wv_i = emit_weight_load(sect, wvf_d, D, D, "wvf")
    emit_v_tm_to_hbm(sect, z_tiles, z_ev, wv_t, wv_e, vf_o)
    sect.wpool_release(wv_i, Ev(p.clock_sem["tensor"], p.clock["tensor"]))
    finish(p, sect)
    return p.build()


def emit_l2norm_pass(sect, in_tiles, in_evs, out_tiles):
    p, T = sect.p, sect.T
    xsq = sect.scr[0]
    stat_sq, _, arow, _ = sect.ln_rows()
    last = None
    for qc in range(sect.NQC):
        q0 = qc * 512
        qn = min(512, T - q0)
        i_q, ps_q, rel_q = sect.pool.get()
        ev_mm2 = None
        for dc in range(8):
            ev_sq = p.op("scalar", lambda s, dc=dc, q0=q0, qn=qn: s.activation(
                xsq[:, :qn], in_tiles[dc][:, q0:q0 + qn], AF.Square),
                waits=[in_evs[dc] if isinstance(in_evs, list) else in_evs, ev_mm2],
                sig=True)
            ev_mm2 = p.op("tensor", lambda t, dc=dc, qn=qn, ps=ps_q: t.matmul(
                ps[0:1, :qn], sect.ones_col[:, :], xsq[:, :qn],
                start=(dc == 0), stop=(dc == 7)),
                waits=[ev_sq, sect.ones_col_ev] + ([rel_q] if dc == 0 else []),
                sig=True)
        ev_c = p.op("vector", lambda v, qn=qn, ps=ps_q: v.tensor_copy(
            stat_sq[:, :qn], ps[0:1, :qn]), waits=[ev_mm2], sig=True)
        sect.pool.release(i_q, ev_c)
        ev_sq2 = p.op("scalar", lambda s, qn=qn: s.activation(
            arow[:, :qn], stat_sq[:, :qn], AF.Sqrt), waits=[ev_c], sig=True)
        ev_r = p.seq("vector", [
            lambda v, qn=qn: v.tensor_scalar_max(arow[:, :qn], arow[:, :qn], 1e-12),
            lambda v, qn=qn: v.reciprocal(arow[:, :qn], arow[:, :qn]),
        ], waits=[ev_sq2])
        i_a, ps_a, rel_a = sect.pool.get()
        ev_ba = p.op("tensor", lambda t, qn=qn, ps=ps_a: t.matmul(
            ps[:, :qn], sect.ones_row[:, :], arow[:, :qn],
            start=True, stop=True), waits=[ev_r, rel_a, sect.ones_ev], sig=True)
        for dc in range(8):
            last = p.op("vector", lambda v, dc=dc, q0=q0, qn=qn, ps_a=ps_a: v.tensor_tensor(
                out_tiles[dc][:, q0:q0 + qn], in_tiles[dc][:, q0:q0 + qn],
                ps_a[:, :qn], ALU.mult),
                waits=[ev_ba], sig=True)
        sect.pool.release(i_a, last)
    return last


def build_B4(T):
    p = Prog("B4")
    sect = Sect(p, T, n_pool=6)
    x_d = p.in_("x", [D, T], F32).ap()
    o_d = p.in_("o", [D, T], F32).ap()
    wo_d = p.in_("wo", [D, D], F32).ap()
    wff_d = p.in_("wff", [D, D], F32).ap()
    xo_o = p.out("xout", [D, T], F32).ap()

    x_tiles = [p.sbuf(f"xs{i}", [128, T], F32) for i in range(8)]
    o_tiles = [p.sbuf(f"os{i}", [128, T], BF16) for i in range(8)]
    z_tiles = [p.sbuf(f"zs{i}", [128, T], BF16) for i in range(8)]
    z2_tiles = o_tiles

    x_evs = emit_load_fm(sect, x_d, x_tiles)
    o_evs = []
    for dc in range(8):
        o_evs.append(sect.ld.put([lambda g, dc=dc: g.dma_start(
            out=o_tiles[dc][:, :], in_=o_d[dc * 128:(dc + 1) * 128, :])]))
    o_ev = o_evs[-1]
    wo_t, wo_e, wo_i = emit_weight_load(sect, wo_d, D, D, "wo4")
    zz_evs = [None] * 8

    def cb1(oc, qc, q0, qn, i_p, ps, ev_mm):
        ev = p.op("scalar", lambda s, oc=oc, q0=q0, qn=qn, ps=ps: s.copy(
            z_tiles[oc][:, q0:q0 + qn], ps[:, :qn]), waits=[ev_mm], sig=True)
        sect.pool.release(i_p, ev)
        zz_evs[oc] = ev
    emit_proj_fm(sect, o_tiles, o_ev, wo_t, wo_e, cb1)
    sect.wpool_release(wo_i, Ev(p.clock_sem["tensor"], p.clock["tensor"]))
    n1_ev = emit_l2norm_pass(sect, z_tiles, zz_evs, z2_tiles)
    wf_t, wf_e, wf_i = emit_weight_load(sect, wff_d, D, D, "wff")
    zz2_evs = [None] * 8

    def cb2(oc, qc, q0, qn, i_p, ps, ev_mm):
        ev = p.op("scalar", lambda s, oc=oc, q0=q0, qn=qn, ps=ps: s.copy(
            z_tiles[oc][:, q0:q0 + qn], ps[:, :qn]), waits=[ev_mm, zz_evs[oc]], sig=True)
        sect.pool.release(i_p, ev)
        zz2_evs[oc] = ev
    emit_proj_fm(sect, z2_tiles, n1_ev, wf_t, wf_e, cb2)
    sect.wpool_release(wf_i, Ev(p.clock_sem["tensor"], p.clock["tensor"]))
    n2_ev = emit_l2norm_pass(sect, z_tiles, zz2_evs, z2_tiles)
    x2_evs = [None] * 8
    prev_a = None
    for dc in range(8):
        for qc in range(sect.NQC):
            q0 = qc * 512
            qn = min(512, T - q0)
            ev_g = p.op("scalar", lambda s, dc=dc, q0=q0, qn=qn: s.activation(
                sect.scr[1][:, :qn], z2_tiles[dc][:, q0:q0 + qn], AF.Gelu),
                waits=[n2_ev, prev_a], sig=True)
            ev_a = p.op("vector", lambda v, dc=dc, q0=q0, qn=qn: v.tensor_tensor(
                x_tiles[dc][:, q0:q0 + qn], sect.scr[1][:, :qn],
                x_tiles[dc][:, q0:q0 + qn], ALU.add),
                waits=[ev_g, x_evs[dc]], sig=True)
            prev_a = ev_a
            x2_evs[dc] = ev_a
    emit_store_fm(sect, x_tiles, x2_evs, xo_o)
    finish(p, sect)
    return p.build()


def _t(a):
    return np.ascontiguousarray(np.asarray(a).T.astype(np.float32))


def _col(b, n=128):
    b = np.asarray(b).reshape(-1)
    return np.ascontiguousarray(b.reshape(-1, n).T.astype(np.float32))


_PROG_CACHE = {}


def _prog(name, builder):
    if name not in _PROG_CACHE:
        _PROG_CACHE[name] = builder()
    return _PROG_CACHE[name]


TT = 2 * S // NCORES * 4 // 4
TOK = B * S // NCORES


def _gather_heads(per_core_fm, batch):
    return np.concatenate([per_core_fm[2 * batch], per_core_fm[2 * batch + 1]], axis=1)


def _attn_in_rows(Qc, pairs):
    out = np.empty((8, 64, S), np.float32)
    for i, (b, h) in enumerate(pairs):
        out[i] = Qc[b][h * 64:(h + 1) * 64]
    return out


def _attn_in_vt(Vc, pairs, aug):
    w = 68 if aug else 64
    out = np.zeros((8, S, w), np.float32)
    for i, (b, h) in enumerate(pairs):
        out[i, :, :64] = Vc[b][:, h * 64:(h + 1) * 64]
        if aug:
            out[i, :, 64] = 1.0
    return out


def _scatter_heads(ofs, pairs_of_core):
    O = {b: np.empty((D, S), np.float32) for b in range(B)}
    for c in range(NCORES):
        for i, (b, h) in enumerate(pairs_of_core[c]):
            O[b][h * 64:(h + 1) * 64] = ofs[c][i]
    return O


def _shard_fm(O):
    return [np.ascontiguousarray(O[c // 2][:, (c % 2) * TOK:(c % 2 + 1) * TOK])
            for c in range(NCORES)]


def _run(nc, in_maps):
    res = run_bass_kernel_spmd(nc, in_maps, core_ids=list(range(NCORES)))
    return res.results


def kernel(**inputs):
    ii = {k: np.asarray(v) for k, v in inputs.items()}
    X = ii["L0_attn_x"].astype(np.float32)
    Xf = X.reshape(-1, D)

    pairs_std = [[(c // 2, (c % 2) * 8 + j) for j in range(8)] for c in range(NCORES)]
    pairs_l2 = [[(c // 2, 2 * j + (c % 2)) for j in range(8)] for c in range(NCORES)]

    nc = _prog("pre", lambda: build_pre(TOK))
    maps = []
    for c in range(NCORES):
        rows = slice(c * TOK, (c + 1) * TOK)
        pos = np.arange(c * TOK, (c + 1) * TOK) % S
        rtc, rts = rope_tables_L0(pos)
        maps.append({
            "x": _t(Xf[rows]),
            "wqt": _t(ii["L0_attn_WQ"]), "wkt": _t(ii["L0_attn_WK"]),
            "wvt": _t(ii["L0_attn_WV"]),
            "rtc": rtc, "rts": rts,
        })
    r = _run(nc, maps)
    Qc = {b: _gather_heads([r[c]["q"] for c in range(8)], b) for b in range(B)}
    Kc = {b: _gather_heads([r[c]["k"] for c in range(8)], b) for b in range(B)}
    Vc = {b: np.concatenate([r[2 * b]["v"], r[2 * b + 1]["v"]], axis=0) for b in range(B)}

    nc = _prog("attnL0", lambda: build_attn_L0())
    maps = [{"qf": _attn_in_rows(Qc, pairs_std[c]),
             "kf": _attn_in_rows(Kc, pairs_std[c]),
             "vt": _attn_in_vt(Vc, pairs_std[c], True)} for c in range(NCORES)]
    r = _run(nc, maps)
    O = _scatter_heads([r[c]["of"] for c in range(8)], pairs_std)
    o_sh = _shard_fm(O)

    nc = _prog("B0", lambda: build_B0(TOK))
    Wq1 = ii["L1_attn_Wq"].transpose(1, 0, 2).reshape(D, D)
    Wk1 = ii["L1_attn_Wk"].transpose(1, 0, 2).reshape(D, D)
    Wv1 = ii["L1_attn_Wv"].transpose(1, 0, 2).reshape(D, D)
    maps = []
    for c in range(NCORES):
        rows = slice(c * TOK, (c + 1) * TOK)
        maps.append({
            "x": _t(Xf[rows]), "o": o_sh[c],
            "wot": _t(ii["L0_attn_WO"]),
            "w1": ii["L0_ffn_w1"].astype(np.float32),
            "b1c": _col(ii["L0_ffn_b1"]),
            "w2": ii["L0_ffn_w2"].astype(np.float32),
            "b2c": _col(ii["L0_ffn_b2"]),
            "wq1": np.ascontiguousarray(Wq1.astype(np.float32)),
            "wk1": np.ascontiguousarray(Wk1.astype(np.float32)),
            "wv1": np.ascontiguousarray(Wv1.astype(np.float32)),
            "bq1c": _col(ii["L1_attn_bq"]), "bk1c": _col(ii["L1_attn_bk"]),
            "bv1r": ii["L1_attn_bv"].reshape(1, D).astype(np.float32),
        })
    r = _run(nc, maps)
    x_sh = [r[c]["x2"] for c in range(8)]
    Qc = {b: _gather_heads([r[c]["q1"] for c in range(8)], b) for b in range(B)}
    Kc = {b: _gather_heads([r[c]["k1"] for c in range(8)], b) for b in range(B)}
    Vc = {b: np.concatenate([r[2 * b]["v1"], r[2 * b + 1]["v1"]], axis=0) for b in range(B)}

    nc = _prog("attnL1", lambda: build_attn_col("L1"))
    maps = [{"qf": _attn_in_rows(Qc, pairs_std[c]),
             "kf": _attn_in_rows(Kc, pairs_std[c]),
             "vt": _attn_in_vt(Vc, pairs_std[c], False)} for c in range(NCORES)]
    r = _run(nc, maps)
    O = _scatter_heads([r[c]["of"] for c in range(8)], pairs_std)
    o_sh = _shard_fm(O)

    nc = _prog("B1", lambda: build_B1(TOK))
    gate_ext = np.concatenate([ii["L1_ffn_gate_w"].T,
                               ii["L1_ffn_shared_gate_w"].T], axis=1)
    wg = np.concatenate([ii["L1_ffn_w_gate"], ii["L1_ffn_sw_gate"][None]], 0)
    wu = np.concatenate([ii["L1_ffn_w_up"], ii["L1_ffn_sw_up"][None]], 0)
    wd = np.concatenate([ii["L1_ffn_w_down"], ii["L1_ffn_sw_down"][None]], 0)
    maps = []
    for c in range(NCORES):
        maps.append({
            "x": x_sh[c], "o": o_sh[c],
            "wc": ii["L1_attn_Wc"].astype(np.float32),
            "bcc": _col(ii["L1_attn_bc"]),
            "gate": np.ascontiguousarray(gate_ext.astype(np.float32)),
            "wg": wg.astype(np.float32), "wu": wu.astype(np.float32),
            "wd": wd.astype(np.float32),
            "wq2": _t(ii["L2_attn_Wq"]), "wk2": _t(ii["L2_attn_Wk"]),
            "wv2": _t(ii["L2_attn_Wv"]),
        })
    r = _run(nc, maps)
    x_sh = [r[c]["x2"] for c in range(8)]
    Qc = {b: _gather_heads([r[c]["q2"] for c in range(8)], b) for b in range(B)}
    Kc = {b: _gather_heads([r[c]["k2"] for c in range(8)], b) for b in range(B)}
    Vc = {b: np.concatenate([r[2 * b]["v2"], r[2 * b + 1]["v2"]], axis=0) for b in range(B)}

    nc = _prog("attnL2", lambda: build_attn_L2())
    ebs, sids, grs = [], [], []
    iarr = np.arange(128)
    jarr = np.arange(512)
    for c in range(NCORES):
        eb = np.zeros((8, 128, 64), np.float32)
        sid = np.zeros((8, 128, 128), np.float32)
        gr = np.zeros((8, 1, 1024), np.float32)
        for i, (b, h) in enumerate(pairs_l2[c]):
            sl = alibi_slope(h)
            for dd in range(-32, 32):
                if dd <= -1:
                    eb[i, :, dd + 32] = sl * (dd * 128 + iarr)
                elif dd >= 4:
                    eb[i, :, dd + 32] = sl * (511 - dd * 128 - iarr)
            sid[i] = 8.0 * sl * np.eye(128)
            gr[i, 0, 0:512] = np.exp(-sl * jarr)
            gr[i, 0, 512:1024] = np.exp(-sl * (511 - jarr))
        ebs.append(eb)
        sids.append(sid)
        grs.append(gr)
    maps = [{"qf": _attn_in_rows(Qc, pairs_l2[c]),
             "kf": _attn_in_rows(Kc, pairs_l2[c]),
             "vt": _attn_in_vt(Vc, pairs_l2[c], True),
             "eb": ebs[c], "sid": sids[c], "grow": grs[c]} for c in range(NCORES)]
    r = _run(nc, maps)
    O = _scatter_heads([r[c]["of"] for c in range(8)], pairs_l2)
    o_sh = _shard_fm(O)

    nc = _prog("B2", lambda: build_B2(TOK))
    gate_ext = np.concatenate([ii["L2_ffn_gate_w"].T, np.zeros((D, 1), np.float32)],
                              axis=1)
    wg = np.concatenate([ii["L2_ffn_w1"], ii["L2_ffn_sw1"][None]], 0)
    wu = np.concatenate([ii["L2_ffn_w3"], ii["L2_ffn_sw3"][None]], 0)
    wd = np.concatenate([ii["L2_ffn_w2"], ii["L2_ffn_sw2"][None]], 0)
    b1 = np.concatenate([ii["L2_ffn_b1"], ii["L2_ffn_sb1"][None]], 0)
    b3 = np.concatenate([ii["L2_ffn_b3"], ii["L2_ffn_sb3"][None]], 0)
    b2 = np.concatenate([ii["L2_ffn_b2"], ii["L2_ffn_sb2"][None]], 0)
    b1c = np.ascontiguousarray(b1.reshape(9, 11, 128).transpose(0, 2, 1)).astype(np.float32)
    b3c = np.ascontiguousarray(b3.reshape(9, 11, 128).transpose(0, 2, 1)).astype(np.float32)
    perm = np.empty(64, np.int64)
    perm[0::2] = np.arange(32)
    perm[1::2] = np.arange(32) + 32
    Wq3 = ii["L3_attn_Wq"].reshape(D, 16, 64)[:, :, perm].reshape(D, D)
    Wk3 = ii["L3_attn_Wk"].reshape(D, 16, 64)[:, :, perm].reshape(D, D)
    maps = []
    for c in range(NCORES):
        pos = np.arange(c * TOK, (c + 1) * TOK) % S
        rtc, rts = rope_tables_L3(pos)
        maps.append({
            "x": x_sh[c], "o": o_sh[c],
            "wo": _t(ii["L2_attn_Wo"]),
            "boc": _col(ii["L2_attn_bo"]),
            "gate": np.ascontiguousarray(gate_ext.astype(np.float32)),
            "gb": ii["L2_ffn_gate_b"].reshape(1, 8).astype(np.float32),
            "wg": wg.astype(np.float32), "wu": wu.astype(np.float32),
            "wd": wd.astype(np.float32),
            "b1c": b1c, "b3c": b3c, "b2r": b2.astype(np.float32),
            "wq3": np.ascontiguousarray(Wq3.astype(np.float32)),
            "wk3": np.ascontiguousarray(Wk3.astype(np.float32)),
            "wv3": ii["L3_attn_Wv"].astype(np.float32),
            "rtc": rtc, "rts": rts,
        })
    r = _run(nc, maps)
    x_sh = [r[c]["x2"] for c in range(8)]
    Qc = {b: _gather_heads([r[c]["q3"] for c in range(8)], b) for b in range(B)}
    Kc = {b: _gather_heads([r[c]["k3"] for c in range(8)], b) for b in range(B)}
    Vc = {b: np.concatenate([r[2 * b]["v3"], r[2 * b + 1]["v3"]], axis=0) for b in range(B)}

    nc = _prog("attnL3a", lambda: build_attn_L3a())
    relp = ii["L3_attn_rel_emb"].astype(np.float32)[:, :, perm]
    jj = np.arange(S)
    bkj = _bucket_np(jj)
    ccols = np.arange(L3A_LD)
    jc = L3A_JOFF - ccols
    valid = (jc >= 0) & (jc < S)
    relr_h = np.zeros((16, 64, L3A_LD), np.float32)
    for h in range(16):
        tmp = np.zeros((L3A_LD, 64), np.float32)
        tmp[valid] = relp[bkj[jc[valid]], h, :]
        relr_h[h] = tmp.T
    maps = []
    for c in range(NCORES):
        relr = np.stack([relr_h[h] for (_, h) in pairs_std[c]])
        maps.append({"qf": _attn_in_rows(Qc, pairs_std[c]),
                     "kf": _attn_in_rows(Kc, pairs_std[c]),
                     "vt": _attn_in_vt(Vc, pairs_std[c], True),
                     "relr": relr})
    r = _run(nc, maps)
    O = _scatter_heads([r[c]["of"] for c in range(8)], pairs_std)
    o_sh = _shard_fm(O)

    nc = _prog("B3", lambda: build_B3(TOK))
    maps = []
    for c in range(NCORES):
        maps.append({
            "x": x_sh[c], "o": o_sh[c],
            "wo": ii["L3_attn_Wo"].astype(np.float32),
            "wqf": ii["L3_ffn_Wq"].astype(np.float32),
            "wkf": ii["L3_ffn_Wk"].astype(np.float32),
            "wvf": ii["L3_ffn_Wv"].astype(np.float32),
        })
    r = _run(nc, maps)
    x_sh = [r[c]["x2"] for c in range(8)]
    Qc = {b: _gather_heads([r[c]["qf"] for c in range(8)], b) for b in range(B)}
    Kc = {b: _gather_heads([r[c]["kf"] for c in range(8)], b) for b in range(B)}
    Vc = {b: np.concatenate([r[2 * b]["vf"], r[2 * b + 1]["vf"]], axis=0) for b in range(B)}

    nc = _prog("attnL3f", lambda: build_attn_col("L3f"))
    maps = [{"qf": _attn_in_rows(Qc, pairs_std[c]),
             "kf": _attn_in_rows(Kc, pairs_std[c]),
             "vt": _attn_in_vt(Vc, pairs_std[c], False)} for c in range(NCORES)]
    r = _run(nc, maps)
    O = _scatter_heads([r[c]["of"] for c in range(8)], pairs_std)
    o_sh = _shard_fm(O)

    nc = _prog("B4", lambda: build_B4(TOK))
    maps = []
    for c in range(NCORES):
        maps.append({
            "x": x_sh[c], "o": o_sh[c],
            "wo": ii["L3_ffn_Wo"].astype(np.float32),
            "wff": ii["L3_ffn_Wff"].astype(np.float32),
        })
    r = _run(nc, maps)
    out = np.empty((B * S, D), np.float32)
    for c in range(NCORES):
        out[c * TOK:(c + 1) * TOK] = r[c]["xout"].T
    return out.reshape(B, S, D)


def _bucket_np(relv, NB=256, MAXD=16384):
    ab = np.abs(relv)
    me, half = NB // 4, NB // 2
    small = ab < me
    safe = np.maximum(ab, 1).astype(np.float32)
    expr = (np.log(safe / np.float32(me)) / np.float32(math.log(MAXD / me))
            * np.float32(half - me))
    large = me + expr.astype(np.int64)
    large = np.minimum(large, half - 1)
    bk = np.where(small, ab, large)
    return np.where(relv < 0, half + bk, bk).astype(np.int64)
